# version 11
# baseline (speedup 1.0000x reference)
import numpy as np
import ml_dtypes
from contextlib import ExitStack

import concourse.bass as bass
import concourse.mybir as mybir
from concourse.bass_utils import run_bass_kernel_spmd

F32 = mybir.dt.float32
BF16 = mybir.dt.bfloat16
AF = mybir.ActivationFunctionType
ALU = mybir.AluOpType
AX = mybir.AxisListType

NCORES = 8
T = 16384
TC = T // NCORES
NT = TC // 128
D = 1024
NDC = D // 128
NH = 16
DH = 64
NSB = 4
SBT = TC // NSB
NE = 8
HD = 4096
NHC = HD // 128
NEH = 14
EPS = 1e-5

ENGINES = ("sync", "gpsimd", "tensor", "vector", "scalar")


class Sched:

    def __init__(self, sems):
        self.sems = sems
        self.cnt = {k: 0 for k in sems}
        self.items = {e: [] for e in ENGINES}

    def wait(self, eng, sem, val):
        if val is None:
            raise ValueError(f"wait on unrecorded value for sem {sem}")
        if val <= 0:
            return
        self.items[eng].append(lambda sec, s=self.sems[sem], v=val: sec.wait_ge(s, v))

    def dmaw(self, eng, sem):
        self.wait(eng, sem, self.cnt[sem])

    def op(self, eng, fn, inc=None, k=1):
        if inc is not None:
            self.cnt[inc] += k
            val = self.cnt[inc]
            self.items[eng].append(
                lambda sec, f=fn, s=self.sems[inc], kk=k: f(sec).then_inc(s, kk))
            return val
        self.items[eng].append(lambda sec, f=fn: f(sec))
        return None

    def emit(self, nc):
        with nc.Block() as blk:
            for eng in ENGINES:
                if not self.items[eng]:
                    continue
                items = self.items[eng]

                def body(sec, items=items):
                    for it in items:
                        it(sec)
                getattr(blk, eng)(body)
        self.items = {e: [] for e in ENGINES}


def build_program(debug=False):
    nc = bass.Bass("TRN2", target_bir_lowering=False)

    x_in = nc.dram_tensor("x", [TC, D], F32, kind="ExternalInput").ap()
    wq_in = nc.dram_tensor("wqT", [D, D], F32, kind="ExternalInput").ap()
    wk_in = nc.dram_tensor("wkT", [D, D], F32, kind="ExternalInput").ap()
    wv_in = nc.dram_tensor("wvT", [D, D], F32, kind="ExternalInput").ap()
    wo_in = nc.dram_tensor("woT", [D, D], F32, kind="ExternalInput").ap()
    qg_in = nc.dram_tensor("qg_t", [128, DH], F32, kind="ExternalInput").ap()
    qb_in = nc.dram_tensor("qb_t", [128, DH], F32, kind="ExternalInput").ap()
    kg_in = nc.dram_tensor("kg_t", [128, DH], F32, kind="ExternalInput").ap()
    kb_in = nc.dram_tensor("kb_t", [128, DH], F32, kind="ExternalInput").ap()
    gw_in = nc.dram_tensor("gwt", [NDC, 128, NE], F32, kind="ExternalInput").ap()
    gb_in = nc.dram_tensor("gbb", [128, NE], F32, kind="ExternalInput").ap()
    w1_in = nc.dram_tensor("w1t", [7, NHC, 128, NDC, 128], F32, kind="ExternalInput").ap()
    w2_in = nc.dram_tensor("w2t", [7, NHC, 128, NDC, 128], F32, kind="ExternalInput").ap()
    b1_in = nc.dram_tensor("b1t", [7, 128, NHC], F32, kind="ExternalInput").ap()
    b2_in = nc.dram_tensor("b2t", [7, 128, NDC], F32, kind="ExternalInput").ap()
    id_in = nc.dram_tensor("ident", [128, 128], BF16, kind="ExternalInput").ap()
    em_in = nc.dram_tensor("emask", [128, 512], F32, kind="ExternalInput").ap()
    io_in = nc.dram_tensor("iota8", [128, NE], F32, kind="ExternalInput").ap()
    se_in = nc.dram_tensor("selbc", [128, NE], F32, kind="ExternalInput").ap()
    y_out = nc.dram_tensor("y", [TC, D], F32, kind="ExternalOutput").ap()

    x2d = nc.dram_tensor("x2d", [TC, D], F32).ap()
    if debug:
        dbg_xln1 = nc.dram_tensor("dbg_xln1", [128, NDC * TC], F32, kind="ExternalOutput").ap()
        dbg_xln0 = nc.dram_tensor("dbg_xln0", [128, NDC * TC], F32, kind="ExternalOutput").ap()
        dbg_qtm = nc.dram_tensor("dbg_qtm", [128, NT * D], F32, kind="ExternalOutput").ap()
        dbg_ktm = nc.dram_tensor("dbg_ktm", [128, NT * D], F32, kind="ExternalOutput").ap()
        dbg_vtm = nc.dram_tensor("dbg_vtm", [128, NT * D], F32, kind="ExternalOutput").ap()
        dbg_ofm = nc.dram_tensor("dbg_ofm", [128, NDC * SBT], F32, kind="ExternalOutput").ap()
        dbg_srun = nc.dram_tensor("dbg_srun", [64, NH * DH], F32, kind="ExternalOutput").ap()
    growd = nc.dram_tensor("growd", [NE, TC], F32).ap()
    sownd = nc.dram_tensor("sownd", [64, NH * DH], F32)
    sgath = nc.dram_tensor("sgath", [NCORES * 64, NH * DH], F32)

    with ExitStack() as es:
        sems = {}
        for name in ["sd", "gd", "pe", "dv", "ac", "pe2", "dv2", "ac2", "cc", "st",
                     "xa", "xb", "ra", "rb", "sta", "stb", "ja", "jb", "jc", "gba",
                     "wa", "wb", "wc", "wd", "pl", "we", "wf"]:
            sems[name] = es.enter_context(nc.semaphore(name))
        s = Sched(sems)

        ident = es.enter_context(nc.sbuf_tensor("c_ident", [128, 128], BF16))
        emask = es.enter_context(nc.sbuf_tensor("c_emask", [128, 512], F32))
        iota8 = es.enter_context(nc.sbuf_tensor("c_iota8", [128, NE], F32))
        selbc = es.enter_context(nc.sbuf_tensor("c_selbc", [128, NE], F32))
        gb_bc = es.enter_context(nc.sbuf_tensor("c_gbb", [128, NE], F32))
        gwf = es.enter_context(nc.sbuf_tensor("c_gwf", [128, NDC * NE], F32))
        gwh = es.enter_context(nc.sbuf_tensor("c_gwh", [128, NDC * NE], BF16))
        gwl = es.enter_context(nc.sbuf_tensor("c_gwl", [128, NDC * NE], BF16))
        b1s = es.enter_context(nc.sbuf_tensor("c_b1s", [128, 7 * NHC], F32))
        b2s = es.enter_context(nc.sbuf_tensor("c_b2s", [128, 7 * NDC], F32))
        s_run = es.enter_context(nc.sbuf_tensor("s_run", [64, NH * DH], F32))
        s_bf = es.enter_context(nc.sbuf_tensor("s_bf", [64, NH * DH], BF16))

        with ExitStack() as eAB:
            qtm = eAB.enter_context(nc.sbuf_tensor("qtm", [128, NT * D], BF16))
            ktm = eAB.enter_context(nc.sbuf_tensor("ktm", [128, NT * D], BF16))
            vtm = eAB.enter_context(nc.sbuf_tensor("vtm", [128, NT * D], BF16))

            with ExitStack() as eA:
                xln1 = eA.enter_context(nc.sbuf_tensor("xln1", [128, NDC * TC], BF16))
                wk_s = eA.enter_context(nc.sbuf_tensor("wk_s", [128, NDC * D], BF16))
                wv_s = eA.enter_context(nc.sbuf_tensor("wv_s", [128, NDC * D], BF16))
                gamq = eA.enter_context(nc.sbuf_tensor("gamq", [128, DH], F32))
                betq = eA.enter_context(nc.sbuf_tensor("betq", [128, DH], F32))
                gamk = eA.enter_context(nc.sbuf_tensor("gamk", [128, DH], F32))
                betk = eA.enter_context(nc.sbuf_tensor("betk", [128, DH], F32))
                xld = [eA.enter_context(nc.sbuf_tensor(f"xld{i}", [128, D], F32)) for i in range(2)]
                sqb = eA.enter_context(nc.sbuf_tensor("sqb", [128, D], F32))
                xlt = [eA.enter_context(nc.sbuf_tensor(f"xlt{i}", [128, D], BF16)) for i in range(2)]
                lnst = eA.enter_context(nc.sbuf_tensor("lnst", [128, 8], F32))
                pf32 = [eA.enter_context(nc.sbuf_tensor(f"pf32_{i}", [128, D], F32)) for i in range(2)]
                fscD = eA.enter_context(nc.sbuf_tensor("fsc", [128, D], F32))
                sqA = eA.enter_context(nc.sbuf_tensor("sqA", [128, D], F32))
                fscP = eA.enter_context(nc.sbuf_tensor("fscP", [128, D], F32))
                hstat2 = [eA.enter_context(nc.sbuf_tensor(f"hstat{i}", [128, 4 * NH], F32)) for i in range(2)]

                ptA = [eA.enter_context(nc.psum_tensor(f"ptA{i}", [128, 128], BF16)) for i in range(2)]
                ppA = [eA.enter_context(nc.psum_tensor(f"ppA{i}", [128, 512], F32)) for i in range(4)]
                psS = [eA.enter_context(nc.psum_tensor(f"psS{i}", [64, 512], F32)) for i in range(2)]

                gd_id = s.op("gpsimd", lambda g: g.dma_start(out=ident[:, :], in_=id_in), "gd", 16)
                for dst, src in ((emask, em_in), (iota8, io_in), (selbc, se_in), (gb_bc, gb_in)):
                    s.op("gpsimd", lambda g, d=dst, i=src: g.dma_start(out=d[:, :], in_=i), "gd", 16)
                gd_gw = s.op("gpsimd", lambda g: g.dma_start(
                    out=gwf[:, :], in_=gw_in.rearrange("c p e -> p c e")), "gd", 16)
                for dst, src in ((gamq, qg_in), (betq, qb_in), (gamk, kg_in), (betk, kb_in)):
                    s.op("gpsimd", lambda g, d=dst, i=src: g.dma_start(out=d[:, :], in_=i), "gd", 16)
                s.op("gpsimd", lambda g: g.dma_start(
                    out=b1s[:, :], in_=b1_in.rearrange("e p h -> p e h")), "gd", 16)
                s.op("gpsimd", lambda g: g.dma_start(
                    out=b2s[:, :], in_=b2_in.rearrange("e p d -> p e d")), "gd", 16)
                gd_consts = s.cnt["gd"]
                s.wait("gpsimd", "gd", gd_consts)
                gd_wk = None
                for dc in range(NDC):
                    gd_wk = s.op("gpsimd", lambda g, dc=dc: g.dma_start(
                        out=wk_s[:, dc * D:(dc + 1) * D],
                        in_=wk_in[dc * 128:(dc + 1) * 128, :]), "gd", 16)
                s.wait("gpsimd", "gd", gd_wk)
                gd_wv = None
                for dc in range(NDC):
                    gd_wv = s.op("gpsimd", lambda g, dc=dc: g.dma_start(
                        out=wv_s[:, dc * D:(dc + 1) * D],
                        in_=wv_in[dc * 128:(dc + 1) * 128, :]), "gd", 16)

                s.wait("gpsimd", "gd", gd_wv)
                s.wait("vector", "gd", gd_consts)
                a = s.op("vector", lambda v: v.tensor_copy(gwh[:, :], gwf[:, :]), "dv")
                s.wait("vector", "dv", a)
                s.op("vector", lambda v: v.tensor_sub(gwl[:, :], gwf[:, :], gwh[:, :]), "dv")

                sd_x = [None] * NT
                dv_ap1 = [None] * NT
                dv_f0 = [None] * NT
                pe_tr1 = [[None] * NDC for _ in range(NT)]
                cp1 = {}
                for t in range(2):
                    s.dmaw("sync", "xa" if t % 2 == 0 else "xb")
                    sd_x[t] = s.op("sync", lambda ss, t=t: ss.dma_start(
                        out=xld[t % 2][:, :], in_=x_in[t * 128:(t + 1) * 128, :]),
                        "xa" if t % 2 == 0 else "xb", 16)

                def emit_b0_tile(t):
                    xt = xld[t % 2]
                    s.wait("vector", "xa" if t % 2 == 0 else "xb", sd_x[t])
                    if t >= 1:
                        s.wait("vector", "dv", dv_ap1[t - 1])
                    if t >= 2:
                        s.wait("vector", "pe", pe_tr1[t - 2][NDC - 1])
                    s.wait("scalar", "xa" if t % 2 == 0 else "xb", sd_x[t])
                    if t >= 1:
                        s.wait("scalar", "dv", dv_f0[t - 1])
                    s.wait("scalar", "ac", s.cnt["ac"])
                    s.op("scalar", lambda sc, xt=xt: sc.activation(
                        sqb[:, :], xt[:, :], AF.Identity, accum_out=lnst[:, 0:1]), "ac")
                    s.wait("scalar", "ac", s.cnt["ac"])
                    sq0 = s.op("scalar", lambda sc, xt=xt: sc.activation(
                        sqb[:, :], xt[:, :], AF.Square, accum_out=lnst[:, 1:2]), "ac")
                    s.wait("vector", "ac", sq0)
                    d = s.op("vector", lambda v: v.tensor_scalar_mul(
                        lnst[:, 2:3], lnst[:, 0:1], 1.0 / D), "dv")
                    s.wait("vector", "dv", d)
                    e = s.op("vector", lambda v: v.tensor_mul(
                        lnst[:, 4:5], lnst[:, 2:3], lnst[:, 2:3]), "dv")
                    s.wait("vector", "dv", e)
                    f = s.op("vector", lambda v: v.tensor_scalar(
                        lnst[:, 5:6], lnst[:, 1:2], 1.0 / D,
                        lnst[:, 4:5], ALU.mult, ALU.subtract), "dv")
                    dv_f0[t] = f
                    s.wait("scalar", "dv", f)
                    g_ = s.op("scalar", lambda sc: sc.activation(
                        lnst[:, 6:7], lnst[:, 5:6], AF.Sqrt, bias=np.float32(EPS)), "ac")
                    s.wait("vector", "ac", g_)
                    h_ = s.op("vector", lambda v: v.reciprocal(lnst[:, 7:8], lnst[:, 6:7]), "dv")
                    s.wait("vector", "dv", h_)
                    dv_ap1[t] = s.op("vector", lambda v, xt=xt, t=t: v.tensor_scalar(
                        xlt[t % 2][:, :], xt[:, :], lnst[:, 2:3], lnst[:, 7:8],
                        ALU.subtract, ALU.mult), "dv")
                    if t + 2 < NT:
                        s.wait("sync", "dv", dv_ap1[t])
                        s.dmaw("sync", "xa" if t % 2 == 0 else "xb")
                        sd_x[t + 2] = s.op("sync", lambda ss, t=t: ss.dma_start(
                            out=xld[t % 2][:, :],
                            in_=x_in[(t + 2) * 128:(t + 3) * 128, :]),
                            "xa" if t % 2 == 0 else "xb", 16)

                    if t == 0:
                        s.wait("tensor", "gd", gd_consts)
                    s.wait("tensor", "dv", dv_ap1[t])
                    for dc in range(NDC):
                        j = t * NDC + dc
                        if j >= 2:
                            s.wait("tensor", cp1[j - 2][0], cp1[j - 2][1])
                        pe_tr1[t][dc] = s.op("tensor", lambda tn, t=t, dc=dc, j=j: tn.transpose(
                            ptA[j % 2][:, :], xlt[t % 2][:, dc * 128:(dc + 1) * 128],
                            ident[:, :]), "pe")
                        dsl = slice(dc * TC + t * 128, dc * TC + (t + 1) * 128)
                        if dc % 2 == 0:
                            s.wait("vector", "pe", pe_tr1[t][dc])
                            val = s.op("vector", lambda v, j=j, dsl=dsl: v.tensor_copy(
                                xln1[:, dsl], ptA[j % 2][:, :]), "dv2")
                            cp1[j] = ("dv2", val)
                        else:
                            s.wait("scalar", "pe", pe_tr1[t][dc])
                            val = s.op("scalar", lambda sc, j=j, dsl=dsl: sc.copy(
                                xln1[:, dsl], ptA[j % 2][:, :]), "ac2")
                            cp1[j] = ("ac2", val)

                def hln_stats(src, sq_ready_dv, hstat):
                    s.wait("scalar", "dv", sq_ready_dv)
                    s.wait("scalar", "ac", s.cnt["ac"])
                    sq_ = s.op("scalar", lambda sc, src=src: sc.activation(
                        sqA[:, :], src[:, :], AF.Square), "ac")
                    a = s.op("vector", lambda v, src=src: v.reduce_sum(
                        hstat[:, 0:16], src[:, :].rearrange("p (h d) -> p h d", d=DH),
                        axis=AX.X), "dv")
                    s.wait("vector", "ac", sq_)
                    c = s.op("vector", lambda v: v.reduce_sum(
                        hstat[:, 48:64], sqA[:, :].rearrange("p (h d) -> p h d", d=DH),
                        axis=AX.X), "dv")
                    s.wait("vector", "dv", a)
                    d = s.op("vector", lambda v: v.tensor_scalar_mul(
                        hstat[:, 32:48], hstat[:, 0:16], 1.0 / DH), "dv")
                    s.wait("vector", "dv", d)
                    e = s.op("vector", lambda v: v.tensor_mul(
                        hstat[:, 0:16], hstat[:, 32:48], hstat[:, 32:48]), "dv")
                    s.wait("vector", "dv", c)
                    f0 = s.op("vector", lambda v: v.tensor_scalar_mul(
                        hstat[:, 48:64], hstat[:, 48:64], 1.0 / DH), "dv")
                    s.wait("vector", "dv", max(f0, e))
                    f = s.op("vector", lambda v: v.tensor_sub(
                        hstat[:, 16:32], hstat[:, 48:64], hstat[:, 0:16]), "dv")
                    s.wait("scalar", "dv", f)
                    g_ = s.op("scalar", lambda sc: sc.activation(
                        hstat[:, 48:64], hstat[:, 16:32], AF.Sqrt, bias=np.float32(EPS)), "ac")
                    s.wait("vector", "ac", g_)
                    h_ = s.op("vector", lambda v: v.reciprocal(
                        hstat[:, 16:32], hstat[:, 48:64]), "dv")
                    s.wait("vector", "dv", h_)

                def hln_apply(src, dstb, dsto, gam, bet, use_pool, hstat):
                    def v3(ap):
                        return ap.rearrange("p (h d) -> p h d", d=DH)

                    def stat_bc(col0):
                        return hstat[:, col0:col0 + NH].rearrange(
                            "p (h o) -> p h o", o=1).broadcast_to([128, NH, DH])

                    def row_bc(t_):
                        return t_[:, :].rearrange(
                            "p (o d) -> p o d", o=1).broadcast_to([128, NH, DH])

                    eng, sem = ("gpsimd", "pl") if use_pool else ("vector", "dv")
                    fsc = fscP if use_pool else fscD
                    if use_pool:
                        s.wait(eng, "dv", s.cnt["dv"])
                    a = s.op(eng, lambda E, src=src: E.tensor_tensor(
                        v3(fsc[:, :]), v3(src[:, :]), stat_bc(32), ALU.subtract), sem)
                    s.wait(eng, sem, a)
                    b = s.op(eng, lambda E, src=src: E.tensor_tensor(
                        v3(src[:, :]), v3(fsc[:, :]), stat_bc(16), ALU.mult), sem)
                    s.wait(eng, sem, b)
                    cc_ = s.op(eng, lambda E, src=src, gam=gam: E.tensor_tensor(
                        v3(fsc[:, :]), v3(src[:, :]), row_bc(gam), ALU.mult), sem)
                    s.wait(eng, sem, cc_)
                    last = s.op(eng, lambda E, dstb=dstb, dsto=dsto, bet=bet: E.tensor_tensor(
                        v3(dstb[:, dsto:dsto + D]), v3(fsc[:, :]), row_bc(bet), ALU.add), sem)
                    return (sem, last)

                def proj_mms(t, wbuf):
                    last = None
                    for dc in range(NDC):
                        lsl = slice(dc * TC + t * 128, dc * TC + (t + 1) * 128)
                        for half in range(2):
                            last = s.op("tensor", lambda tn, dc=dc, half=half, lsl=lsl, w=wbuf:
                                        tn.matmul(ppA[half][:, :], xln1[:, lsl],
                                                  w[:, dc * D + half * 512: dc * D + (half + 1) * 512],
                                                  start=(dc == 0), stop=(dc == NDC - 1)), "pe")
                    return last

                pe_k = [None] * NT
                pe_v = [None] * NT
                ac_k = [None] * NT
                ac_v = [None] * NT
                dv_k = [None] * NT
                dv_b = [None] * NT
                dv_sq = [None] * NT
                dv_b2 = [None] * NT
                dv_sq2 = [None] * NT
                pe_st = [None] * NT

                def emit_b1_ktile(t):
                    if t == 0:
                        s.wait("tensor", "gd", gd_wk)
                    s.wait("tensor", cp1[t * NDC + NDC - 2][0], cp1[t * NDC + NDC - 2][1])
                    s.wait("tensor", cp1[t * NDC + NDC - 1][0], cp1[t * NDC + NDC - 1][1])
                    if t >= 1:
                        s.wait("tensor", "ac", ac_k[t - 1])
                    pe_k[t] = proj_mms(t, wk_s)
                    if t == 0:
                        s.wait("tensor", "gd", gd_wv)
                    if t >= 1:
                        s.wait("tensor", "ac", ac_v[t - 1])
                    last = None
                    for dc in range(NDC):
                        lsl = slice(dc * TC + t * 128, dc * TC + (t + 1) * 128)
                        for half in range(2):
                            last = s.op("tensor", lambda tn, dc=dc, half=half, lsl=lsl:
                                        tn.matmul(ppA[2 + half][:, :], xln1[:, lsl],
                                                  wv_s[:, dc * D + half * 512: dc * D + (half + 1) * 512],
                                                  start=(dc == 0), stop=(dc == NDC - 1)), "pe")
                    pe_v[t] = last

                    s.wait("scalar", "pe", pe_k[t])
                    if t >= 2:
                        s.wait("scalar", dv_k[t - 2][0], dv_k[t - 2][1])
                    s.op("scalar", lambda sc, t=t: sc.copy(pf32[t % 2][:, 0:512], ppA[0][:, :]), "ac")
                    ac_k[t] = s.op("scalar", lambda sc, t=t: sc.copy(
                        pf32[t % 2][:, 512:1024], ppA[1][:, :]), "ac")
                    s.wait("scalar", "pe", pe_v[t])
                    s.op("scalar", lambda sc, t=t: sc.copy(
                        vtm[:, t * D: t * D + 512], ppA[2][:, :]), "ac")
                    ac_v[t] = s.op("scalar", lambda sc, t=t: sc.copy(
                        vtm[:, t * D + 512:(t + 1) * D], ppA[3][:, :]), "ac")

                    s.wait("vector", "ac", ac_k[t])
                    if t >= 2:
                        s.wait("vector", dv_k[t - 2][0], dv_k[t - 2][1])
                    hln_stats(pf32[t % 2], dv_sq[t - 1] if t >= 1 else 0, hstat2[t % 2])
                    dv_sq[t] = s.cnt["dv"]
                    dv_k[t] = hln_apply(pf32[t % 2], ktm, t * D, gamk, betk, t % 2 == 1,
                                        hstat2[t % 2])
                    dv_b[t] = s.cnt["dv"]

                    s.wait("tensor", dv_k[t][0], dv_k[t][1])
                    s.wait("tensor", "ac", ac_v[t])
                    lastst = None
                    for h in range(NH):
                        lastst = s.op("tensor", lambda tn, t=t, h=h: tn.matmul(
                            psS[h // 8][:, (h % 8) * DH:(h % 8 + 1) * DH],
                            ktm[:, t * D + h * DH: t * D + (h + 1) * DH],
                            vtm[:, t * D + h * DH: t * D + (h + 1) * DH],
                            start=(t == 0 and h % 8 == 0),
                            stop=(t == NT - 1 and h % 8 == 7)), "pe")
                    pe_st[t] = lastst

                for t in range(NT):
                    emit_b0_tile(t)
                    if t >= 1:
                        emit_b1_ktile(t - 1)
                emit_b1_ktile(NT - 1)

                s.wait("gpsimd", "pe", pe_k[NT - 1])
                gd_wq = None
                for dc in range(NDC):
                    gd_wq = s.op("gpsimd", lambda g, dc=dc: g.dma_start(
                        out=wk_s[:, dc * D:(dc + 1) * D],
                        in_=wq_in[dc * 128:(dc + 1) * 128, :]), "gd", 16)
                s.wait("gpsimd", "gd", gd_wq)

                pe_q = [None] * NT
                ac_q = [None] * NT
                dv_q = [None] * NT
                for t in range(NT):
                    if t == 0:
                        s.wait("tensor", "gd", gd_wq)
                        s.wait("tensor", "ac", ac_k[NT - 1])
                    if t >= 1:
                        s.wait("tensor", "ac", ac_q[t - 1])
                    pe_q[t] = proj_mms(t, wk_s)
                    s.wait("scalar", "pe", pe_q[t])
                    if t >= 2:
                        s.wait("scalar", dv_q[t - 2][0], dv_q[t - 2][1])
                    else:
                        s.wait("scalar", dv_k[NT - 1][0], dv_k[NT - 1][1])
                    s.op("scalar", lambda sc, t=t: sc.copy(pf32[t % 2][:, 0:512], ppA[0][:, :]), "ac")
                    ac_q[t] = s.op("scalar", lambda sc, t=t: sc.copy(
                        pf32[t % 2][:, 512:1024], ppA[1][:, :]), "ac")
                    s.wait("vector", "ac", ac_q[t])
                    prev = dv_q[t - 2] if t >= 2 else dv_k[NT - 2 + t]
                    s.wait("vector", prev[0], prev[1])
                    hln_stats(pf32[t % 2], dv_sq2[t - 1] if t >= 1 else dv_sq[NT - 1],
                              hstat2[t % 2])
                    dv_sq2[t] = s.cnt["dv"]
                    dv_q[t] = hln_apply(pf32[t % 2], qtm, t * D, gamq, betq, t % 2 == 1,
                                        hstat2[t % 2])
                    dv_b2[t] = s.cnt["dv"]

                s.wait("vector", "pe", pe_st[NT - 1])
                s.wait("vector", dv_q[NT - 1][0], dv_q[NT - 1][1])
                s.wait("vector", dv_q[NT - 2][0], dv_q[NT - 2][1])
                s.op("vector", lambda v: v.tensor_copy(fscD[0:64, 0:512], psS[0][:, :]), "dv")
                dv_sown = s.op("vector", lambda v: v.tensor_copy(fscD[0:64, 512:1024], psS[1][:, :]), "dv")
                s.wait("sync", "dv", dv_sown)
                s.dmaw("sync", "sd")
                sd_sown = s.op("sync", lambda ss: ss.dma_start(out=sownd.ap(), in_=fscD[0:64, :]), "sd", 16)
                s.wait("gpsimd", "sd", sd_sown)
                s.op("gpsimd", lambda g: g.collective_compute(
                    "AllGather", ALU.bypass, replica_groups=[list(range(NCORES))],
                    ins=[sownd.ap().opt()], outs=[sgath.ap().opt()]), "cc", 1)

                s.emit(nc)
                if debug:
                    sdd = s.op("gpsimd", lambda g: g.dma_start(out=dbg_xln1, in_=xln1[:, :]), "gd", 16)
                    sdd = s.op("gpsimd", lambda g: g.dma_start(out=dbg_qtm, in_=qtm[:, :]), "gd", 16)
                    sdd = s.op("gpsimd", lambda g: g.dma_start(out=dbg_ktm, in_=ktm[:, :]), "gd", 16)
                    sdd = s.op("gpsimd", lambda g: g.dma_start(out=dbg_vtm, in_=vtm[:, :]), "gd", 16)
                    s.wait("gpsimd", "gd", sdd)
                    s.emit(nc)

            with ExitStack() as eB:
                wo_s = eB.enter_context(nc.sbuf_tensor("wo_s", [128, NDC * D], BF16))
                qfm = eB.enter_context(nc.sbuf_tensor("qfm", [64, NH * SBT], BF16))
                kfm = eB.enter_context(nc.sbuf_tensor("kfm", [64, NH * SBT], BF16))
                ofm = eB.enter_context(nc.sbuf_tensor("ofm", [128, NDC * SBT], BF16))
                stb = [eB.enter_context(nc.sbuf_tensor(f"stb{i}", [128, SBT], BF16)) for i in range(4)]
                sgt = eB.enter_context(nc.sbuf_tensor("sgt", [64, NH * DH], F32))
                tmpS = eB.enter_context(nc.sbuf_tensor("tmpS", [64, NH * DH], F32))
                xrl = [eB.enter_context(nc.sbuf_tensor(f"xrl{i}", [128, D], F32)) for i in range(2)]
                x2st = [eB.enter_context(nc.sbuf_tensor(f"x2st{i}", [128, D], F32)) for i in range(2)]

                ptB = [eB.enter_context(nc.psum_tensor(f"ptB{i}", [128, 128], BF16)) for i in range(2)]
                psc = [eB.enter_context(nc.psum_tensor(f"psc{i}", [128, 512], F32)) for i in range(2)]
                po = [eB.enter_context(nc.psum_tensor(f"po{i}", [64, 512], F32)) for i in range(2)]
                psS2 = [eB.enter_context(nc.psum_tensor(f"psS2_{i}", [64, 512], F32)) for i in range(2)]

                gd_wo = None
                for dc in range(NDC):
                    gd_wo = s.op("gpsimd", lambda g, dc=dc: g.dma_start(
                        out=wo_s[:, dc * D:(dc + 1) * D],
                        in_=wo_in[dc * 128:(dc + 1) * 128, :]), "gd", 16)
                s.wait("gpsimd", "gd", gd_wo)
                s.wait("gpsimd", "cc", 1)
                dv_comb = [None] * NCORES
                gd_sg = [None] * NCORES
                gd_sg[0] = s.op("gpsimd", lambda g: g.dma_start(
                    out=sgt[:, :], in_=sgath.ap()[0:64, :]), "gd", 16)
                for r in range(NCORES):
                    s.wait("vector", "gd", gd_sg[r])
                    if r == 0:
                        dv_comb[0] = s.op("vector", lambda v: v.tensor_scalar_mul(
                            s_run[:, :], sgt[:, :], selbc[0:64, 0:1]), "dv")
                    else:
                        s.wait("vector", "dv", dv_comb[r - 1])
                        a = s.op("vector", lambda v, r=r: v.tensor_scalar_mul(
                            tmpS[:, :], sgt[:, :], selbc[0:64, r:r + 1]), "dv")
                        s.wait("vector", "dv", a)
                        dv_comb[r] = s.op("vector", lambda v: v.tensor_add(
                            s_run[:, :], s_run[:, :], tmpS[:, :]), "dv")
                    if r + 1 < NCORES:
                        s.wait("gpsimd", "dv", dv_comb[r] if r == 0 else a)
                        s.dmaw("gpsimd", "gd")
                        gd_sg[r + 1] = s.op("gpsimd", lambda g, r=r: g.dma_start(
                            out=sgt[:, :], in_=sgath.ap()[(r + 1) * 64:(r + 2) * 64, :]), "gd", 16)
                s.wait("scalar", "dv", dv_comb[NCORES - 1])
                sbf_ready = s.op("scalar", lambda sc: sc.copy(s_bf[:, :], s_run[:, :]), "ac")

                def emit_transposes(sb):
                    cpd = {}
                    for qi, (src, dstb) in enumerate(((qtm, qfm), (ktm, kfm))):
                        for tt in range(NSB):
                            t = sb * NSB + tt
                            for dc in range(NDC):
                                j = qi * NSB * NDC + tt * NDC + dc
                                if j >= 2:
                                    s.wait("tensor", "dv2", cpd[j - 2][0])
                                    s.wait("tensor", "ac2", cpd[j - 2][1])
                                tv = s.op("tensor", lambda tn, t=t, dc=dc, j=j, src=src:
                                          tn.transpose(ptB[j % 2][:, :],
                                                       src[:, t * D + dc * 128: t * D + (dc + 1) * 128],
                                                       ident[:, :]), "pe")
                                hlo, hhi = 2 * dc, 2 * dc + 1
                                s.wait("vector", "pe", tv)
                                v1 = s.op("vector", lambda v, j=j, b=dstb, hlo=hlo, tt=tt:
                                          v.tensor_copy(
                                              b[:, hlo * SBT + tt * 128: hlo * SBT + (tt + 1) * 128],
                                              ptB[j % 2][0:64, :]), "dv2")
                                s.wait("scalar", "pe", tv)
                                v2 = s.op("scalar", lambda sc, j=j, b=dstb, hhi=hhi, tt=tt:
                                          sc.copy(
                                              b[:, hhi * SBT + tt * 128: hhi * SBT + (tt + 1) * 128],
                                              ptB[j % 2][64:128, :]), "ac2")
                                cpd[j] = (v1, v2)
                    return {"dv2": s.cnt["dv2"], "ac2": s.cnt["ac2"]}

                tr_done = emit_transposes(0)
                s.emit(nc)

                dv_x2 = {}
                sd_xr = {}
                sd_x2w = {}
                for sb in range(NSB):
                    s.wait("tensor", "dv2", tr_done["dv2"])
                    s.wait("tensor", "ac2", tr_done["ac2"])
                    s.wait("tensor", "ac", sbf_ready)
                    if (sb * NSB - 1, 1) in dv_x2:
                        s.wait("tensor", "dv", dv_x2[(sb * NSB - 1, 1)])

                    me = {}
                    oev = {}
                    av_done = [None] * NH
                    sinc_last = [None] * NH
                    sc_done = {}

                    def emit_qs(h):
                        if h >= 2:
                            s.wait("tensor", oev[h - 2][0], oev[h - 2][1])
                        s.op("tensor", lambda tn, h=h: tn.matmul(
                            po[h % 2][:, :], s_bf[:, h * DH:(h + 1) * DH],
                            qfm[:, h * SBT:(h + 1) * SBT],
                            start=True, stop=False), "pe")

                    def emit_st(h, kc):
                        m = h * NSB + kc
                        n = SBT - kc * 128
                        if m >= 2:
                            s.wait("tensor", "dv", me[m - 2])
                        sc_done[m] = s.op("tensor", lambda tn, h=h, kc=kc, m=m, n=n:
                                          tn.matmul(psc[m % 2][:, 0:n],
                                                    kfm[:, h * SBT + kc * 128: h * SBT + (kc + 1) * 128],
                                                    qfm[:, h * SBT + kc * 128:(h + 1) * SBT],
                                                    start=True, stop=True), "pe")
                        s.wait("vector", "pe", sc_done[m])
                        me[m] = s.op("vector", lambda v, m=m, n=n: v.tensor_mul(
                            stb[m % 4][:, 0:n], psc[m % 2][:, 0:n], emask[:, 0:n]), "dv")

                    def emit_av(h, kc):
                        m = h * NSB + kc
                        n = SBT - kc * 128
                        s.wait("tensor", "dv", me[m])
                        av_done[h] = s.op("tensor", lambda tn, h=h, kc=kc, m=m, sb=sb, n=n:
                                          tn.matmul(po[h % 2][:, kc * 128:SBT],
                                                    vtm[:, (sb * NSB + kc) * D + h * DH:
                                                        (sb * NSB + kc) * D + (h + 1) * DH],
                                                    stb[m % 4][:, 0:n],
                                                    start=False, stop=(kc == NSB - 1)), "pe")

                    def emit_sinc(h):
                        for kc in range(NSB):
                            t = sb * NSB + kc
                            sinc_last[h] = s.op("tensor", lambda tn, t=t, h=h, kc=kc: tn.matmul(
                                psS2[h // 8][:, (h % 8) * DH:(h % 8 + 1) * DH],
                                ktm[:, t * D + h * DH: t * D + (h + 1) * DH],
                                vtm[:, t * D + h * DH: t * D + (h + 1) * DH],
                                start=(kc == 0 and h % 8 == 0),
                                stop=(kc == NSB - 1 and h % 8 == 7)), "pe")

                    def emit_oev(h):
                        c = h // 2
                        rlo = (h % 2) * 64
                        if h % 2 == 0:
                            s.wait("vector", "pe", av_done[h])
                            val = s.op("vector", lambda v, h=h, c=c, rlo=rlo: v.tensor_copy(
                                ofm[rlo:rlo + 64, c * SBT:(c + 1) * SBT], po[h % 2][:, :]), "dv2")
                            oev[h] = ("dv2", val)
                        else:
                            s.wait("scalar", "pe", av_done[h])
                            val = s.op("scalar", lambda sc, h=h, c=c, rlo=rlo: sc.copy(
                                ofm[rlo:rlo + 64, c * SBT:(c + 1) * SBT], po[h % 2][:, :]), "ac2")
                            oev[h] = ("ac2", val)

                    for h in range(NH):
                        emit_qs(h)
                        emit_st(h, 0)
                        emit_st(h, 1)
                        if h >= 1:
                            emit_av(h - 1, 2)
                            emit_av(h - 1, 3)
                            emit_sinc(h - 1)
                            emit_oev(h - 1)
                        emit_st(h, 2)
                        emit_st(h, 3)
                        emit_av(h, 0)
                        emit_av(h, 1)
                    emit_av(NH - 1, 2)
                    emit_av(NH - 1, 3)
                    emit_sinc(NH - 1)
                    emit_oev(NH - 1)
                    last_sinc = sinc_last[NH - 1]

                    s.wait("vector", "pe", last_sinc)
                    a = s.op("vector", lambda v: v.tensor_add(
                        s_run[:, 0:512], s_run[:, 0:512], psS2[0][:, :]), "dv")
                    s.wait("vector", "dv", a)
                    dv_supd = s.op("vector", lambda v: v.tensor_add(
                        s_run[:, 512:1024], s_run[:, 512:1024], psS2[1][:, :]), "dv")
                    if sb < NSB - 1:
                        s.wait("scalar", "dv", dv_supd)
                        sbf_ready = s.op("scalar", lambda sc: sc.copy(s_bf[:, :], s_run[:, :]), "ac")
                        tr_done = emit_transposes(sb + 1)

                    oev_dv2 = s.cnt["dv2"]
                    oev_ac2 = s.cnt["ac2"]
                    s.wait("tensor", "dv2", oev_dv2)
                    s.wait("tensor", "ac2", oev_ac2)
                    if sb == 0:
                        s.wait("tensor", "gd", gd_wo)
                    for tt in range(NSB):
                        tglob = sb * NSB + tt
                        if (tglob - 2, 1) in dv_x2:
                            s.wait("sync", "dv", dv_x2[(tglob - 2, 1)])
                        s.dmaw("sync", "ra" if tglob % 2 == 0 else "rb")
                        sd_xr[tglob] = s.op("sync", lambda ss, tglob=tglob: ss.dma_start(
                            out=xrl[tglob % 2][:, :], in_=x_in[tglob * 128:(tglob + 1) * 128, :]),
                            "ra" if tglob % 2 == 0 else "rb", 16)
                        for half in range(2):
                            if tt == 0:
                                s.wait("tensor", "dv", me[NH * NSB - 1])
                            else:
                                s.wait("tensor", "dv", dv_x2[(tglob - 1, half)])
                            last = None
                            for c in range(NDC):
                                last = s.op("tensor", lambda tn, tt=tt, half=half, c=c:
                                            tn.matmul(psc[half][:, :],
                                                      ofm[:, c * SBT + tt * 128: c * SBT + (tt + 1) * 128],
                                                      wo_s[:, c * D + half * 512: c * D + (half + 1) * 512],
                                                      start=(c == 0), stop=(c == NDC - 1)), "pe")
                            s.wait("vector", "pe", last)
                            s.wait("vector", "ra" if tglob % 2 == 0 else "rb", sd_xr[tglob])
                            if tglob >= 2:
                                s.wait("vector", "sta" if tglob % 2 == 0 else "stb",
                                       sd_x2w[tglob - 2])
                            dv_x2[(tglob, half)] = s.op("vector", lambda v, tglob=tglob, half=half:
                                                        v.tensor_add(
                                                            x2st[tglob % 2][:, half * 512:(half + 1) * 512],
                                                            xrl[tglob % 2][:, half * 512:(half + 1) * 512],
                                                            psc[half][:, :]), "dv")
                        s.wait("sync", "dv", dv_x2[(tglob, 0)])
                        s.wait("sync", "dv", dv_x2[(tglob, 1)])
                        s.dmaw("sync", "sta" if tglob % 2 == 0 else "stb")
                        sd_x2w[tglob] = s.op("sync", lambda ss, tglob=tglob: ss.dma_start(
                            out=x2d[tglob * 128:(tglob + 1) * 128, :],
                            in_=x2st[tglob % 2][:, :]),
                            "sta" if tglob % 2 == 0 else "stb", 16)
                    s.emit(nc)
                    if debug and sb == 0:
                        sdd = s.op("gpsimd", lambda g: g.dma_start(out=dbg_ofm, in_=ofm[:, :]), "gd", 16)
                        sdd = s.op("gpsimd", lambda g: g.dma_start(out=dbg_srun, in_=s_run[:, :]), "gd", 16)
                        s.wait("gpsimd", "gd", sdd)
                        s.emit(nc)

        with ExitStack() as eC:
            xhfm = eC.enter_context(nc.sbuf_tensor("xhfm", [128, NDC * TC], BF16))
            yac = eC.enter_context(nc.sbuf_tensor("yac", [128, NDC * TC], BF16))
            G_all = eC.enter_context(nc.sbuf_tensor("G_all", [128, 128], BF16))
            GTf = eC.enter_context(nc.sbuf_tensor("GTf", [128, 128], F32))

            with ExitStack() as eD:
                x2l = [eD.enter_context(nc.sbuf_tensor(f"x2l{i}", [128, D], F32)) for i in range(3)]
                xl2tm = [eD.enter_context(nc.sbuf_tensor(f"xl2tm{i}", [128, D], F32)) for i in range(2)]
                xh_tm = [eD.enter_context(nc.sbuf_tensor(f"xh_tm{i}", [128, D], BF16)) for i in range(2)]
                xl_tm = [eD.enter_context(nc.sbuf_tensor(f"xl_tm{i}", [128, D], BF16)) for i in range(2)]
                xlfm = [eD.enter_context(nc.sbuf_tensor(f"xlfm{i}", [128, NDC * 128], BF16)) for i in range(2)]
                sqb3 = eD.enter_context(nc.sbuf_tensor("sqb3", [128, D], F32))
                lnst3 = eD.enter_context(nc.sbuf_tensor("lnst3", [128, 8], F32))
                Lt4 = eD.enter_context(nc.sbuf_tensor("Lt4", [128, 32], F32))
                tA = eD.enter_context(nc.sbuf_tensor("tA", [128, 32], F32))
                tB = eD.enter_context(nc.sbuf_tensor("tB", [128, 32], F32))
                q4 = [eD.enter_context(nc.sbuf_tensor(f"q4_{i}", [128, 4], F32)) for i in range(8)]
                ptC = [eD.enter_context(nc.psum_tensor(f"ptC{i}", [128, 128], BF16)) for i in range(2)]
                pgl = [eD.enter_context(nc.psum_tensor(f"pgl{i}", [128, 2 * NE], F32)) for i in range(2)]

                dv_g = [None] * NT
                dv_tile = [None] * NT
                dv_f3 = [None] * NT
                dv_ap3 = [None] * NT
                lt4_done = [None] * (NT // 4)
                sd_x2r = [None] * NT
                pe_gate = [None] * NT
                ac_hi = [None] * NT
                cpC = {}
                trj = 0

                s.op("vector", lambda v: v.memset(G_all[:, 0:16], 0.0), "dv")

                s.wait("sync", "sta", s.cnt["sta"])
                s.wait("sync", "stb", s.cnt["stb"])
                for t in range(3):
                    s.dmaw("sync", ["ja", "jb", "jc"][t % 3])
                    sd_x2r[t] = s.op("sync", lambda ss, t=t: ss.dma_start(
                        out=x2l[t % 3][:, :], in_=x2d[t * 128:(t + 1) * 128, :]),
                        ["ja", "jb", "jc"][t % 3], 16)

                for t in range(NT):
                    xt = x2l[t % 3]
                    s.wait("vector", ["ja", "jb", "jc"][t % 3], sd_x2r[t])
                    if t >= 1:
                        s.wait("vector", "dv", dv_tile[t - 1])
                    if t >= 2:
                        s.wait("vector", "ac", ac_hi[t - 2])
                    s.wait("scalar", ["ja", "jb", "jc"][t % 3], sd_x2r[t])
                    if t >= 1:
                        s.wait("scalar", "dv", dv_f3[t - 1])
                    s.wait("scalar", "ac", s.cnt["ac"])
                    s.op("scalar", lambda sc, xt=xt: sc.activation(
                        sqb3[:, :], xt[:, :], AF.Identity, accum_out=lnst3[:, 0:1]), "ac")
                    s.wait("scalar", "ac", s.cnt["ac"])
                    sq3 = s.op("scalar", lambda sc, xt=xt: sc.activation(
                        sqb3[:, :], xt[:, :], AF.Square, accum_out=lnst3[:, 1:2]), "ac")
                    s.wait("vector", "ac", sq3)
                    d = s.op("vector", lambda v: v.tensor_scalar_mul(lnst3[:, 2:3], lnst3[:, 0:1], 1.0 / D), "dv")
                    s.wait("vector", "dv", d)
                    e = s.op("vector", lambda v: v.tensor_mul(lnst3[:, 4:5], lnst3[:, 2:3], lnst3[:, 2:3]), "dv")
                    s.wait("vector", "dv", e)
                    f = s.op("vector", lambda v: v.tensor_scalar(
                        lnst3[:, 5:6], lnst3[:, 1:2], 1.0 / D, lnst3[:, 4:5],
                        ALU.mult, ALU.subtract), "dv")
                    dv_f3[t] = f
                    s.wait("scalar", "dv", f)
                    g_ = s.op("scalar", lambda sc: sc.activation(
                        lnst3[:, 6:7], lnst3[:, 5:6], AF.Sqrt, bias=np.float32(EPS)), "ac")
                    s.wait("vector", "ac", g_)
                    h_ = s.op("vector", lambda v: v.reciprocal(lnst3[:, 7:8], lnst3[:, 6:7]), "dv")
                    s.wait("vector", "dv", h_)
                    ap_ = s.op("vector", lambda v, xt=xt, t=t: v.tensor_scalar(
                        xl2tm[t % 2][:, :], xt[:, :], lnst3[:, 2:3], lnst3[:, 7:8],
                        ALU.subtract, ALU.mult), "dv")
                    dv_ap3[t] = ap_
                    s.wait("scalar", "dv", ap_)
                    if t >= 2:
                        s.wait("scalar", "pe", pe_gate[t - 2])
                    ac_hi[t] = s.op("scalar", lambda sc, t=t: sc.copy(
                        xh_tm[t % 2][:, :], xl2tm[t % 2][:, :]), "ac")
                    s.wait("vector", "ac", ac_hi[t])
                    lo_ = s.op("vector", lambda v, t=t: v.tensor_sub(
                        xl_tm[t % 2][:, :], xl2tm[t % 2][:, :], xh_tm[t % 2][:, :]), "dv")

                    s.wait("tensor", "ac", ac_hi[t])
                    s.wait("tensor", "dv", lo_)
                    for srcb, kind in ((xh_tm, "h"), (xl_tm, "l")):
                        for dc in range(NDC):
                            j = trj
                            trj += 1
                            if j >= 2:
                                s.wait("tensor", cpC[j - 2][0], cpC[j - 2][1])
                            tv = s.op("tensor", lambda tn, t=t, dc=dc, j=j, srcb=srcb: tn.transpose(
                                ptC[j % 2][:, :], srcb[t % 2][:, dc * 128:(dc + 1) * 128],
                                ident[:, :]), "pe")
                            if kind == "h":
                                dsl = slice(dc * TC + t * 128, dc * TC + (t + 1) * 128)
                                s.wait("scalar", "pe", tv)
                                val = s.op("scalar", lambda sc, j=j, dsl=dsl: sc.copy(
                                    xhfm[:, dsl], ptC[j % 2][:, :]), "ac2")
                                cpC[j] = ("ac2", val)
                            else:
                                dsl = slice(dc * 128, (dc + 1) * 128)
                                s.wait("vector", "pe", tv)
                                val = s.op("vector", lambda v, j=j, dsl=dsl, t=t: v.tensor_copy(
                                    xlfm[t % 2][:, dsl], ptC[j % 2][:, :]), "dv2")
                                cpC[j] = ("dv2", val)

                    s.wait("tensor", "ac2", cpC[t * 16 + 7][1])
                    s.wait("tensor", "dv2", cpC[t * 16 + 15][1])
                    g = t // 4
                    bank = (t % 4) // 2
                    col0 = (t % 2) * NE
                    if g >= 1 and t % 4 == 0:
                        s.wait("tensor", "dv", lt4_done[g - 1])
                    nmm = 0
                    for term in range(3):
                        rbuf = gwh if term != 1 else gwl
                        uselo = (term == 2)
                        for dc in range(NDC):
                            st_ = (nmm == 0 and t % 2 == 0)
                            sp_ = (nmm == 23 and t % 2 == 1)
                            if uselo:
                                pe_gate[t] = s.op("tensor", lambda tn, dc=dc, t=t, rbuf=rbuf,
                                                  st_=st_, sp_=sp_, bank=bank, col0=col0: tn.matmul(
                                    pgl[bank][:, col0:col0 + NE],
                                    xlfm[t % 2][:, dc * 128:(dc + 1) * 128],
                                    rbuf[:, dc * NE:(dc + 1) * NE], start=st_, stop=sp_), "pe")
                            else:
                                pe_gate[t] = s.op("tensor", lambda tn, dc=dc, t=t, rbuf=rbuf,
                                                  st_=st_, sp_=sp_, bank=bank, col0=col0: tn.matmul(
                                    pgl[bank][:, col0:col0 + NE],
                                    xhfm[:, dc * TC + t * 128: dc * TC + (t + 1) * 128],
                                    rbuf[:, dc * NE:(dc + 1) * NE], start=st_, stop=sp_), "pe")
                            nmm += 1
                    dv_tile[t] = s.cnt["dv"]

                    if t % 4 == 3:
                        s.wait("vector", "pe", pe_gate[t])

                        def dvc(fn):
                            val = s.op("vector", fn, "dv")
                            s.wait("vector", "dv", val)
                            return val

                        def v48(ap):
                            return ap.rearrange("p (g e) -> p g e", e=NE)

                        def bc4(t4_):
                            return t4_[:, :].rearrange(
                                "p (g o) -> p g o", o=1).broadcast_to([128, 4, NE])

                        iota_bc = iota8[:, :].rearrange(
                            "p (o e) -> p o e", o=1).broadcast_to([128, 4, NE])
                        gb2 = gb_bc[:, :].rearrange(
                            "p (o e) -> p o e", o=1).broadcast_to([128, 2, NE])
                        dvc(lambda v: v.tensor_tensor(
                            Lt4[:, 0:16].rearrange("p (g e) -> p g e", e=NE),
                            pgl[0][:, :].rearrange("p (g e) -> p g e", e=NE),
                            gb2, ALU.add))
                        lt4_done[g] = dvc(lambda v: v.tensor_tensor(
                            Lt4[:, 16:32].rearrange("p (g e) -> p g e", e=NE),
                            pgl[1][:, :].rearrange("p (g e) -> p g e", e=NE),
                            gb2, ALU.add))
                        dvc(lambda v: v.tensor_reduce(q4[0][:, :], v48(Lt4[:, :]), AX.X, ALU.max))
                        dvc(lambda v: v.tensor_tensor(v48(tA[:, :]), v48(Lt4[:, :]),
                                                      bc4(q4[0]), ALU.subtract))
                        s.wait("scalar", "dv", s.cnt["dv"])
                        ex_ = s.op("scalar", lambda sc: sc.activation(tB[:, :], tA[:, :], AF.Exp), "ac")
                        s.wait("vector", "ac", ex_)
                        dvc(lambda v: v.tensor_reduce(q4[1][:, :], v48(tB[:, :]), AX.X, ALU.add))
                        dvc(lambda v: v.reciprocal(q4[2][:, :], q4[1][:, :]))
                        dvc(lambda v: v.tensor_tensor(v48(tA[:, :]), v48(Lt4[:, :]),
                                                      bc4(q4[0]), ALU.is_equal))
                        dvc(lambda v: v.tensor_scalar(tA[:, :], tA[:, :], -64.0, 64.0,
                                                      ALU.mult, ALU.add))
                        dvc(lambda v: v.tensor_tensor(v48(tA[:, :]), v48(tA[:, :]),
                                                      iota_bc, ALU.add))
                        dvc(lambda v: v.tensor_reduce(q4[3][:, :], v48(tA[:, :]), AX.X, ALU.min))
                        dvc(lambda v: v.tensor_tensor(v48(tA[:, :]), iota_bc,
                                                      bc4(q4[3]), ALU.is_equal))
                        dvc(lambda v: v.scalar_tensor_tensor(tA[:, :], tA[:, :], -1e30,
                                                             Lt4[:, :], ALU.mult, ALU.add))
                        dvc(lambda v: v.tensor_reduce(q4[4][:, :], v48(tA[:, :]), AX.X, ALU.max))
                        dvc(lambda v: v.tensor_tensor(v48(tA[:, :]), v48(tA[:, :]),
                                                      bc4(q4[4]), ALU.is_equal))
                        dvc(lambda v: v.tensor_scalar(tA[:, :], tA[:, :], -64.0, 64.0,
                                                      ALU.mult, ALU.add))
                        dvc(lambda v: v.tensor_tensor(v48(tA[:, :]), v48(tA[:, :]),
                                                      iota_bc, ALU.add))
                        dvc(lambda v: v.tensor_reduce(q4[5][:, :], v48(tA[:, :]), AX.X, ALU.min))
                        dvc(lambda v: v.tensor_max(q4[6][:, :], q4[3][:, :], q4[5][:, :]))
                        dvc(lambda v: v.tensor_tensor(q4[7][:, :], q4[3][:, :], q4[5][:, :], ALU.is_gt))
                        dvc(lambda v: v.tensor_sub(q4[0][:, :], q4[0][:, :], q4[4][:, :]))
                        dvc(lambda v: v.tensor_mul(q4[0][:, :], q4[0][:, :], q4[7][:, :]))
                        dvc(lambda v: v.tensor_add(q4[0][:, :], q4[0][:, :], q4[4][:, :]))
                        dvc(lambda v: v.tensor_reduce(q4[1][:, :], v48(Lt4[:, :]), AX.X, ALU.max))
                        dvc(lambda v: v.tensor_sub(q4[0][:, :], q4[0][:, :], q4[1][:, :]))
                        s.wait("scalar", "dv", s.cnt["dv"])
                        ex2 = s.op("scalar", lambda sc: sc.activation(
                            q4[1][:, :], q4[0][:, :], AF.Exp), "ac")
                        s.wait("vector", "ac", ex2)
                        dvc(lambda v: v.tensor_mul(q4[1][:, :], q4[1][:, :], q4[2][:, :]))
                        lastg = None
                        for e in range(1, NE):
                            if lastg is not None:
                                s.wait("vector", "dv", lastg)
                            aa = s.op("vector", lambda v, e=e: v.tensor_scalar(
                                tA[:, 0:4], q4[6][:, :], float(e), None, ALU.is_equal), "dv")
                            s.wait("vector", "dv", aa)
                            lastg = s.op("vector", lambda v, e=e, g=g: v.tensor_mul(
                                G_all[:, e * 16 + 4 * g: e * 16 + 4 * g + 4],
                                tA[:, 0:4], q4[1][:, :]), "dv")
                        for tt_ in range(4 * g, 4 * g + 4):
                            dv_g[tt_] = lastg
                        dv_tile[t] = lastg
                    if t + 3 < NT:
                        s.wait("sync", "dv", dv_ap3[t])
                        s.dmaw("sync", ["ja", "jb", "jc"][t % 3])
                        sd_x2r[t + 3] = s.op("sync", lambda ss, t=t: ss.dma_start(
                            out=x2l[t % 3][:, :],
                            in_=x2d[(t + 3) * 128:(t + 4) * 128, :]),
                            ["ja", "jb", "jc"][t % 3], 16)

                s.wait("tensor", "dv", dv_g[NT - 1])
                s.wait("tensor", cpC[trj - 2][0], cpC[trj - 2][1])
                s.wait("tensor", cpC[trj - 1][0], cpC[trj - 1][1])
                gtr = s.op("tensor", lambda tn: tn.transpose(ptC[0][:, :], G_all[:, :], ident[:, :]), "pe")
                s.wait("vector", "pe", gtr)
                gcp = s.op("vector", lambda v: v.tensor_copy(GTf[:, :], ptC[0][:, :]), "dv")
                s.wait("sync", "dv", gcp)
                sd_grow = None
                for e in range(1, NE):
                    sd_grow = s.op("sync", lambda ss, e=e: ss.dma_start(
                        out=growd[e, :].rearrange("(a t) -> a t", t=128),
                        in_=GTf[e * 16:(e + 1) * 16, :]), "sd", 16)
                s.wait("sync", "sd", sd_grow)
                s.emit(nc)

            with ExitStack() as eE:
                hbuf = eE.enter_context(nc.sbuf_tensor("hbuf", [128, NHC * 1024], BF16))
                xe = [eE.enter_context(nc.sbuf_tensor(f"xe{i}", [128, NDC * 1024], BF16)) for i in range(2)]
                w1b = [eE.enter_context(nc.sbuf_tensor(f"w1b{i}", [128, 2048], BF16)) for i in range(3)]
                w2b = [eE.enter_context(nc.sbuf_tensor(f"w2b{i}", [128, 1024], BF16)) for i in range(2)]
                gbc = eE.enter_context(nc.sbuf_tensor("gbc", [128, TC], F32))
                m01 = [eE.enter_context(nc.sbuf_tensor(f"m01_{i}", [128, TC], BF16)) for i in range(2)]
                ytmp = [eE.enter_context(nc.sbuf_tensor(f"ytmp{i}", [128, 512], F32)) for i in range(2)]
                ps = [eE.enter_context(nc.psum_tensor(f"ps{i}", [128, 512], F32)) for i in range(8)]

                base = {k: s.cnt[k] for k in ("pe", "dv", "ac", "gd", "sd")}
                NW1 = 16
                NW2 = 32

                sd_gbc = {}
                for ei in range(7):
                    sd_gbc[ei] = s.cnt["gba"] + 16 * (ei + 1)
                gd_w1, gd_w2 = {}, {}
                w1sem, w2sem = {}, {}
                W1S = ["wa", "wb", "we"]
                W2S = ["wc", "wd"]
                w1c = {i: s.cnt[W1S[i]] for i in range(3)}
                w2c = {i: s.cnt[W2S[i]] for i in range(2)}
                g1i = 0
                g2i = 0
                for eh in range(NEH):
                    for fi in range(NW1):
                        p = g1i % 3
                        g1i += 1
                        w1c[p] += 16
                        gd_w1[(eh, fi)] = w1c[p]
                        w1sem[(eh, fi)] = W1S[p]
                    for dg in range(4):
                        for fi in range(8):
                            p = g2i % 2
                            g2i += 1
                            w2c[p] += 16
                            gd_w2[(eh, dg, fi)] = w2c[p]
                            w2sem[(eh, dg, fi)] = W2S[p]
                pe_h, pe_y, pe_hlast = {}, {}, {}
                pec = base["pe"]
                for eh in range(NEH):
                    for hc in range(NHC):
                        for tb in range(2):
                            pec += 1
                            pe_h[(eh, hc, tb)] = pec
                    pe_hlast[eh] = pec
                    for dg in range(4):
                        for fi in range(8):
                            pec += 1
                            pe_y[(eh, dg, fi)] = pec
                ac_h = {}
                acc = base["ac"]
                for eh in range(NEH):
                    for hc in range(NHC):
                        for tb in range(2):
                            acc += 1
                            ac_h[(eh, hc, tb)] = acc
                dv_prep, dv_yev = {}, {}
                dvc_ = base["dv"] + 1

                def count_prep(eh):
                    nonlocal dvc_
                    ei, hp = eh // 2, eh % 2
                    if hp == 0:
                        dvc_ += 1
                    dvc_ += NDC
                    dv_prep[eh] = dvc_

                def count_yev(eh):
                    nonlocal dvc_
                    for dg in range(4):
                        dvc_ += 8
                        dv_yev[(eh, dg)] = dvc_

                count_prep(0)
                count_prep(1)
                for eh in range(NEH):
                    if eh >= 1:
                        count_yev(eh - 1)
                    if eh + 2 < NEH:
                        count_prep(eh + 2)
                count_yev(NEH - 1)
                dv_final = dvc_

                for ei in range(7):
                    e = ei + 1
                    if ei == 0:
                        s.wait("sync", "sd", sd_grow)
                    if ei >= 1:
                        s.wait("sync", "dv", dv_prep[2 * ei - 1])
                    s.dmaw("sync", "gba")
                    got = s.op("sync", lambda ss, e=e: ss.dma_start(
                        out=gbc[:, :], in_=growd[e:e + 1, :].broadcast_to([128, TC])), "gba", 16)
                    assert got == sd_gbc[ei]

                for eh in range(NEH):
                    ei, hp = eh // 2, eh % 2
                    e = ei + 1
                    for fi in range(NW1):
                        g1 = eh * NW1 + fi
                        if g1 >= 3:
                            peh, pfi = (g1 - 3) // NW1, (g1 - 3) % NW1
                            s.wait("gpsimd", "pe", pe_h[(peh, pfi * 2 + 1, 1)])
                        s.dmaw("gpsimd", w1sem[(eh, fi)])
                        got = s.op("gpsimd", lambda g, ei=ei, fi=fi, g1=g1: g.dma_start(
                            out=w1b[g1 % 3][:, :],
                            in_=w1_in[ei, fi * 2:(fi + 1) * 2, :, :, :].rearrange("a p c h -> p a c h")),
                            w1sem[(eh, fi)], 16)
                        assert got == gd_w1[(eh, fi)]
                    for dg in range(4):
                        for fi in range(8):
                            g2 = (eh * 4 + dg) * 8 + fi
                            if g2 >= 2:
                                pg = g2 - 2
                                peh, prem = pg // 32, pg % 32
                                pdg, pfi = prem // 8, prem % 8
                                s.wait("gpsimd", "pe", pe_y[(peh, pdg, pfi)])
                            s.dmaw("gpsimd", w2sem[(eh, dg, fi)])
                            got = s.op("gpsimd", lambda g, ei=ei, dg=dg, fi=fi, g2=g2: g.dma_start(
                                out=w2b[g2 % 2][:, :],
                                in_=w2_in[ei, fi * 4:(fi + 1) * 4, :, dg * 2:(dg + 1) * 2, :].rearrange(
                                    "a p c h -> p a c h")), w2sem[(eh, dg, fi)], 16)
                            assert got == gd_w2[(eh, dg, fi)]

                for eh in range(NEH):
                    ei, hp = eh // 2, eh % 2
                    s.wait("tensor", "dv", dv_prep[eh])
                    if eh >= 1:
                        s.wait("tensor", "dv", dv_yev[(eh - 1, 2)])
                    for hc in range(NHC):
                        fi = hc // 2
                        g1 = eh * NW1 + fi
                        s.wait("tensor", w1sem[(eh, fi)], gd_w1[(eh, fi)])
                        if hc >= 2:
                            s.wait("tensor", "ac", ac_h[(eh, hc - 2, 1)])
                        for tb in range(2):
                            got = None
                            for dc in range(NDC):
                                inc = "pe" if dc == NDC - 1 else None
                                got = s.op("tensor", lambda tn, hc=hc, tb=tb, dc=dc, g1=g1, eh=eh:
                                           tn.matmul(ps[2 * (hc % 2) + tb][:, :],
                                                     w1b[g1 % 3][:, ((hc % 2) * NDC + dc) * 128:
                                                                 ((hc % 2) * NDC + dc + 1) * 128],
                                                     xe[eh % 2][:, dc * 1024 + tb * 512:
                                                                dc * 1024 + (tb + 1) * 512],
                                                     start=(dc == 0), stop=(dc == NDC - 1)), inc)
                            assert got == pe_h[(eh, hc, tb)]
                    s.wait("tensor", "ac", ac_h[(eh, NHC - 1, 1)])
                    for dg in range(4):
                        if dg >= 2:
                            s.wait("tensor", "dv", dv_yev[(eh, dg - 2)])
                        elif dg == 1 and eh >= 1:
                            s.wait("tensor", "dv", dv_yev[(eh - 1, 3)])
                        for fi in range(8):
                            g2 = (eh * 4 + dg) * 8 + fi
                            s.wait("tensor", w2sem[(eh, dg, fi)], gd_w2[(eh, dg, fi)])
                            got = None
                            for hcl in range(4):
                                hc = fi * 4 + hcl
                                for dcl in range(2):
                                    for tb in range(2):
                                        pidx = 4 * (dg % 2) + dcl * 2 + tb
                                        inc = "pe" if (hcl == 3 and dcl == 1 and tb == 1) else None
                                        got = s.op("tensor", lambda tn, hc=hc, hcl=hcl, dcl=dcl,
                                                   tb=tb, g2=g2, pidx=pidx: tn.matmul(
                                            ps[pidx][:, :],
                                            w2b[g2 % 2][:, (hcl * 2 + dcl) * 128:
                                                        (hcl * 2 + dcl + 1) * 128],
                                            hbuf[:, hc * 1024 + tb * 512: hc * 1024 + (tb + 1) * 512],
                                            start=(hc == 0), stop=(hc == NHC - 1)), inc)
                            assert got == pe_y[(eh, dg, fi)]
                assert s.cnt["pe"] == pec

                for eh in range(NEH):
                    ei = eh // 2
                    for hc in range(NHC):
                        for tb in range(2):
                            s.wait("scalar", "pe", pe_h[(eh, hc, tb)])
                            got = s.op("scalar", lambda sc, hc=hc, tb=tb, ei=ei: sc.activation(
                                hbuf[:, hc * 1024 + tb * 512: hc * 1024 + (tb + 1) * 512],
                                ps[2 * (hc % 2) + tb][:, :], AF.Relu,
                                bias=b1s[:, ei * NHC + hc: ei * NHC + hc + 1]), "ac")
                            assert got == ac_h[(eh, hc, tb)]

                got = s.op("vector", lambda v: v.memset(yac[:, :], 0.0), "dv")
                assert got == base["dv"] + 1

                def rec_prep(eh):
                    ei, hp = eh // 2, eh % 2
                    if eh >= 2:
                        s.wait("vector", "pe", pe_hlast[eh - 2])
                    s.wait("vector", "gba", sd_gbc[ei])
                    last = None
                    if hp == 0:
                        last = s.op("vector", lambda v, ei=ei: v.tensor_single_scalar(
                            m01[ei % 2][:, :], gbc[:, :], 0.0, ALU.is_gt), "dv")
                    for dc in range(NDC):
                        last = s.op("vector", lambda v, dc=dc, hp=hp, eh=eh: v.tensor_mul(
                            xe[eh % 2][:, dc * 1024:(dc + 1) * 1024],
                            xhfm[:, dc * TC + hp * 1024: dc * TC + (hp + 1) * 1024],
                            gbc[:, hp * 1024:(hp + 1) * 1024]), "dv")
                    assert last == dv_prep[eh]

                def rec_yev(eh):
                    ei, hp = eh // 2, eh % 2
                    for dg in range(4):
                        s.wait("vector", "pe", pe_y[(eh, dg, 7)])
                        last = None
                        for dcl in range(2):
                            dc = dg * 2 + dcl
                            for tb in range(2):
                                pidx = 4 * (dg % 2) + dcl * 2 + tb
                                ysl = slice(dc * TC + hp * 1024 + tb * 512,
                                            dc * TC + hp * 1024 + (tb + 1) * 512)
                                msl = slice(hp * 1024 + tb * 512, hp * 1024 + (tb + 1) * 512)
                                aa = s.op("vector", lambda v, pidx=pidx, ei=ei, dc=dc, msl=msl, tb=tb:
                                          v.scalar_tensor_tensor(
                                              ytmp[tb][:, :], ps[pidx][:, :],
                                              b2s[:, ei * NDC + dc: ei * NDC + dc + 1],
                                              m01[ei % 2][:, msl], ALU.add, ALU.mult), "dv")
                                s.wait("vector", "dv", aa)
                                last = s.op("vector", lambda v, ysl=ysl, tb=tb: v.tensor_add(
                                    yac[:, ysl], yac[:, ysl], ytmp[tb][:, :]), "dv")
                        assert last == dv_yev[(eh, dg)]

                rec_prep(0)
                rec_prep(1)
                for eh in range(NEH):
                    if eh >= 1:
                        rec_yev(eh - 1)
                    if eh + 2 < NEH:
                        rec_prep(eh + 2)
                rec_yev(NEH - 1)
                assert s.cnt["dv"] == dv_final

                s.emit(nc)

            with ExitStack() as eF:
                x2l5 = [eF.enter_context(nc.sbuf_tensor(f"x2l5_{i}", [128, D], F32)) for i in range(2)]
                ostg = [eF.enter_context(nc.sbuf_tensor(f"ostg{i}", [128, D], F32)) for i in range(2)]
                ptD = [eF.enter_context(nc.psum_tensor(f"ptD{i}", [128, 128], BF16)) for i in range(2)]

                dv_o5 = {}
                sd_5 = [None] * NT
                st_5 = [None] * NT
                tr5 = {}
                for t in range(2):
                    s.dmaw("sync", "ja" if t % 2 == 0 else "jb")
                    sd_5[t] = s.op("sync", lambda ss, t=t: ss.dma_start(
                        out=x2l5[t % 2][:, :], in_=x2d[t * 128:(t + 1) * 128, :]),
                        "ja" if t % 2 == 0 else "jb", 16)
                for t in range(NT):
                    for dc in range(NDC):
                        j = t * NDC + dc
                        if j >= 2:
                            s.wait("tensor", "dv", dv_o5[((j - 2) // NDC, (j - 2) % NDC)])
                        tr5[j] = s.op("tensor", lambda tn, t=t, dc=dc, j=j: tn.transpose(
                            ptD[j % 2][:, :], yac[:, dc * TC + t * 128: dc * TC + (t + 1) * 128],
                            ident[:, :]), "pe")
                        s.wait("vector", "pe", tr5[j])
                        s.wait("vector", "ja" if t % 2 == 0 else "jb", sd_5[t])
                        if t >= 2 and dc == 0:
                            s.wait("vector", "sta" if t % 2 == 0 else "stb", st_5[t - 2])
                        dv_o5[(t, dc)] = s.op("vector", lambda v, t=t, dc=dc, j=j: v.tensor_add(
                            ostg[t % 2][:, dc * 128:(dc + 1) * 128],
                            x2l5[t % 2][:, dc * 128:(dc + 1) * 128],
                            ptD[j % 2][:, :]), "dv")
                    s.wait("sync", "dv", dv_o5[(t, NDC - 1)])
                    s.dmaw("sync", "sta" if t % 2 == 0 else "stb")
                    st_5[t] = s.op("sync", lambda ss, t=t: ss.dma_start(
                        out=y_out[t * 128:(t + 1) * 128, :], in_=ostg[t % 2][:, :]),
                        "sta" if t % 2 == 0 else "stb", 16)
                    if t + 2 < NT:
                        s.dmaw("sync", "ja" if t % 2 == 0 else "jb")
                        sd_5[t + 2] = s.op("sync", lambda ss, t=t: ss.dma_start(
                            out=x2l5[t % 2][:, :], in_=x2d[(t + 2) * 128:(t + 3) * 128, :]),
                            "ja" if t % 2 == 0 else "jb", 16)
                s.wait("sync", "sta", s.cnt["sta"])
                s.wait("sync", "stb", s.cnt["stb"])
                s.emit(nc)

    return nc


_PROG_CACHE = {}


def _get_program():
    if "nc" not in _PROG_CACHE:
        _PROG_CACHE["nc"] = build_program()
    return _PROG_CACHE["nc"]


def kernel(L0_attn_x, L0_attn_Wq, L0_attn_Wk, L0_attn_Wv, L0_attn_Wo,
           L0_attn_q_gamma, L0_attn_q_beta, L0_attn_k_gamma, L0_attn_k_beta,
           L0_ffn_gate_w, L0_ffn_gate_b, L0_ffn_w1, L0_ffn_b1, L0_ffn_w2,
           L0_ffn_b2):
    f = np.float32
    bf = ml_dtypes.bfloat16
    x = np.ascontiguousarray(np.asarray(L0_attn_x, dtype=f).reshape(T, D))

    wqT = np.ascontiguousarray(np.asarray(L0_attn_Wq, dtype=f).T)
    wkT = np.ascontiguousarray(np.asarray(L0_attn_Wk, dtype=f).T)
    wvT = np.ascontiguousarray(np.asarray(L0_attn_Wv, dtype=f).T)
    woT = np.ascontiguousarray(np.asarray(L0_attn_Wo, dtype=f).T)

    qg_t = np.ascontiguousarray(np.broadcast_to(np.asarray(L0_attn_q_gamma, f), (128, DH)))
    qb_t = np.ascontiguousarray(np.broadcast_to(np.asarray(L0_attn_q_beta, f), (128, DH)))
    kg_t = np.ascontiguousarray(np.broadcast_to(np.asarray(L0_attn_k_gamma, f), (128, DH)))
    kb_t = np.ascontiguousarray(np.broadcast_to(np.asarray(L0_attn_k_beta, f), (128, DH)))

    gwt = np.ascontiguousarray(np.asarray(L0_ffn_gate_w, f).reshape(NDC, 128, NE))
    gbb = np.ascontiguousarray(np.broadcast_to(np.asarray(L0_ffn_gate_b, f), (128, NE)))

    w1 = np.asarray(L0_ffn_w1, f)
    w2 = np.asarray(L0_ffn_w2, f)
    b1 = np.asarray(L0_ffn_b1, f)
    b2 = np.asarray(L0_ffn_b2, f)
    w1t = np.ascontiguousarray(
        w1[1:].reshape(7, NDC, 128, NHC, 128).transpose(0, 3, 2, 1, 4))
    w2t = np.ascontiguousarray(
        w2[1:].reshape(7, NHC, 128, NDC, 128))
    b1t = np.ascontiguousarray(b1[1:].reshape(7, NHC, 128).transpose(0, 2, 1))
    b2t = np.ascontiguousarray(b2[1:].reshape(7, NDC, 128).transpose(0, 2, 1))

    identity = np.eye(128, dtype=bf)
    em = np.ones((128, 512), dtype=f)
    tri = (np.arange(128)[:, None] <= np.arange(128)[None, :]).astype(f)
    em[:, 0:128] = tri
    iota8 = np.ascontiguousarray(np.broadcast_to(np.arange(NE, dtype=f), (128, NE)))

    common = {
        "wqT": wqT, "wkT": wkT, "wvT": wvT, "woT": woT,
        "qg_t": qg_t, "qb_t": qb_t, "kg_t": kg_t, "kb_t": kb_t,
        "gwt": gwt, "gbb": gbb, "w1t": w1t, "w2t": w2t, "b1t": b1t, "b2t": b2t,
        "ident": identity, "emask": em, "iota8": iota8,
    }
    in_maps = []
    for c in range(NCORES):
        sel = np.zeros((128, NE), dtype=f)
        sel[:, :c] = 1.0
        m = dict(common)
        m["x"] = np.ascontiguousarray(x[c * TC:(c + 1) * TC, :])
        m["selbc"] = sel
        in_maps.append(m)

    nc = _get_program()
    res = run_bass_kernel_spmd(nc, in_maps, core_ids=list(range(NCORES)))
    out = np.concatenate([res.results[c]["y"] for c in range(NCORES)], axis=0)
    return out.reshape(1, T, D).astype(np.float32)


# version 12 (final)
# speedup vs baseline: 1.0132x; 1.0132x over previous
import numpy as np
import ml_dtypes
from contextlib import ExitStack

import concourse.bass as bass
import concourse.mybir as mybir
from concourse.bass_utils import run_bass_kernel_spmd

F32 = mybir.dt.float32
BF16 = mybir.dt.bfloat16
AF = mybir.ActivationFunctionType
ALU = mybir.AluOpType
AX = mybir.AxisListType

NCORES = 8
T = 16384
TC = T // NCORES
NT = TC // 128
D = 1024
NDC = D // 128
NH = 16
DH = 64
NSB = 4
SBT = TC // NSB
NE = 8
HD = 4096
NHC = HD // 128
NEH = 14
EPS = 1e-5

ENGINES = ("sync", "gpsimd", "tensor", "vector", "scalar")


class Sched:

    def __init__(self, sems):
        self.sems = sems
        self.cnt = {k: 0 for k in sems}
        self.items = {e: [] for e in ENGINES}

    def wait(self, eng, sem, val):
        if val is None:
            raise ValueError(f"wait on unrecorded value for sem {sem}")
        if val <= 0:
            return
        self.items[eng].append(lambda sec, s=self.sems[sem], v=val: sec.wait_ge(s, v))

    def dmaw(self, eng, sem):
        self.wait(eng, sem, self.cnt[sem])

    def op(self, eng, fn, inc=None, k=1):
        if inc is not None:
            self.cnt[inc] += k
            val = self.cnt[inc]
            self.items[eng].append(
                lambda sec, f=fn, s=self.sems[inc], kk=k: f(sec).then_inc(s, kk))
            return val
        self.items[eng].append(lambda sec, f=fn: f(sec))
        return None

    def emit(self, nc):
        with nc.Block() as blk:
            for eng in ENGINES:
                if not self.items[eng]:
                    continue
                items = self.items[eng]

                def body(sec, items=items):
                    for it in items:
                        it(sec)
                getattr(blk, eng)(body)
        self.items = {e: [] for e in ENGINES}


def build_program(debug=False):
    nc = bass.Bass("TRN2", target_bir_lowering=False)

    x_in = nc.dram_tensor("x", [TC, D], F32, kind="ExternalInput").ap()
    wq_in = nc.dram_tensor("wqT", [D, D], F32, kind="ExternalInput").ap()
    wk_in = nc.dram_tensor("wkT", [D, D], F32, kind="ExternalInput").ap()
    wv_in = nc.dram_tensor("wvT", [D, D], F32, kind="ExternalInput").ap()
    wo_in = nc.dram_tensor("woT", [D, D], F32, kind="ExternalInput").ap()
    qg_in = nc.dram_tensor("qg_t", [128, DH], F32, kind="ExternalInput").ap()
    qb_in = nc.dram_tensor("qb_t", [128, DH], F32, kind="ExternalInput").ap()
    kg_in = nc.dram_tensor("kg_t", [128, DH], F32, kind="ExternalInput").ap()
    kb_in = nc.dram_tensor("kb_t", [128, DH], F32, kind="ExternalInput").ap()
    gw_in = nc.dram_tensor("gwt", [NDC, 128, NE], F32, kind="ExternalInput").ap()
    gb_in = nc.dram_tensor("gbb", [128, NE], F32, kind="ExternalInput").ap()
    w1_in = nc.dram_tensor("w1t", [7, NHC, 128, NDC, 128], F32, kind="ExternalInput").ap()
    w2_in = nc.dram_tensor("w2t", [7, NHC, 128, NDC, 128], F32, kind="ExternalInput").ap()
    b1_in = nc.dram_tensor("b1t", [7, 128, NHC], F32, kind="ExternalInput").ap()
    b2_in = nc.dram_tensor("b2t", [7, 128, NDC], F32, kind="ExternalInput").ap()
    id_in = nc.dram_tensor("ident", [128, 128], BF16, kind="ExternalInput").ap()
    em_in = nc.dram_tensor("emask", [128, 512], F32, kind="ExternalInput").ap()
    io_in = nc.dram_tensor("iota8", [128, NE], F32, kind="ExternalInput").ap()
    se_in = nc.dram_tensor("selbc", [128, NE], F32, kind="ExternalInput").ap()
    y_out = nc.dram_tensor("y", [TC, D], F32, kind="ExternalOutput").ap()

    x2d = nc.dram_tensor("x2d", [TC, D], F32).ap()
    if debug:
        dbg_xln1 = nc.dram_tensor("dbg_xln1", [128, NDC * TC], F32, kind="ExternalOutput").ap()
        dbg_xln0 = nc.dram_tensor("dbg_xln0", [128, NDC * TC], F32, kind="ExternalOutput").ap()
        dbg_qtm = nc.dram_tensor("dbg_qtm", [128, NT * D], F32, kind="ExternalOutput").ap()
        dbg_ktm = nc.dram_tensor("dbg_ktm", [128, NT * D], F32, kind="ExternalOutput").ap()
        dbg_vtm = nc.dram_tensor("dbg_vtm", [128, NT * D], F32, kind="ExternalOutput").ap()
        dbg_ofm = nc.dram_tensor("dbg_ofm", [128, NDC * SBT], F32, kind="ExternalOutput").ap()
        dbg_srun = nc.dram_tensor("dbg_srun", [64, NH * DH], F32, kind="ExternalOutput").ap()
    growd = nc.dram_tensor("growd", [NE, TC], F32).ap()
    sownd = nc.dram_tensor("sownd", [64, NH * DH], F32)
    sgath = nc.dram_tensor("sgath", [NCORES * 64, NH * DH], F32)

    with ExitStack() as es:
        sems = {}
        for name in ["sd", "gd", "pe", "dv", "ac", "pe2", "dv2", "ac2", "cc", "st",
                     "xa", "xb", "ra", "rb", "sta", "stb", "ja", "jb", "jc", "gba",
                     "wa", "wb", "wc", "wd", "pl", "we", "wf"]:
            sems[name] = es.enter_context(nc.semaphore(name))
        s = Sched(sems)

        ident = es.enter_context(nc.sbuf_tensor("c_ident", [128, 128], BF16))
        emask = es.enter_context(nc.sbuf_tensor("c_emask", [128, 512], F32))
        iota8 = es.enter_context(nc.sbuf_tensor("c_iota8", [128, NE], F32))
        selbc = es.enter_context(nc.sbuf_tensor("c_selbc", [128, NE], F32))
        gb_bc = es.enter_context(nc.sbuf_tensor("c_gbb", [128, NE], F32))
        gwf = es.enter_context(nc.sbuf_tensor("c_gwf", [128, NDC * NE], F32))
        gwh = es.enter_context(nc.sbuf_tensor("c_gwh", [128, NDC * NE], BF16))
        gwl = es.enter_context(nc.sbuf_tensor("c_gwl", [128, NDC * NE], BF16))
        b1s = es.enter_context(nc.sbuf_tensor("c_b1s", [128, 7 * NHC], F32))
        b2s = es.enter_context(nc.sbuf_tensor("c_b2s", [128, 7 * NDC], F32))
        s_run = es.enter_context(nc.sbuf_tensor("s_run", [64, NH * DH], F32))
        s_bf = es.enter_context(nc.sbuf_tensor("s_bf", [64, NH * DH], BF16))

        with ExitStack() as eAB:
            qtm = eAB.enter_context(nc.sbuf_tensor("qtm", [128, NT * D], BF16))
            ktm = eAB.enter_context(nc.sbuf_tensor("ktm", [128, NT * D], BF16))
            vtm = eAB.enter_context(nc.sbuf_tensor("vtm", [128, NT * D], BF16))

            with ExitStack() as eA:
                xln1 = eA.enter_context(nc.sbuf_tensor("xln1", [128, NDC * TC], BF16))
                wk_s = eA.enter_context(nc.sbuf_tensor("wk_s", [128, NDC * D], BF16))
                wv_s = eA.enter_context(nc.sbuf_tensor("wv_s", [128, NDC * D], BF16))
                gamq = eA.enter_context(nc.sbuf_tensor("gamq", [128, DH], F32))
                betq = eA.enter_context(nc.sbuf_tensor("betq", [128, DH], F32))
                gamk = eA.enter_context(nc.sbuf_tensor("gamk", [128, DH], F32))
                betk = eA.enter_context(nc.sbuf_tensor("betk", [128, DH], F32))
                xld = [eA.enter_context(nc.sbuf_tensor(f"xld{i}", [128, D], F32)) for i in range(2)]
                sqb = eA.enter_context(nc.sbuf_tensor("sqb", [128, D], F32))
                xlt = [eA.enter_context(nc.sbuf_tensor(f"xlt{i}", [128, D], BF16)) for i in range(2)]
                lnst = eA.enter_context(nc.sbuf_tensor("lnst", [128, 8], F32))
                pf32 = [eA.enter_context(nc.sbuf_tensor(f"pf32_{i}", [128, D], F32)) for i in range(2)]
                fsc = eA.enter_context(nc.sbuf_tensor("fsc", [128, D], F32))
                sqA = eA.enter_context(nc.sbuf_tensor("sqA", [128, D], F32))
                sowns = eA.enter_context(nc.sbuf_tensor("sowns", [64, NH * DH], F32))
                hstat = eA.enter_context(nc.sbuf_tensor("hstat", [128, 4 * NH], F32))

                ptA = [eA.enter_context(nc.psum_tensor(f"ptA{i}", [128, 128], BF16)) for i in range(2)]
                ppA = [eA.enter_context(nc.psum_tensor(f"ppA{i}", [128, 512], F32)) for i in range(4)]
                psS = [eA.enter_context(nc.psum_tensor(f"psS{i}", [64, 512], F32)) for i in range(2)]

                gd_id = s.op("gpsimd", lambda g: g.dma_start(out=ident[:, :], in_=id_in), "gd", 16)
                for dst, src in ((emask, em_in), (iota8, io_in), (selbc, se_in), (gb_bc, gb_in)):
                    s.op("gpsimd", lambda g, d=dst, i=src: g.dma_start(out=d[:, :], in_=i), "gd", 16)
                gd_gw = s.op("gpsimd", lambda g: g.dma_start(
                    out=gwf[:, :], in_=gw_in.rearrange("c p e -> p c e")), "gd", 16)
                for dst, src in ((gamq, qg_in), (betq, qb_in), (gamk, kg_in), (betk, kb_in)):
                    s.op("gpsimd", lambda g, d=dst, i=src: g.dma_start(out=d[:, :], in_=i), "gd", 16)
                s.op("gpsimd", lambda g: g.dma_start(
                    out=b1s[:, :], in_=b1_in.rearrange("e p h -> p e h")), "gd", 16)
                s.op("gpsimd", lambda g: g.dma_start(
                    out=b2s[:, :], in_=b2_in.rearrange("e p d -> p e d")), "gd", 16)
                gd_consts = s.cnt["gd"]
                s.wait("gpsimd", "gd", gd_consts)
                gd_wk = None
                for dc in range(NDC):
                    gd_wk = s.op("gpsimd", lambda g, dc=dc: g.dma_start(
                        out=wk_s[:, dc * D:(dc + 1) * D],
                        in_=wk_in[dc * 128:(dc + 1) * 128, :]), "gd", 16)
                s.wait("gpsimd", "gd", gd_wk)
                gd_wv = None
                for dc in range(NDC):
                    gd_wv = s.op("gpsimd", lambda g, dc=dc: g.dma_start(
                        out=wv_s[:, dc * D:(dc + 1) * D],
                        in_=wv_in[dc * 128:(dc + 1) * 128, :]), "gd", 16)

                s.wait("gpsimd", "gd", gd_wv)
                s.wait("vector", "gd", gd_consts)
                a = s.op("vector", lambda v: v.tensor_copy(gwh[:, :], gwf[:, :]), "dv")
                s.wait("vector", "dv", a)
                s.op("vector", lambda v: v.tensor_sub(gwl[:, :], gwf[:, :], gwh[:, :]), "dv")

                sd_x = [None] * NT
                dv_ap1 = [None] * NT
                dv_f0 = [None] * NT
                pe_tr1 = [[None] * NDC for _ in range(NT)]
                cp1 = {}
                for t in range(2):
                    s.dmaw("sync", "xa" if t % 2 == 0 else "xb")
                    sd_x[t] = s.op("sync", lambda ss, t=t: ss.dma_start(
                        out=xld[t % 2][:, :], in_=x_in[t * 128:(t + 1) * 128, :]),
                        "xa" if t % 2 == 0 else "xb", 16)

                def emit_b0_tile(t):
                    xt = xld[t % 2]
                    s.wait("vector", "xa" if t % 2 == 0 else "xb", sd_x[t])
                    if t >= 1:
                        s.wait("vector", "dv", dv_ap1[t - 1])
                    if t >= 2:
                        s.wait("vector", "pe", pe_tr1[t - 2][NDC - 1])
                    s.wait("scalar", "xa" if t % 2 == 0 else "xb", sd_x[t])
                    if t >= 1:
                        s.wait("scalar", "dv", dv_f0[t - 1])
                    s.wait("scalar", "ac", s.cnt["ac"])
                    s.op("scalar", lambda sc, xt=xt: sc.activation(
                        sqb[:, :], xt[:, :], AF.Identity, accum_out=lnst[:, 0:1]), "ac")
                    s.wait("scalar", "ac", s.cnt["ac"])
                    sq0 = s.op("scalar", lambda sc, xt=xt: sc.activation(
                        sqb[:, :], xt[:, :], AF.Square, accum_out=lnst[:, 1:2]), "ac")
                    s.wait("vector", "ac", sq0)
                    d = s.op("vector", lambda v: v.tensor_scalar_mul(
                        lnst[:, 2:3], lnst[:, 0:1], 1.0 / D), "dv")
                    s.wait("vector", "dv", d)
                    e = s.op("vector", lambda v: v.tensor_mul(
                        lnst[:, 4:5], lnst[:, 2:3], lnst[:, 2:3]), "dv")
                    s.wait("vector", "dv", e)
                    f = s.op("vector", lambda v: v.tensor_scalar(
                        lnst[:, 5:6], lnst[:, 1:2], 1.0 / D,
                        lnst[:, 4:5], ALU.mult, ALU.subtract), "dv")
                    dv_f0[t] = f
                    s.wait("scalar", "dv", f)
                    g_ = s.op("scalar", lambda sc: sc.activation(
                        lnst[:, 6:7], lnst[:, 5:6], AF.Sqrt, bias=np.float32(EPS)), "ac")
                    s.wait("vector", "ac", g_)
                    h_ = s.op("vector", lambda v: v.reciprocal(lnst[:, 7:8], lnst[:, 6:7]), "dv")
                    s.wait("vector", "dv", h_)
                    dv_ap1[t] = s.op("vector", lambda v, xt=xt, t=t: v.tensor_scalar(
                        xlt[t % 2][:, :], xt[:, :], lnst[:, 2:3], lnst[:, 7:8],
                        ALU.subtract, ALU.mult), "dv")
                    if t + 2 < NT:
                        s.wait("sync", "dv", dv_ap1[t])
                        s.dmaw("sync", "xa" if t % 2 == 0 else "xb")
                        sd_x[t + 2] = s.op("sync", lambda ss, t=t: ss.dma_start(
                            out=xld[t % 2][:, :],
                            in_=x_in[(t + 2) * 128:(t + 3) * 128, :]),
                            "xa" if t % 2 == 0 else "xb", 16)

                    if t == 0:
                        s.wait("tensor", "gd", gd_consts)
                    s.wait("tensor", "dv", dv_ap1[t])
                    for dc in range(NDC):
                        j = t * NDC + dc
                        if j >= 2:
                            s.wait("tensor", cp1[j - 2][0], cp1[j - 2][1])
                        pe_tr1[t][dc] = s.op("tensor", lambda tn, t=t, dc=dc, j=j: tn.transpose(
                            ptA[j % 2][:, :], xlt[t % 2][:, dc * 128:(dc + 1) * 128],
                            ident[:, :]), "pe")
                        dsl = slice(dc * TC + t * 128, dc * TC + (t + 1) * 128)
                        if dc % 2 == 0:
                            s.wait("vector", "pe", pe_tr1[t][dc])
                            val = s.op("vector", lambda v, j=j, dsl=dsl: v.tensor_copy(
                                xln1[:, dsl], ptA[j % 2][:, :]), "dv2")
                            cp1[j] = ("dv2", val)
                        else:
                            s.wait("scalar", "pe", pe_tr1[t][dc])
                            val = s.op("scalar", lambda sc, j=j, dsl=dsl: sc.copy(
                                xln1[:, dsl], ptA[j % 2][:, :]), "ac2")
                            cp1[j] = ("ac2", val)

                def hln_stats(src, sq_ready_dv):
                    s.wait("scalar", "dv", sq_ready_dv)
                    s.wait("scalar", "ac", s.cnt["ac"])
                    sq_ = s.op("scalar", lambda sc, src=src: sc.activation(
                        sqA[:, :], src[:, :], AF.Square), "ac")
                    a = s.op("vector", lambda v, src=src: v.reduce_sum(
                        hstat[:, 0:16], src[:, :].rearrange("p (h d) -> p h d", d=DH),
                        axis=AX.X), "dv")
                    s.wait("vector", "ac", sq_)
                    c = s.op("vector", lambda v: v.reduce_sum(
                        hstat[:, 48:64], sqA[:, :].rearrange("p (h d) -> p h d", d=DH),
                        axis=AX.X), "dv")
                    s.wait("vector", "dv", a)
                    d = s.op("vector", lambda v: v.tensor_scalar_mul(
                        hstat[:, 32:48], hstat[:, 0:16], 1.0 / DH), "dv")
                    s.wait("vector", "dv", d)
                    e = s.op("vector", lambda v: v.tensor_mul(
                        hstat[:, 0:16], hstat[:, 32:48], hstat[:, 32:48]), "dv")
                    s.wait("vector", "dv", c)
                    f0 = s.op("vector", lambda v: v.tensor_scalar_mul(
                        hstat[:, 48:64], hstat[:, 48:64], 1.0 / DH), "dv")
                    s.wait("vector", "dv", max(f0, e))
                    f = s.op("vector", lambda v: v.tensor_sub(
                        hstat[:, 16:32], hstat[:, 48:64], hstat[:, 0:16]), "dv")
                    s.wait("scalar", "dv", f)
                    g_ = s.op("scalar", lambda sc: sc.activation(
                        hstat[:, 48:64], hstat[:, 16:32], AF.Sqrt, bias=np.float32(EPS)), "ac")
                    s.wait("vector", "ac", g_)
                    h_ = s.op("vector", lambda v: v.reciprocal(
                        hstat[:, 16:32], hstat[:, 48:64]), "dv")
                    s.wait("vector", "dv", h_)

                def hln_apply(src, dstb, dsto, gam, bet):
                    def v3(ap):
                        return ap.rearrange("p (h d) -> p h d", d=DH)

                    def stat_bc(col0):
                        return hstat[:, col0:col0 + NH].rearrange(
                            "p (h o) -> p h o", o=1).broadcast_to([128, NH, DH])

                    def row_bc(t_):
                        return t_[:, :].rearrange(
                            "p (o d) -> p o d", o=1).broadcast_to([128, NH, DH])

                    a = s.op("vector", lambda v, src=src: v.tensor_tensor(
                        v3(fsc[:, :]), v3(src[:, :]), stat_bc(32), ALU.subtract), "dv")
                    s.wait("vector", "dv", a)
                    b = s.op("vector", lambda v, src=src: v.tensor_tensor(
                        v3(src[:, :]), v3(fsc[:, :]), stat_bc(16), ALU.mult), "dv")
                    s.wait("vector", "dv", b)
                    cc_ = s.op("vector", lambda v, src=src, gam=gam: v.tensor_tensor(
                        v3(fsc[:, :]), v3(src[:, :]), row_bc(gam), ALU.mult), "dv")
                    s.wait("vector", "dv", cc_)
                    last = s.op("vector", lambda v, dstb=dstb, dsto=dsto, bet=bet: v.tensor_tensor(
                        v3(dstb[:, dsto:dsto + D]), v3(fsc[:, :]), row_bc(bet), ALU.add), "dv")
                    return last

                def proj_mms(t, wbuf):
                    last = None
                    for dc in range(NDC):
                        lsl = slice(dc * TC + t * 128, dc * TC + (t + 1) * 128)
                        for half in range(2):
                            last = s.op("tensor", lambda tn, dc=dc, half=half, lsl=lsl, w=wbuf:
                                        tn.matmul(ppA[half][:, :], xln1[:, lsl],
                                                  w[:, dc * D + half * 512: dc * D + (half + 1) * 512],
                                                  start=(dc == 0), stop=(dc == NDC - 1)), "pe")
                    return last

                pe_k = [None] * NT
                pe_v = [None] * NT
                ac_k = [None] * NT
                ac_v = [None] * NT
                dv_k = [None] * NT
                dv_b = [None] * NT
                dv_sq = [None] * NT
                dv_b2 = [None] * NT
                dv_sq2 = [None] * NT
                pe_st = [None] * NT

                def emit_b1_ktile(t):
                    if t == 0:
                        s.wait("tensor", "gd", gd_wk)
                    s.wait("tensor", cp1[t * NDC + NDC - 2][0], cp1[t * NDC + NDC - 2][1])
                    s.wait("tensor", cp1[t * NDC + NDC - 1][0], cp1[t * NDC + NDC - 1][1])
                    if t >= 1:
                        s.wait("tensor", "ac", ac_k[t - 1])
                    pe_k[t] = proj_mms(t, wk_s)
                    if t == 0:
                        s.wait("tensor", "gd", gd_wv)
                    if t >= 1:
                        s.wait("tensor", "ac", ac_v[t - 1])
                    last = None
                    for dc in range(NDC):
                        lsl = slice(dc * TC + t * 128, dc * TC + (t + 1) * 128)
                        for half in range(2):
                            last = s.op("tensor", lambda tn, dc=dc, half=half, lsl=lsl:
                                        tn.matmul(ppA[2 + half][:, :], xln1[:, lsl],
                                                  wv_s[:, dc * D + half * 512: dc * D + (half + 1) * 512],
                                                  start=(dc == 0), stop=(dc == NDC - 1)), "pe")
                    pe_v[t] = last

                    s.wait("scalar", "pe", pe_k[t])
                    if t >= 2:
                        s.wait("scalar", "dv", dv_k[t - 2])
                    s.op("scalar", lambda sc, t=t: sc.copy(pf32[t % 2][:, 0:512], ppA[0][:, :]), "ac")
                    ac_k[t] = s.op("scalar", lambda sc, t=t: sc.copy(
                        pf32[t % 2][:, 512:1024], ppA[1][:, :]), "ac")
                    s.wait("scalar", "pe", pe_v[t])
                    s.op("scalar", lambda sc, t=t: sc.copy(
                        vtm[:, t * D: t * D + 512], ppA[2][:, :]), "ac")
                    ac_v[t] = s.op("scalar", lambda sc, t=t: sc.copy(
                        vtm[:, t * D + 512:(t + 1) * D], ppA[3][:, :]), "ac")

                    s.wait("vector", "ac", ac_k[t])
                    if t >= 1:
                        s.wait("vector", "dv", dv_k[t - 1])
                    hln_stats(pf32[t % 2], dv_sq[t - 1] if t >= 1 else 0)
                    dv_sq[t] = s.cnt["dv"]
                    dv_k[t] = hln_apply(pf32[t % 2], ktm, t * D, gamk, betk)
                    dv_b[t] = s.cnt["dv"]

                    s.wait("tensor", "dv", dv_k[t])
                    s.wait("tensor", "ac", ac_v[t])
                    lastst = None
                    for h in range(NH):
                        lastst = s.op("tensor", lambda tn, t=t, h=h: tn.matmul(
                            psS[h // 8][:, (h % 8) * DH:(h % 8 + 1) * DH],
                            ktm[:, t * D + h * DH: t * D + (h + 1) * DH],
                            vtm[:, t * D + h * DH: t * D + (h + 1) * DH],
                            start=(t == 0 and h % 8 == 0),
                            stop=(t == NT - 1 and h % 8 == 7)), "pe")
                    pe_st[t] = lastst

                for t in range(NT):
                    emit_b0_tile(t)
                    if t >= 1:
                        emit_b1_ktile(t - 1)
                emit_b1_ktile(NT - 1)

                s.wait("vector", "pe", pe_st[NT - 1])
                s.op("vector", lambda v: v.tensor_copy(sowns[:, 0:512], psS[0][:, :]), "dv")
                dv_sown = s.op("vector", lambda v: v.tensor_copy(sowns[:, 512:1024], psS[1][:, :]), "dv")
                s.wait("sync", "dv", dv_sown)
                s.dmaw("sync", "sd")
                sd_sown = s.op("sync", lambda ss: ss.dma_start(out=sownd.ap(), in_=sowns[:, :]), "sd", 16)
                s.wait("gpsimd", "sd", sd_sown)
                s.op("gpsimd", lambda g: g.collective_compute(
                    "AllGather", ALU.bypass, replica_groups=[list(range(NCORES))],
                    ins=[sownd.ap().opt()], outs=[sgath.ap().opt()]), "cc", 1)

                s.wait("gpsimd", "pe", pe_k[NT - 1])
                gd_wq = None
                for dc in range(NDC):
                    gd_wq = s.op("gpsimd", lambda g, dc=dc: g.dma_start(
                        out=wk_s[:, dc * D:(dc + 1) * D],
                        in_=wq_in[dc * 128:(dc + 1) * 128, :]), "gd", 16)
                s.wait("gpsimd", "gd", gd_wq)

                pe_q = [None] * NT
                ac_q = [None] * NT
                dv_q = [None] * NT
                for t in range(NT):
                    if t == 0:
                        s.wait("tensor", "gd", gd_wq)
                        s.wait("tensor", "ac", ac_k[NT - 1])
                    if t >= 1:
                        s.wait("tensor", "ac", ac_q[t - 1])
                    pe_q[t] = proj_mms(t, wk_s)
                    s.wait("scalar", "pe", pe_q[t])
                    if t >= 2:
                        s.wait("scalar", "dv", dv_q[t - 2])
                    else:
                        s.wait("scalar", "dv", dv_k[NT - 1])
                    s.op("scalar", lambda sc, t=t: sc.copy(pf32[t % 2][:, 0:512], ppA[0][:, :]), "ac")
                    ac_q[t] = s.op("scalar", lambda sc, t=t: sc.copy(
                        pf32[t % 2][:, 512:1024], ppA[1][:, :]), "ac")
                    s.wait("vector", "ac", ac_q[t])
                    s.wait("vector", "dv", dv_q[t - 1] if t >= 1 else dv_k[NT - 1])
                    hln_stats(pf32[t % 2], dv_sq2[t - 1] if t >= 1 else dv_sq[NT - 1])
                    dv_sq2[t] = s.cnt["dv"]
                    dv_q[t] = hln_apply(pf32[t % 2], qtm, t * D, gamq, betq)
                    dv_b2[t] = s.cnt["dv"]

                s.emit(nc)
                if debug:
                    sdd = s.op("gpsimd", lambda g: g.dma_start(out=dbg_xln1, in_=xln1[:, :]), "gd", 16)
                    sdd = s.op("gpsimd", lambda g: g.dma_start(out=dbg_qtm, in_=qtm[:, :]), "gd", 16)
                    sdd = s.op("gpsimd", lambda g: g.dma_start(out=dbg_ktm, in_=ktm[:, :]), "gd", 16)
                    sdd = s.op("gpsimd", lambda g: g.dma_start(out=dbg_vtm, in_=vtm[:, :]), "gd", 16)
                    s.wait("gpsimd", "gd", sdd)
                    s.emit(nc)

            with ExitStack() as eB:
                wo_s = eB.enter_context(nc.sbuf_tensor("wo_s", [128, NDC * D], BF16))
                qfm = eB.enter_context(nc.sbuf_tensor("qfm", [64, NH * SBT], BF16))
                kfm = eB.enter_context(nc.sbuf_tensor("kfm", [64, NH * SBT], BF16))
                ofm = eB.enter_context(nc.sbuf_tensor("ofm", [128, NDC * SBT], BF16))
                stb = [eB.enter_context(nc.sbuf_tensor(f"stb{i}", [128, SBT], BF16)) for i in range(4)]
                sgt = eB.enter_context(nc.sbuf_tensor("sgt", [64, NH * DH], F32))
                tmpS = eB.enter_context(nc.sbuf_tensor("tmpS", [64, NH * DH], F32))
                xrl = [eB.enter_context(nc.sbuf_tensor(f"xrl{i}", [128, D], F32)) for i in range(2)]
                x2st = [eB.enter_context(nc.sbuf_tensor(f"x2st{i}", [128, D], F32)) for i in range(2)]

                ptB = [eB.enter_context(nc.psum_tensor(f"ptB{i}", [128, 128], BF16)) for i in range(2)]
                psc = [eB.enter_context(nc.psum_tensor(f"psc{i}", [128, 512], F32)) for i in range(2)]
                po = [eB.enter_context(nc.psum_tensor(f"po{i}", [64, 512], F32)) for i in range(2)]
                psS2 = [eB.enter_context(nc.psum_tensor(f"psS2_{i}", [64, 512], F32)) for i in range(2)]

                gd_wo = None
                for dc in range(NDC):
                    gd_wo = s.op("gpsimd", lambda g, dc=dc: g.dma_start(
                        out=wo_s[:, dc * D:(dc + 1) * D],
                        in_=wo_in[dc * 128:(dc + 1) * 128, :]), "gd", 16)
                s.wait("gpsimd", "gd", gd_wo)
                s.wait("gpsimd", "cc", 1)
                dv_comb = [None] * NCORES
                gd_sg = [None] * NCORES
                gd_sg[0] = s.op("gpsimd", lambda g: g.dma_start(
                    out=sgt[:, :], in_=sgath.ap()[0:64, :]), "gd", 16)
                for r in range(NCORES):
                    s.wait("vector", "gd", gd_sg[r])
                    if r == 0:
                        dv_comb[0] = s.op("vector", lambda v: v.tensor_scalar_mul(
                            s_run[:, :], sgt[:, :], selbc[0:64, 0:1]), "dv")
                    else:
                        s.wait("vector", "dv", dv_comb[r - 1])
                        a = s.op("vector", lambda v, r=r: v.tensor_scalar_mul(
                            tmpS[:, :], sgt[:, :], selbc[0:64, r:r + 1]), "dv")
                        s.wait("vector", "dv", a)
                        dv_comb[r] = s.op("vector", lambda v: v.tensor_add(
                            s_run[:, :], s_run[:, :], tmpS[:, :]), "dv")
                    if r + 1 < NCORES:
                        s.wait("gpsimd", "dv", dv_comb[r] if r == 0 else a)
                        s.dmaw("gpsimd", "gd")
                        gd_sg[r + 1] = s.op("gpsimd", lambda g, r=r: g.dma_start(
                            out=sgt[:, :], in_=sgath.ap()[(r + 1) * 64:(r + 2) * 64, :]), "gd", 16)
                s.wait("scalar", "dv", dv_comb[NCORES - 1])
                sbf_ready = s.op("scalar", lambda sc: sc.copy(s_bf[:, :], s_run[:, :]), "ac")

                def emit_transposes(sb):
                    cpd = {}
                    for qi, (src, dstb) in enumerate(((qtm, qfm), (ktm, kfm))):
                        for tt in range(NSB):
                            t = sb * NSB + tt
                            for dc in range(NDC):
                                j = qi * NSB * NDC + tt * NDC + dc
                                if j >= 2:
                                    s.wait("tensor", "dv2", cpd[j - 2][0])
                                    s.wait("tensor", "ac2", cpd[j - 2][1])
                                tv = s.op("tensor", lambda tn, t=t, dc=dc, j=j, src=src:
                                          tn.transpose(ptB[j % 2][:, :],
                                                       src[:, t * D + dc * 128: t * D + (dc + 1) * 128],
                                                       ident[:, :]), "pe")
                                hlo, hhi = 2 * dc, 2 * dc + 1
                                s.wait("vector", "pe", tv)
                                v1 = s.op("vector", lambda v, j=j, b=dstb, hlo=hlo, tt=tt:
                                          v.tensor_copy(
                                              b[:, hlo * SBT + tt * 128: hlo * SBT + (tt + 1) * 128],
                                              ptB[j % 2][0:64, :]), "dv2")
                                s.wait("scalar", "pe", tv)
                                v2 = s.op("scalar", lambda sc, j=j, b=dstb, hhi=hhi, tt=tt:
                                          sc.copy(
                                              b[:, hhi * SBT + tt * 128: hhi * SBT + (tt + 1) * 128],
                                              ptB[j % 2][64:128, :]), "ac2")
                                cpd[j] = (v1, v2)
                    return {"dv2": s.cnt["dv2"], "ac2": s.cnt["ac2"]}

                tr_done = emit_transposes(0)
                s.emit(nc)

                dv_x2 = {}
                sd_xr = {}
                sd_x2w = {}
                for sb in range(NSB):
                    s.wait("tensor", "dv2", tr_done["dv2"])
                    s.wait("tensor", "ac2", tr_done["ac2"])
                    s.wait("tensor", "ac", sbf_ready)
                    if (sb * NSB - 1, 1) in dv_x2:
                        s.wait("tensor", "dv", dv_x2[(sb * NSB - 1, 1)])

                    me = {}
                    oev = {}
                    av_done = [None] * NH
                    sinc_last = [None] * NH
                    sc_done = {}

                    def emit_qs(h):
                        if h >= 2:
                            s.wait("tensor", oev[h - 2][0], oev[h - 2][1])
                        s.op("tensor", lambda tn, h=h: tn.matmul(
                            po[h % 2][:, :], s_bf[:, h * DH:(h + 1) * DH],
                            qfm[:, h * SBT:(h + 1) * SBT],
                            start=True, stop=False), "pe")

                    def emit_st(h, kc):
                        m = h * NSB + kc
                        n = SBT - kc * 128
                        if m >= 2:
                            s.wait("tensor", "dv", me[m - 2])
                        sc_done[m] = s.op("tensor", lambda tn, h=h, kc=kc, m=m, n=n:
                                          tn.matmul(psc[m % 2][:, 0:n],
                                                    kfm[:, h * SBT + kc * 128: h * SBT + (kc + 1) * 128],
                                                    qfm[:, h * SBT + kc * 128:(h + 1) * SBT],
                                                    start=True, stop=True), "pe")
                        s.wait("vector", "pe", sc_done[m])
                        me[m] = s.op("vector", lambda v, m=m, n=n: v.tensor_mul(
                            stb[m % 4][:, 0:n], psc[m % 2][:, 0:n], emask[:, 0:n]), "dv")

                    def emit_av(h, kc):
                        m = h * NSB + kc
                        n = SBT - kc * 128
                        s.wait("tensor", "dv", me[m])
                        av_done[h] = s.op("tensor", lambda tn, h=h, kc=kc, m=m, sb=sb, n=n:
                                          tn.matmul(po[h % 2][:, kc * 128:SBT],
                                                    vtm[:, (sb * NSB + kc) * D + h * DH:
                                                        (sb * NSB + kc) * D + (h + 1) * DH],
                                                    stb[m % 4][:, 0:n],
                                                    start=False, stop=(kc == NSB - 1)), "pe")

                    def emit_sinc(h):
                        for kc in range(NSB):
                            t = sb * NSB + kc
                            sinc_last[h] = s.op("tensor", lambda tn, t=t, h=h, kc=kc: tn.matmul(
                                psS2[h // 8][:, (h % 8) * DH:(h % 8 + 1) * DH],
                                ktm[:, t * D + h * DH: t * D + (h + 1) * DH],
                                vtm[:, t * D + h * DH: t * D + (h + 1) * DH],
                                start=(kc == 0 and h % 8 == 0),
                                stop=(kc == NSB - 1 and h % 8 == 7)), "pe")

                    def emit_oev(h):
                        c = h // 2
                        rlo = (h % 2) * 64
                        if h % 2 == 0:
                            s.wait("vector", "pe", av_done[h])
                            val = s.op("vector", lambda v, h=h, c=c, rlo=rlo: v.tensor_copy(
                                ofm[rlo:rlo + 64, c * SBT:(c + 1) * SBT], po[h % 2][:, :]), "dv2")
                            oev[h] = ("dv2", val)
                        else:
                            s.wait("scalar", "pe", av_done[h])
                            val = s.op("scalar", lambda sc, h=h, c=c, rlo=rlo: sc.copy(
                                ofm[rlo:rlo + 64, c * SBT:(c + 1) * SBT], po[h % 2][:, :]), "ac2")
                            oev[h] = ("ac2", val)

                    for h in range(NH):
                        emit_qs(h)
                        emit_st(h, 0)
                        emit_st(h, 1)
                        if h >= 1:
                            emit_av(h - 1, 2)
                            emit_av(h - 1, 3)
                            emit_sinc(h - 1)
                            emit_oev(h - 1)
                        emit_st(h, 2)
                        emit_st(h, 3)
                        emit_av(h, 0)
                        emit_av(h, 1)
                    emit_av(NH - 1, 2)
                    emit_av(NH - 1, 3)
                    emit_sinc(NH - 1)
                    emit_oev(NH - 1)
                    last_sinc = sinc_last[NH - 1]

                    s.wait("vector", "pe", last_sinc)
                    a = s.op("vector", lambda v: v.tensor_add(
                        s_run[:, 0:512], s_run[:, 0:512], psS2[0][:, :]), "dv")
                    s.wait("vector", "dv", a)
                    dv_supd = s.op("vector", lambda v: v.tensor_add(
                        s_run[:, 512:1024], s_run[:, 512:1024], psS2[1][:, :]), "dv")
                    if sb < NSB - 1:
                        s.wait("scalar", "dv", dv_supd)
                        sbf_ready = s.op("scalar", lambda sc: sc.copy(s_bf[:, :], s_run[:, :]), "ac")
                        tr_done = emit_transposes(sb + 1)

                    oev_dv2 = s.cnt["dv2"]
                    oev_ac2 = s.cnt["ac2"]
                    s.wait("tensor", "dv2", oev_dv2)
                    s.wait("tensor", "ac2", oev_ac2)
                    if sb == 0:
                        s.wait("tensor", "gd", gd_wo)
                    for tt in range(NSB):
                        tglob = sb * NSB + tt
                        if (tglob - 2, 1) in dv_x2:
                            s.wait("sync", "dv", dv_x2[(tglob - 2, 1)])
                        s.dmaw("sync", "ra" if tglob % 2 == 0 else "rb")
                        sd_xr[tglob] = s.op("sync", lambda ss, tglob=tglob: ss.dma_start(
                            out=xrl[tglob % 2][:, :], in_=x_in[tglob * 128:(tglob + 1) * 128, :]),
                            "ra" if tglob % 2 == 0 else "rb", 16)
                        for half in range(2):
                            if tt == 0:
                                s.wait("tensor", "dv", me[NH * NSB - 1])
                            else:
                                s.wait("tensor", "dv", dv_x2[(tglob - 1, half)])
                            last = None
                            for c in range(NDC):
                                last = s.op("tensor", lambda tn, tt=tt, half=half, c=c:
                                            tn.matmul(psc[half][:, :],
                                                      ofm[:, c * SBT + tt * 128: c * SBT + (tt + 1) * 128],
                                                      wo_s[:, c * D + half * 512: c * D + (half + 1) * 512],
                                                      start=(c == 0), stop=(c == NDC - 1)), "pe")
                            s.wait("vector", "pe", last)
                            s.wait("vector", "ra" if tglob % 2 == 0 else "rb", sd_xr[tglob])
                            if tglob >= 2:
                                s.wait("vector", "sta" if tglob % 2 == 0 else "stb",
                                       sd_x2w[tglob - 2])
                            dv_x2[(tglob, half)] = s.op("vector", lambda v, tglob=tglob, half=half:
                                                        v.tensor_add(
                                                            x2st[tglob % 2][:, half * 512:(half + 1) * 512],
                                                            xrl[tglob % 2][:, half * 512:(half + 1) * 512],
                                                            psc[half][:, :]), "dv")
                        s.wait("sync", "dv", dv_x2[(tglob, 0)])
                        s.wait("sync", "dv", dv_x2[(tglob, 1)])
                        s.dmaw("sync", "sta" if tglob % 2 == 0 else "stb")
                        sd_x2w[tglob] = s.op("sync", lambda ss, tglob=tglob: ss.dma_start(
                            out=x2d[tglob * 128:(tglob + 1) * 128, :],
                            in_=x2st[tglob % 2][:, :]),
                            "sta" if tglob % 2 == 0 else "stb", 16)
                    s.emit(nc)
                    if debug and sb == 0:
                        sdd = s.op("gpsimd", lambda g: g.dma_start(out=dbg_ofm, in_=ofm[:, :]), "gd", 16)
                        sdd = s.op("gpsimd", lambda g: g.dma_start(out=dbg_srun, in_=s_run[:, :]), "gd", 16)
                        s.wait("gpsimd", "gd", sdd)
                        s.emit(nc)

        with ExitStack() as eC:
            xhfm = eC.enter_context(nc.sbuf_tensor("xhfm", [128, NDC * TC], BF16))
            yac = eC.enter_context(nc.sbuf_tensor("yac", [128, NDC * TC], BF16))
            G_all = eC.enter_context(nc.sbuf_tensor("G_all", [128, 128], BF16))
            GTf = eC.enter_context(nc.sbuf_tensor("GTf", [128, 128], F32))

            with ExitStack() as eD:
                x2l = [eD.enter_context(nc.sbuf_tensor(f"x2l{i}", [128, D], F32)) for i in range(3)]
                xl2tm = [eD.enter_context(nc.sbuf_tensor(f"xl2tm{i}", [128, D], F32)) for i in range(2)]
                xh_tm = [eD.enter_context(nc.sbuf_tensor(f"xh_tm{i}", [128, D], BF16)) for i in range(2)]
                xl_tm = [eD.enter_context(nc.sbuf_tensor(f"xl_tm{i}", [128, D], BF16)) for i in range(2)]
                xlfm = [eD.enter_context(nc.sbuf_tensor(f"xlfm{i}", [128, NDC * 128], BF16)) for i in range(2)]
                sqb3 = eD.enter_context(nc.sbuf_tensor("sqb3", [128, D], F32))
                lnst3 = eD.enter_context(nc.sbuf_tensor("lnst3", [128, 8], F32))
                Lt4 = eD.enter_context(nc.sbuf_tensor("Lt4", [128, 32], F32))
                tA = eD.enter_context(nc.sbuf_tensor("tA", [128, 32], F32))
                tB = eD.enter_context(nc.sbuf_tensor("tB", [128, 32], F32))
                q4 = [eD.enter_context(nc.sbuf_tensor(f"q4_{i}", [128, 4], F32)) for i in range(8)]
                ptC = [eD.enter_context(nc.psum_tensor(f"ptC{i}", [128, 128], BF16)) for i in range(2)]
                pgl = [eD.enter_context(nc.psum_tensor(f"pgl{i}", [128, 2 * NE], F32)) for i in range(2)]

                dv_g = [None] * NT
                dv_tile = [None] * NT
                dv_f3 = [None] * NT
                dv_ap3 = [None] * NT
                lt4_done = [None] * (NT // 4)
                sd_x2r = [None] * NT
                pe_gate = [None] * NT
                ac_hi = [None] * NT
                cpC = {}
                trj = 0

                s.op("vector", lambda v: v.memset(G_all[:, 0:16], 0.0), "dv")

                s.wait("sync", "sta", s.cnt["sta"])
                s.wait("sync", "stb", s.cnt["stb"])
                for t in range(3):
                    s.dmaw("sync", ["ja", "jb", "jc"][t % 3])
                    sd_x2r[t] = s.op("sync", lambda ss, t=t: ss.dma_start(
                        out=x2l[t % 3][:, :], in_=x2d[t * 128:(t + 1) * 128, :]),
                        ["ja", "jb", "jc"][t % 3], 16)

                for t in range(NT):
                    xt = x2l[t % 3]
                    s.wait("vector", ["ja", "jb", "jc"][t % 3], sd_x2r[t])
                    if t >= 1:
                        s.wait("vector", "dv", dv_tile[t - 1])
                    if t >= 2:
                        s.wait("vector", "ac", ac_hi[t - 2])
                    s.wait("scalar", ["ja", "jb", "jc"][t % 3], sd_x2r[t])
                    if t >= 1:
                        s.wait("scalar", "dv", dv_f3[t - 1])
                    s.wait("scalar", "ac", s.cnt["ac"])
                    s.op("scalar", lambda sc, xt=xt: sc.activation(
                        sqb3[:, :], xt[:, :], AF.Identity, accum_out=lnst3[:, 0:1]), "ac")
                    s.wait("scalar", "ac", s.cnt["ac"])
                    sq3 = s.op("scalar", lambda sc, xt=xt: sc.activation(
                        sqb3[:, :], xt[:, :], AF.Square, accum_out=lnst3[:, 1:2]), "ac")
                    s.wait("vector", "ac", sq3)
                    d = s.op("vector", lambda v: v.tensor_scalar_mul(lnst3[:, 2:3], lnst3[:, 0:1], 1.0 / D), "dv")
                    s.wait("vector", "dv", d)
                    e = s.op("vector", lambda v: v.tensor_mul(lnst3[:, 4:5], lnst3[:, 2:3], lnst3[:, 2:3]), "dv")
                    s.wait("vector", "dv", e)
                    f = s.op("vector", lambda v: v.tensor_scalar(
                        lnst3[:, 5:6], lnst3[:, 1:2], 1.0 / D, lnst3[:, 4:5],
                        ALU.mult, ALU.subtract), "dv")
                    dv_f3[t] = f
                    s.wait("scalar", "dv", f)
                    g_ = s.op("scalar", lambda sc: sc.activation(
                        lnst3[:, 6:7], lnst3[:, 5:6], AF.Sqrt, bias=np.float32(EPS)), "ac")
                    s.wait("vector", "ac", g_)
                    h_ = s.op("vector", lambda v: v.reciprocal(lnst3[:, 7:8], lnst3[:, 6:7]), "dv")
                    s.wait("vector", "dv", h_)
                    ap_ = s.op("vector", lambda v, xt=xt, t=t: v.tensor_scalar(
                        xl2tm[t % 2][:, :], xt[:, :], lnst3[:, 2:3], lnst3[:, 7:8],
                        ALU.subtract, ALU.mult), "dv")
                    dv_ap3[t] = ap_
                    s.wait("scalar", "dv", ap_)
                    if t >= 2:
                        s.wait("scalar", "pe", pe_gate[t - 2])
                    ac_hi[t] = s.op("scalar", lambda sc, t=t: sc.copy(
                        xh_tm[t % 2][:, :], xl2tm[t % 2][:, :]), "ac")
                    s.wait("vector", "ac", ac_hi[t])
                    lo_ = s.op("vector", lambda v, t=t: v.tensor_sub(
                        xl_tm[t % 2][:, :], xl2tm[t % 2][:, :], xh_tm[t % 2][:, :]), "dv")

                    s.wait("tensor", "ac", ac_hi[t])
                    s.wait("tensor", "dv", lo_)
                    for srcb, kind in ((xh_tm, "h"), (xl_tm, "l")):
                        for dc in range(NDC):
                            j = trj
                            trj += 1
                            if j >= 2:
                                s.wait("tensor", cpC[j - 2][0], cpC[j - 2][1])
                            tv = s.op("tensor", lambda tn, t=t, dc=dc, j=j, srcb=srcb: tn.transpose(
                                ptC[j % 2][:, :], srcb[t % 2][:, dc * 128:(dc + 1) * 128],
                                ident[:, :]), "pe")
                            if kind == "h":
                                dsl = slice(dc * TC + t * 128, dc * TC + (t + 1) * 128)
                                s.wait("scalar", "pe", tv)
                                val = s.op("scalar", lambda sc, j=j, dsl=dsl: sc.copy(
                                    xhfm[:, dsl], ptC[j % 2][:, :]), "ac2")
                                cpC[j] = ("ac2", val)
                            else:
                                dsl = slice(dc * 128, (dc + 1) * 128)
                                s.wait("vector", "pe", tv)
                                val = s.op("vector", lambda v, j=j, dsl=dsl, t=t: v.tensor_copy(
                                    xlfm[t % 2][:, dsl], ptC[j % 2][:, :]), "dv2")
                                cpC[j] = ("dv2", val)

                    s.wait("tensor", "ac2", cpC[t * 16 + 7][1])
                    s.wait("tensor", "dv2", cpC[t * 16 + 15][1])
                    g = t // 4
                    bank = (t % 4) // 2
                    col0 = (t % 2) * NE
                    if g >= 1 and t % 4 == 0:
                        s.wait("tensor", "dv", lt4_done[g - 1])
                    nmm = 0
                    for term in range(3):
                        rbuf = gwh if term != 1 else gwl
                        uselo = (term == 2)
                        for dc in range(NDC):
                            st_ = (nmm == 0 and t % 2 == 0)
                            sp_ = (nmm == 23 and t % 2 == 1)
                            if uselo:
                                pe_gate[t] = s.op("tensor", lambda tn, dc=dc, t=t, rbuf=rbuf,
                                                  st_=st_, sp_=sp_, bank=bank, col0=col0: tn.matmul(
                                    pgl[bank][:, col0:col0 + NE],
                                    xlfm[t % 2][:, dc * 128:(dc + 1) * 128],
                                    rbuf[:, dc * NE:(dc + 1) * NE], start=st_, stop=sp_), "pe")
                            else:
                                pe_gate[t] = s.op("tensor", lambda tn, dc=dc, t=t, rbuf=rbuf,
                                                  st_=st_, sp_=sp_, bank=bank, col0=col0: tn.matmul(
                                    pgl[bank][:, col0:col0 + NE],
                                    xhfm[:, dc * TC + t * 128: dc * TC + (t + 1) * 128],
                                    rbuf[:, dc * NE:(dc + 1) * NE], start=st_, stop=sp_), "pe")
                            nmm += 1
                    dv_tile[t] = s.cnt["dv"]

                    if t % 4 == 3:
                        s.wait("vector", "pe", pe_gate[t])

                        def dvc(fn):
                            val = s.op("vector", fn, "dv")
                            s.wait("vector", "dv", val)
                            return val

                        def v48(ap):
                            return ap.rearrange("p (g e) -> p g e", e=NE)

                        def bc4(t4_):
                            return t4_[:, :].rearrange(
                                "p (g o) -> p g o", o=1).broadcast_to([128, 4, NE])

                        iota_bc = iota8[:, :].rearrange(
                            "p (o e) -> p o e", o=1).broadcast_to([128, 4, NE])
                        gb2 = gb_bc[:, :].rearrange(
                            "p (o e) -> p o e", o=1).broadcast_to([128, 2, NE])
                        dvc(lambda v: v.tensor_tensor(
                            Lt4[:, 0:16].rearrange("p (g e) -> p g e", e=NE),
                            pgl[0][:, :].rearrange("p (g e) -> p g e", e=NE),
                            gb2, ALU.add))
                        lt4_done[g] = dvc(lambda v: v.tensor_tensor(
                            Lt4[:, 16:32].rearrange("p (g e) -> p g e", e=NE),
                            pgl[1][:, :].rearrange("p (g e) -> p g e", e=NE),
                            gb2, ALU.add))
                        dvc(lambda v: v.tensor_reduce(q4[0][:, :], v48(Lt4[:, :]), AX.X, ALU.max))
                        dvc(lambda v: v.tensor_tensor(v48(tA[:, :]), v48(Lt4[:, :]),
                                                      bc4(q4[0]), ALU.subtract))
                        s.wait("scalar", "dv", s.cnt["dv"])
                        ex_ = s.op("scalar", lambda sc: sc.activation(tB[:, :], tA[:, :], AF.Exp), "ac")
                        s.wait("vector", "ac", ex_)
                        dvc(lambda v: v.tensor_reduce(q4[1][:, :], v48(tB[:, :]), AX.X, ALU.add))
                        dvc(lambda v: v.reciprocal(q4[2][:, :], q4[1][:, :]))
                        dvc(lambda v: v.tensor_tensor(v48(tA[:, :]), v48(Lt4[:, :]),
                                                      bc4(q4[0]), ALU.is_equal))
                        dvc(lambda v: v.tensor_scalar(tA[:, :], tA[:, :], -64.0, 64.0,
                                                      ALU.mult, ALU.add))
                        dvc(lambda v: v.tensor_tensor(v48(tA[:, :]), v48(tA[:, :]),
                                                      iota_bc, ALU.add))
                        dvc(lambda v: v.tensor_reduce(q4[3][:, :], v48(tA[:, :]), AX.X, ALU.min))
                        dvc(lambda v: v.tensor_tensor(v48(tA[:, :]), iota_bc,
                                                      bc4(q4[3]), ALU.is_equal))
                        dvc(lambda v: v.scalar_tensor_tensor(tA[:, :], tA[:, :], -1e30,
                                                             Lt4[:, :], ALU.mult, ALU.add))
                        dvc(lambda v: v.tensor_reduce(q4[4][:, :], v48(tA[:, :]), AX.X, ALU.max))
                        dvc(lambda v: v.tensor_tensor(v48(tA[:, :]), v48(tA[:, :]),
                                                      bc4(q4[4]), ALU.is_equal))
                        dvc(lambda v: v.tensor_scalar(tA[:, :], tA[:, :], -64.0, 64.0,
                                                      ALU.mult, ALU.add))
                        dvc(lambda v: v.tensor_tensor(v48(tA[:, :]), v48(tA[:, :]),
                                                      iota_bc, ALU.add))
                        dvc(lambda v: v.tensor_reduce(q4[5][:, :], v48(tA[:, :]), AX.X, ALU.min))
                        dvc(lambda v: v.tensor_max(q4[6][:, :], q4[3][:, :], q4[5][:, :]))
                        dvc(lambda v: v.tensor_tensor(q4[7][:, :], q4[3][:, :], q4[5][:, :], ALU.is_gt))
                        dvc(lambda v: v.tensor_sub(q4[0][:, :], q4[0][:, :], q4[4][:, :]))
                        dvc(lambda v: v.tensor_mul(q4[0][:, :], q4[0][:, :], q4[7][:, :]))
                        dvc(lambda v: v.tensor_add(q4[0][:, :], q4[0][:, :], q4[4][:, :]))
                        dvc(lambda v: v.tensor_reduce(q4[1][:, :], v48(Lt4[:, :]), AX.X, ALU.max))
                        dvc(lambda v: v.tensor_sub(q4[0][:, :], q4[0][:, :], q4[1][:, :]))
                        s.wait("scalar", "dv", s.cnt["dv"])
                        ex2 = s.op("scalar", lambda sc: sc.activation(
                            q4[1][:, :], q4[0][:, :], AF.Exp), "ac")
                        s.wait("vector", "ac", ex2)
                        dvc(lambda v: v.tensor_mul(q4[1][:, :], q4[1][:, :], q4[2][:, :]))
                        lastg = None
                        for e in range(1, NE):
                            if lastg is not None:
                                s.wait("vector", "dv", lastg)
                            aa = s.op("vector", lambda v, e=e: v.tensor_scalar(
                                tA[:, 0:4], q4[6][:, :], float(e), None, ALU.is_equal), "dv")
                            s.wait("vector", "dv", aa)
                            lastg = s.op("vector", lambda v, e=e, g=g: v.tensor_mul(
                                G_all[:, e * 16 + 4 * g: e * 16 + 4 * g + 4],
                                tA[:, 0:4], q4[1][:, :]), "dv")
                        for tt_ in range(4 * g, 4 * g + 4):
                            dv_g[tt_] = lastg
                        dv_tile[t] = lastg
                    if t + 3 < NT:
                        s.wait("sync", "dv", dv_ap3[t])
                        s.dmaw("sync", ["ja", "jb", "jc"][t % 3])
                        sd_x2r[t + 3] = s.op("sync", lambda ss, t=t: ss.dma_start(
                            out=x2l[t % 3][:, :],
                            in_=x2d[(t + 3) * 128:(t + 4) * 128, :]),
                            ["ja", "jb", "jc"][t % 3], 16)

                s.wait("tensor", "dv", dv_g[NT - 1])
                s.wait("tensor", cpC[trj - 2][0], cpC[trj - 2][1])
                s.wait("tensor", cpC[trj - 1][0], cpC[trj - 1][1])
                gtr = s.op("tensor", lambda tn: tn.transpose(ptC[0][:, :], G_all[:, :], ident[:, :]), "pe")
                s.wait("vector", "pe", gtr)
                gcp = s.op("vector", lambda v: v.tensor_copy(GTf[:, :], ptC[0][:, :]), "dv")
                s.wait("sync", "dv", gcp)
                sd_grow = None
                for e in range(1, NE):
                    sd_grow = s.op("sync", lambda ss, e=e: ss.dma_start(
                        out=growd[e, :].rearrange("(a t) -> a t", t=128),
                        in_=GTf[e * 16:(e + 1) * 16, :]), "sd", 16)
                s.wait("sync", "sd", sd_grow)
                s.emit(nc)

            with ExitStack() as eE:
                hbuf = eE.enter_context(nc.sbuf_tensor("hbuf", [128, NHC * 1024], BF16))
                xe = [eE.enter_context(nc.sbuf_tensor(f"xe{i}", [128, NDC * 1024], BF16)) for i in range(2)]
                w1b = [eE.enter_context(nc.sbuf_tensor(f"w1b{i}", [128, 2048], BF16)) for i in range(3)]
                w2b = [eE.enter_context(nc.sbuf_tensor(f"w2b{i}", [128, 1024], BF16)) for i in range(2)]
                gbc = eE.enter_context(nc.sbuf_tensor("gbc", [128, TC], F32))
                m01 = [eE.enter_context(nc.sbuf_tensor(f"m01_{i}", [128, TC], BF16)) for i in range(2)]
                ytmp = [eE.enter_context(nc.sbuf_tensor(f"ytmp{i}", [128, 512], F32)) for i in range(2)]
                ps = [eE.enter_context(nc.psum_tensor(f"ps{i}", [128, 512], F32)) for i in range(8)]

                base = {k: s.cnt[k] for k in ("pe", "dv", "ac", "gd", "sd")}
                NW1 = 16
                NW2 = 32

                sd_gbc = {}
                for ei in range(7):
                    sd_gbc[ei] = s.cnt["gba"] + 16 * (ei + 1)
                gd_w1, gd_w2 = {}, {}
                w1sem, w2sem = {}, {}
                W1S = ["wa", "wb", "we"]
                W2S = ["wc", "wd"]
                w1c = {i: s.cnt[W1S[i]] for i in range(3)}
                w2c = {i: s.cnt[W2S[i]] for i in range(2)}
                g1i = 0
                g2i = 0
                for eh in range(NEH):
                    for fi in range(NW1):
                        p = g1i % 3
                        g1i += 1
                        w1c[p] += 16
                        gd_w1[(eh, fi)] = w1c[p]
                        w1sem[(eh, fi)] = W1S[p]
                    for dg in range(4):
                        for fi in range(8):
                            p = g2i % 2
                            g2i += 1
                            w2c[p] += 16
                            gd_w2[(eh, dg, fi)] = w2c[p]
                            w2sem[(eh, dg, fi)] = W2S[p]
                pe_h, pe_y, pe_hlast = {}, {}, {}
                pec = base["pe"]
                for eh in range(NEH):
                    for hc in range(NHC):
                        for tb in range(2):
                            pec += 1
                            pe_h[(eh, hc, tb)] = pec
                    pe_hlast[eh] = pec
                    for dg in range(4):
                        for fi in range(8):
                            pec += 1
                            pe_y[(eh, dg, fi)] = pec
                ac_h = {}
                acc = base["ac"]
                for eh in range(NEH):
                    for hc in range(NHC):
                        for tb in range(2):
                            acc += 1
                            ac_h[(eh, hc, tb)] = acc
                dv_prep, dv_yev = {}, {}
                dvc_ = base["dv"] + 1

                def count_prep(eh):
                    nonlocal dvc_
                    ei, hp = eh // 2, eh % 2
                    if hp == 0:
                        dvc_ += 1
                    dvc_ += NDC
                    dv_prep[eh] = dvc_

                def count_yev(eh):
                    nonlocal dvc_
                    for dg in range(4):
                        dvc_ += 8
                        dv_yev[(eh, dg)] = dvc_

                count_prep(0)
                count_prep(1)
                for eh in range(NEH):
                    if eh >= 1:
                        count_yev(eh - 1)
                    if eh + 2 < NEH:
                        count_prep(eh + 2)
                count_yev(NEH - 1)
                dv_final = dvc_

                for ei in range(7):
                    e = ei + 1
                    if ei == 0:
                        s.wait("sync", "sd", sd_grow)
                    if ei >= 1:
                        s.wait("sync", "dv", dv_prep[2 * ei - 1])
                    s.dmaw("sync", "gba")
                    got = s.op("sync", lambda ss, e=e: ss.dma_start(
                        out=gbc[:, :], in_=growd[e:e + 1, :].broadcast_to([128, TC])), "gba", 16)
                    assert got == sd_gbc[ei]

                for eh in range(NEH):
                    ei, hp = eh // 2, eh % 2
                    e = ei + 1
                    for fi in range(NW1):
                        g1 = eh * NW1 + fi
                        if g1 >= 3:
                            peh, pfi = (g1 - 3) // NW1, (g1 - 3) % NW1
                            s.wait("gpsimd", "pe", pe_h[(peh, pfi * 2 + 1, 1)])
                        s.dmaw("gpsimd", w1sem[(eh, fi)])
                        got = s.op("gpsimd", lambda g, ei=ei, fi=fi, g1=g1: g.dma_start(
                            out=w1b[g1 % 3][:, :],
                            in_=w1_in[ei, fi * 2:(fi + 1) * 2, :, :, :].rearrange("a p c h -> p a c h")),
                            w1sem[(eh, fi)], 16)
                        assert got == gd_w1[(eh, fi)]
                    for dg in range(4):
                        for fi in range(8):
                            g2 = (eh * 4 + dg) * 8 + fi
                            if g2 >= 2:
                                pg = g2 - 2
                                peh, prem = pg // 32, pg % 32
                                pdg, pfi = prem // 8, prem % 8
                                s.wait("gpsimd", "pe", pe_y[(peh, pdg, pfi)])
                            s.dmaw("gpsimd", w2sem[(eh, dg, fi)])
                            got = s.op("gpsimd", lambda g, ei=ei, dg=dg, fi=fi, g2=g2: g.dma_start(
                                out=w2b[g2 % 2][:, :],
                                in_=w2_in[ei, fi * 4:(fi + 1) * 4, :, dg * 2:(dg + 1) * 2, :].rearrange(
                                    "a p c h -> p a c h")), w2sem[(eh, dg, fi)], 16)
                            assert got == gd_w2[(eh, dg, fi)]

                for eh in range(NEH):
                    ei, hp = eh // 2, eh % 2
                    s.wait("tensor", "dv", dv_prep[eh])
                    if eh >= 1:
                        s.wait("tensor", "dv", dv_yev[(eh - 1, 2)])
                    for hc in range(NHC):
                        fi = hc // 2
                        g1 = eh * NW1 + fi
                        s.wait("tensor", w1sem[(eh, fi)], gd_w1[(eh, fi)])
                        if hc >= 2:
                            s.wait("tensor", "ac", ac_h[(eh, hc - 2, 1)])
                        for tb in range(2):
                            got = None
                            for dc in range(NDC):
                                inc = "pe" if dc == NDC - 1 else None
                                got = s.op("tensor", lambda tn, hc=hc, tb=tb, dc=dc, g1=g1, eh=eh:
                                           tn.matmul(ps[2 * (hc % 2) + tb][:, :],
                                                     w1b[g1 % 3][:, ((hc % 2) * NDC + dc) * 128:
                                                                 ((hc % 2) * NDC + dc + 1) * 128],
                                                     xe[eh % 2][:, dc * 1024 + tb * 512:
                                                                dc * 1024 + (tb + 1) * 512],
                                                     start=(dc == 0), stop=(dc == NDC - 1)), inc)
                            assert got == pe_h[(eh, hc, tb)]
                    s.wait("tensor", "ac", ac_h[(eh, NHC - 1, 1)])
                    for dg in range(4):
                        if dg >= 2:
                            s.wait("tensor", "dv", dv_yev[(eh, dg - 2)])
                        elif dg == 1 and eh >= 1:
                            s.wait("tensor", "dv", dv_yev[(eh - 1, 3)])
                        for fi in range(8):
                            g2 = (eh * 4 + dg) * 8 + fi
                            s.wait("tensor", w2sem[(eh, dg, fi)], gd_w2[(eh, dg, fi)])
                            got = None
                            for hcl in range(4):
                                hc = fi * 4 + hcl
                                for dcl in range(2):
                                    for tb in range(2):
                                        pidx = 4 * (dg % 2) + dcl * 2 + tb
                                        inc = "pe" if (hcl == 3 and dcl == 1 and tb == 1) else None
                                        got = s.op("tensor", lambda tn, hc=hc, hcl=hcl, dcl=dcl,
                                                   tb=tb, g2=g2, pidx=pidx: tn.matmul(
                                            ps[pidx][:, :],
                                            w2b[g2 % 2][:, (hcl * 2 + dcl) * 128:
                                                        (hcl * 2 + dcl + 1) * 128],
                                            hbuf[:, hc * 1024 + tb * 512: hc * 1024 + (tb + 1) * 512],
                                            start=(hc == 0), stop=(hc == NHC - 1)), inc)
                            assert got == pe_y[(eh, dg, fi)]
                assert s.cnt["pe"] == pec

                for eh in range(NEH):
                    ei = eh // 2
                    for hc in range(NHC):
                        for tb in range(2):
                            s.wait("scalar", "pe", pe_h[(eh, hc, tb)])
                            got = s.op("scalar", lambda sc, hc=hc, tb=tb, ei=ei: sc.activation(
                                hbuf[:, hc * 1024 + tb * 512: hc * 1024 + (tb + 1) * 512],
                                ps[2 * (hc % 2) + tb][:, :], AF.Relu,
                                bias=b1s[:, ei * NHC + hc: ei * NHC + hc + 1]), "ac")
                            assert got == ac_h[(eh, hc, tb)]

                got = s.op("vector", lambda v: v.memset(yac[:, :], 0.0), "dv")
                assert got == base["dv"] + 1

                def rec_prep(eh):
                    ei, hp = eh // 2, eh % 2
                    if eh >= 2:
                        s.wait("vector", "pe", pe_hlast[eh - 2])
                    s.wait("vector", "gba", sd_gbc[ei])
                    last = None
                    if hp == 0:
                        last = s.op("vector", lambda v, ei=ei: v.tensor_single_scalar(
                            m01[ei % 2][:, :], gbc[:, :], 0.0, ALU.is_gt), "dv")
                    for dc in range(NDC):
                        last = s.op("vector", lambda v, dc=dc, hp=hp, eh=eh: v.tensor_mul(
                            xe[eh % 2][:, dc * 1024:(dc + 1) * 1024],
                            xhfm[:, dc * TC + hp * 1024: dc * TC + (hp + 1) * 1024],
                            gbc[:, hp * 1024:(hp + 1) * 1024]), "dv")
                    assert last == dv_prep[eh]

                def rec_yev(eh):
                    ei, hp = eh // 2, eh % 2
                    for dg in range(4):
                        s.wait("vector", "pe", pe_y[(eh, dg, 7)])
                        last = None
                        for dcl in range(2):
                            dc = dg * 2 + dcl
                            for tb in range(2):
                                pidx = 4 * (dg % 2) + dcl * 2 + tb
                                ysl = slice(dc * TC + hp * 1024 + tb * 512,
                                            dc * TC + hp * 1024 + (tb + 1) * 512)
                                msl = slice(hp * 1024 + tb * 512, hp * 1024 + (tb + 1) * 512)
                                aa = s.op("vector", lambda v, pidx=pidx, ei=ei, dc=dc, msl=msl, tb=tb:
                                          v.scalar_tensor_tensor(
                                              ytmp[tb][:, :], ps[pidx][:, :],
                                              b2s[:, ei * NDC + dc: ei * NDC + dc + 1],
                                              m01[ei % 2][:, msl], ALU.add, ALU.mult), "dv")
                                s.wait("vector", "dv", aa)
                                last = s.op("vector", lambda v, ysl=ysl, tb=tb: v.tensor_add(
                                    yac[:, ysl], yac[:, ysl], ytmp[tb][:, :]), "dv")
                        assert last == dv_yev[(eh, dg)]

                rec_prep(0)
                rec_prep(1)
                for eh in range(NEH):
                    if eh >= 1:
                        rec_yev(eh - 1)
                    if eh + 2 < NEH:
                        rec_prep(eh + 2)
                rec_yev(NEH - 1)
                assert s.cnt["dv"] == dv_final

                s.emit(nc)

            with ExitStack() as eF:
                x2l5 = [eF.enter_context(nc.sbuf_tensor(f"x2l5_{i}", [128, D], F32)) for i in range(2)]
                ostg = [eF.enter_context(nc.sbuf_tensor(f"ostg{i}", [128, D], F32)) for i in range(2)]
                ptD = [eF.enter_context(nc.psum_tensor(f"ptD{i}", [128, 128], BF16)) for i in range(2)]

                dv_o5 = {}
                sd_5 = [None] * NT
                st_5 = [None] * NT
                tr5 = {}
                for t in range(2):
                    s.dmaw("sync", "ja" if t % 2 == 0 else "jb")
                    sd_5[t] = s.op("sync", lambda ss, t=t: ss.dma_start(
                        out=x2l5[t % 2][:, :], in_=x2d[t * 128:(t + 1) * 128, :]),
                        "ja" if t % 2 == 0 else "jb", 16)
                for t in range(NT):
                    for dc in range(NDC):
                        j = t * NDC + dc
                        if j >= 2:
                            s.wait("tensor", "dv", dv_o5[((j - 2) // NDC, (j - 2) % NDC)])
                        tr5[j] = s.op("tensor", lambda tn, t=t, dc=dc, j=j: tn.transpose(
                            ptD[j % 2][:, :], yac[:, dc * TC + t * 128: dc * TC + (t + 1) * 128],
                            ident[:, :]), "pe")
                        s.wait("vector", "pe", tr5[j])
                        s.wait("vector", "ja" if t % 2 == 0 else "jb", sd_5[t])
                        if t >= 2 and dc == 0:
                            s.wait("vector", "sta" if t % 2 == 0 else "stb", st_5[t - 2])
                        dv_o5[(t, dc)] = s.op("vector", lambda v, t=t, dc=dc, j=j: v.tensor_add(
                            ostg[t % 2][:, dc * 128:(dc + 1) * 128],
                            x2l5[t % 2][:, dc * 128:(dc + 1) * 128],
                            ptD[j % 2][:, :]), "dv")
                    s.wait("sync", "dv", dv_o5[(t, NDC - 1)])
                    s.dmaw("sync", "sta" if t % 2 == 0 else "stb")
                    st_5[t] = s.op("sync", lambda ss, t=t: ss.dma_start(
                        out=y_out[t * 128:(t + 1) * 128, :], in_=ostg[t % 2][:, :]),
                        "sta" if t % 2 == 0 else "stb", 16)
                    if t + 2 < NT:
                        s.dmaw("sync", "ja" if t % 2 == 0 else "jb")
                        sd_5[t + 2] = s.op("sync", lambda ss, t=t: ss.dma_start(
                            out=x2l5[t % 2][:, :], in_=x2d[(t + 2) * 128:(t + 3) * 128, :]),
                            "ja" if t % 2 == 0 else "jb", 16)
                s.wait("sync", "sta", s.cnt["sta"])
                s.wait("sync", "stb", s.cnt["stb"])
                s.emit(nc)

    return nc


_PROG_CACHE = {}


def _get_program():
    if "nc" not in _PROG_CACHE:
        _PROG_CACHE["nc"] = build_program()
    return _PROG_CACHE["nc"]


def kernel(L0_attn_x, L0_attn_Wq, L0_attn_Wk, L0_attn_Wv, L0_attn_Wo,
           L0_attn_q_gamma, L0_attn_q_beta, L0_attn_k_gamma, L0_attn_k_beta,
           L0_ffn_gate_w, L0_ffn_gate_b, L0_ffn_w1, L0_ffn_b1, L0_ffn_w2,
           L0_ffn_b2):
    f = np.float32
    bf = ml_dtypes.bfloat16
    x = np.ascontiguousarray(np.asarray(L0_attn_x, dtype=f).reshape(T, D))

    wqT = np.ascontiguousarray(np.asarray(L0_attn_Wq, dtype=f).T)
    wkT = np.ascontiguousarray(np.asarray(L0_attn_Wk, dtype=f).T)
    wvT = np.ascontiguousarray(np.asarray(L0_attn_Wv, dtype=f).T)
    woT = np.ascontiguousarray(np.asarray(L0_attn_Wo, dtype=f).T)

    qg_t = np.ascontiguousarray(np.broadcast_to(np.asarray(L0_attn_q_gamma, f), (128, DH)))
    qb_t = np.ascontiguousarray(np.broadcast_to(np.asarray(L0_attn_q_beta, f), (128, DH)))
    kg_t = np.ascontiguousarray(np.broadcast_to(np.asarray(L0_attn_k_gamma, f), (128, DH)))
    kb_t = np.ascontiguousarray(np.broadcast_to(np.asarray(L0_attn_k_beta, f), (128, DH)))

    gwt = np.ascontiguousarray(np.asarray(L0_ffn_gate_w, f).reshape(NDC, 128, NE))
    gbb = np.ascontiguousarray(np.broadcast_to(np.asarray(L0_ffn_gate_b, f), (128, NE)))

    w1 = np.asarray(L0_ffn_w1, f)
    w2 = np.asarray(L0_ffn_w2, f)
    b1 = np.asarray(L0_ffn_b1, f)
    b2 = np.asarray(L0_ffn_b2, f)
    w1t = np.ascontiguousarray(
        w1[1:].reshape(7, NDC, 128, NHC, 128).transpose(0, 3, 2, 1, 4))
    w2t = np.ascontiguousarray(
        w2[1:].reshape(7, NHC, 128, NDC, 128))
    b1t = np.ascontiguousarray(b1[1:].reshape(7, NHC, 128).transpose(0, 2, 1))
    b2t = np.ascontiguousarray(b2[1:].reshape(7, NDC, 128).transpose(0, 2, 1))

    identity = np.eye(128, dtype=bf)
    em = np.ones((128, 512), dtype=f)
    tri = (np.arange(128)[:, None] <= np.arange(128)[None, :]).astype(f)
    em[:, 0:128] = tri
    iota8 = np.ascontiguousarray(np.broadcast_to(np.arange(NE, dtype=f), (128, NE)))

    common = {
        "wqT": wqT, "wkT": wkT, "wvT": wvT, "woT": woT,
        "qg_t": qg_t, "qb_t": qb_t, "kg_t": kg_t, "kb_t": kb_t,
        "gwt": gwt, "gbb": gbb, "w1t": w1t, "w2t": w2t, "b1t": b1t, "b2t": b2t,
        "ident": identity, "emask": em, "iota8": iota8,
    }
    in_maps = []
    for c in range(NCORES):
        sel = np.zeros((128, NE), dtype=f)
        sel[:, :c] = 1.0
        m = dict(common)
        m["x"] = np.ascontiguousarray(x[c * TC:(c + 1) * TC, :])
        m["selbc"] = sel
        in_maps.append(m)

    nc = _get_program()
    res = run_bass_kernel_spmd(nc, in_maps, core_ids=list(range(NCORES)))
    out = np.concatenate([res.results[c]["y"] for c in range(NCORES)], axis=0)
    return out.reshape(1, T, D).astype(np.float32)


# version 13
# speedup vs baseline: 1.0162x; 1.0030x over previous
import numpy as np
import ml_dtypes
from contextlib import ExitStack

import concourse.bass as bass
import concourse.mybir as mybir
from concourse.bass_utils import run_bass_kernel_spmd

F32 = mybir.dt.float32
BF16 = mybir.dt.bfloat16
AF = mybir.ActivationFunctionType
ALU = mybir.AluOpType
AX = mybir.AxisListType

NCORES = 8
T = 16384
TC = T // NCORES
NT = TC // 128
D = 1024
NDC = D // 128
NH = 16
DH = 64
NSB = 4
SBT = TC // NSB
NE = 8
HD = 4096
NHC = HD // 128
NEH = 14
EPS = 1e-5

ENGINES = ("sync", "gpsimd", "tensor", "vector", "scalar")


class Sched:

    def __init__(self, sems):
        self.sems = sems
        self.cnt = {k: 0 for k in sems}
        self.items = {e: [] for e in ENGINES}

    def wait(self, eng, sem, val):
        if val is None:
            raise ValueError(f"wait on unrecorded value for sem {sem}")
        if val <= 0:
            return
        self.items[eng].append(lambda sec, s=self.sems[sem], v=val: sec.wait_ge(s, v))

    def dmaw(self, eng, sem):
        self.wait(eng, sem, self.cnt[sem])

    def op(self, eng, fn, inc=None, k=1):
        if inc is not None:
            self.cnt[inc] += k
            val = self.cnt[inc]
            self.items[eng].append(
                lambda sec, f=fn, s=self.sems[inc], kk=k: f(sec).then_inc(s, kk))
            return val
        self.items[eng].append(lambda sec, f=fn: f(sec))
        return None

    def emit(self, nc):
        with nc.Block() as blk:
            for eng in ENGINES:
                if not self.items[eng]:
                    continue
                items = self.items[eng]

                def body(sec, items=items):
                    for it in items:
                        it(sec)
                getattr(blk, eng)(body)
        self.items = {e: [] for e in ENGINES}


def build_program(debug=False):
    nc = bass.Bass("TRN2", target_bir_lowering=False)

    x_in = nc.dram_tensor("x", [TC, D], F32, kind="ExternalInput").ap()
    wq_in = nc.dram_tensor("wqT", [D, D], F32, kind="ExternalInput").ap()
    wk_in = nc.dram_tensor("wkT", [D, D], F32, kind="ExternalInput").ap()
    wv_in = nc.dram_tensor("wvT", [D, D], F32, kind="ExternalInput").ap()
    wo_in = nc.dram_tensor("woT", [D, D], F32, kind="ExternalInput").ap()
    qg_in = nc.dram_tensor("qg_t", [128, DH], F32, kind="ExternalInput").ap()
    qb_in = nc.dram_tensor("qb_t", [128, DH], F32, kind="ExternalInput").ap()
    kg_in = nc.dram_tensor("kg_t", [128, DH], F32, kind="ExternalInput").ap()
    kb_in = nc.dram_tensor("kb_t", [128, DH], F32, kind="ExternalInput").ap()
    gw_in = nc.dram_tensor("gwt", [NDC, 128, NE], F32, kind="ExternalInput").ap()
    gb_in = nc.dram_tensor("gbb", [128, NE], F32, kind="ExternalInput").ap()
    w1_in = nc.dram_tensor("w1t", [7, NHC, 128, NDC, 128], F32, kind="ExternalInput").ap()
    w2_in = nc.dram_tensor("w2t", [7, NHC, 128, NDC, 128], F32, kind="ExternalInput").ap()
    b1_in = nc.dram_tensor("b1t", [7, 128, NHC], F32, kind="ExternalInput").ap()
    b2_in = nc.dram_tensor("b2t", [7, 128, NDC], F32, kind="ExternalInput").ap()
    id_in = nc.dram_tensor("ident", [128, 128], BF16, kind="ExternalInput").ap()
    em_in = nc.dram_tensor("emask", [128, 512], BF16, kind="ExternalInput").ap()
    io_in = nc.dram_tensor("iota8", [128, NE], F32, kind="ExternalInput").ap()
    se_in = nc.dram_tensor("selbc", [128, NE], F32, kind="ExternalInput").ap()
    y_out = nc.dram_tensor("y", [TC, D], F32, kind="ExternalOutput").ap()

    x2d = nc.dram_tensor("x2d", [TC, D], F32).ap()
    if debug:
        dbg_xln1 = nc.dram_tensor("dbg_xln1", [128, NDC * TC], F32, kind="ExternalOutput").ap()
        dbg_xln0 = nc.dram_tensor("dbg_xln0", [128, NDC * TC], F32, kind="ExternalOutput").ap()
        dbg_qtm = nc.dram_tensor("dbg_qtm", [128, NT * D], F32, kind="ExternalOutput").ap()
        dbg_ktm = nc.dram_tensor("dbg_ktm", [128, NT * D], F32, kind="ExternalOutput").ap()
        dbg_vtm = nc.dram_tensor("dbg_vtm", [128, NT * D], F32, kind="ExternalOutput").ap()
        dbg_ofm = nc.dram_tensor("dbg_ofm", [128, NDC * SBT], F32, kind="ExternalOutput").ap()
        dbg_srun = nc.dram_tensor("dbg_srun", [64, NH * DH], F32, kind="ExternalOutput").ap()
    growd = nc.dram_tensor("growd", [NE, TC], F32).ap()
    sownd = nc.dram_tensor("sownd", [64, NH * DH], F32)
    sgath = nc.dram_tensor("sgath", [NCORES * 64, NH * DH], F32)

    with ExitStack() as es:
        sems = {}
        for name in ["sd", "gd", "pe", "dv", "ac", "pe2", "dv2", "ac2", "cc", "st",
                     "xa", "xb", "ra", "rb", "sta", "stb", "ja", "jb", "jc", "gba",
                     "wa", "wb", "wc", "wd", "pl", "we", "wf"]:
            sems[name] = es.enter_context(nc.semaphore(name))
        s = Sched(sems)

        ident = es.enter_context(nc.sbuf_tensor("c_ident", [128, 128], BF16))
        emask = es.enter_context(nc.sbuf_tensor("c_emask", [128, 512], BF16))
        iota8 = es.enter_context(nc.sbuf_tensor("c_iota8", [128, NE], F32))
        selbc = es.enter_context(nc.sbuf_tensor("c_selbc", [128, NE], F32))
        gb_bc = es.enter_context(nc.sbuf_tensor("c_gbb", [128, NE], F32))
        gwf = es.enter_context(nc.sbuf_tensor("c_gwf", [128, NDC * NE], F32))
        gwh = es.enter_context(nc.sbuf_tensor("c_gwh", [128, NDC * NE], BF16))
        gwl = es.enter_context(nc.sbuf_tensor("c_gwl", [128, NDC * NE], BF16))
        b1s = es.enter_context(nc.sbuf_tensor("c_b1s", [128, 7 * NHC], F32))
        b2s = es.enter_context(nc.sbuf_tensor("c_b2s", [128, 7 * NDC], F32))
        s_run = es.enter_context(nc.sbuf_tensor("s_run", [64, NH * DH], F32))
        s_bf = es.enter_context(nc.sbuf_tensor("s_bf", [64, NH * DH], BF16))

        with ExitStack() as eAB:
            qtm = eAB.enter_context(nc.sbuf_tensor("qtm", [128, NT * D], BF16))
            ktm = eAB.enter_context(nc.sbuf_tensor("ktm", [128, NT * D], BF16))
            vtm = eAB.enter_context(nc.sbuf_tensor("vtm", [128, NT * D], BF16))

            with ExitStack() as eA:
                xln1 = eA.enter_context(nc.sbuf_tensor("xln1", [128, NDC * TC], BF16))
                wk_s = eA.enter_context(nc.sbuf_tensor("wk_s", [128, NDC * D], BF16))
                wv_s = eA.enter_context(nc.sbuf_tensor("wv_s", [128, NDC * D], BF16))
                gamq = eA.enter_context(nc.sbuf_tensor("gamq", [128, DH], F32))
                betq = eA.enter_context(nc.sbuf_tensor("betq", [128, DH], F32))
                gamk = eA.enter_context(nc.sbuf_tensor("gamk", [128, DH], F32))
                betk = eA.enter_context(nc.sbuf_tensor("betk", [128, DH], F32))
                xld = [eA.enter_context(nc.sbuf_tensor(f"xld{i}", [128, D], F32)) for i in range(2)]
                sqb = eA.enter_context(nc.sbuf_tensor("sqb", [128, D], F32))
                xlt = [eA.enter_context(nc.sbuf_tensor(f"xlt{i}", [128, D], BF16)) for i in range(2)]
                lnst = eA.enter_context(nc.sbuf_tensor("lnst", [128, 8], F32))
                pf32 = [eA.enter_context(nc.sbuf_tensor(f"pf32_{i}", [128, D], F32)) for i in range(2)]
                fsc = eA.enter_context(nc.sbuf_tensor("fsc", [128, D], F32))
                sqA = eA.enter_context(nc.sbuf_tensor("sqA", [128, D], F32))
                sowns = eA.enter_context(nc.sbuf_tensor("sowns", [64, NH * DH], F32))
                hstat = eA.enter_context(nc.sbuf_tensor("hstat", [128, 4 * NH], F32))

                ptA = [eA.enter_context(nc.psum_tensor(f"ptA{i}", [128, 128], BF16)) for i in range(2)]
                ppA = [eA.enter_context(nc.psum_tensor(f"ppA{i}", [128, 512], F32)) for i in range(4)]
                psS = [eA.enter_context(nc.psum_tensor(f"psS{i}", [64, 512], F32)) for i in range(2)]

                gd_id = s.op("gpsimd", lambda g: g.dma_start(out=ident[:, :], in_=id_in), "gd", 16)
                for dst, src in ((emask, em_in), (iota8, io_in), (selbc, se_in), (gb_bc, gb_in)):
                    s.op("gpsimd", lambda g, d=dst, i=src: g.dma_start(out=d[:, :], in_=i), "gd", 16)
                gd_gw = s.op("gpsimd", lambda g: g.dma_start(
                    out=gwf[:, :], in_=gw_in.rearrange("c p e -> p c e")), "gd", 16)
                for dst, src in ((gamq, qg_in), (betq, qb_in), (gamk, kg_in), (betk, kb_in)):
                    s.op("gpsimd", lambda g, d=dst, i=src: g.dma_start(out=d[:, :], in_=i), "gd", 16)
                s.op("gpsimd", lambda g: g.dma_start(
                    out=b1s[:, :], in_=b1_in.rearrange("e p h -> p e h")), "gd", 16)
                s.op("gpsimd", lambda g: g.dma_start(
                    out=b2s[:, :], in_=b2_in.rearrange("e p d -> p e d")), "gd", 16)
                gd_consts = s.cnt["gd"]
                s.wait("gpsimd", "gd", gd_consts)
                gd_wk = None
                for dc in range(NDC):
                    gd_wk = s.op("gpsimd", lambda g, dc=dc: g.dma_start(
                        out=wk_s[:, dc * D:(dc + 1) * D],
                        in_=wk_in[dc * 128:(dc + 1) * 128, :]), "gd", 16)
                s.wait("gpsimd", "gd", gd_wk)
                gd_wv = None
                for dc in range(NDC):
                    gd_wv = s.op("gpsimd", lambda g, dc=dc: g.dma_start(
                        out=wv_s[:, dc * D:(dc + 1) * D],
                        in_=wv_in[dc * 128:(dc + 1) * 128, :]), "gd", 16)

                s.wait("gpsimd", "gd", gd_wv)
                s.wait("vector", "gd", gd_consts)
                a = s.op("vector", lambda v: v.tensor_copy(gwh[:, :], gwf[:, :]), "dv")
                s.wait("vector", "dv", a)
                s.op("vector", lambda v: v.tensor_sub(gwl[:, :], gwf[:, :], gwh[:, :]), "dv")

                sd_x = [None] * NT
                dv_ap1 = [None] * NT
                dv_f0 = [None] * NT
                pe_tr1 = [[None] * NDC for _ in range(NT)]
                cp1 = {}
                for t in range(2):
                    s.dmaw("sync", "xa" if t % 2 == 0 else "xb")
                    sd_x[t] = s.op("sync", lambda ss, t=t: ss.dma_start(
                        out=xld[t % 2][:, :], in_=x_in[t * 128:(t + 1) * 128, :]),
                        "xa" if t % 2 == 0 else "xb", 16)

                def emit_b0_tile(t):
                    xt = xld[t % 2]
                    s.wait("vector", "xa" if t % 2 == 0 else "xb", sd_x[t])
                    if t >= 1:
                        s.wait("vector", "dv", dv_ap1[t - 1])
                    if t >= 2:
                        s.wait("vector", "pe", pe_tr1[t - 2][NDC - 1])
                    s.wait("scalar", "xa" if t % 2 == 0 else "xb", sd_x[t])
                    if t >= 1:
                        s.wait("scalar", "dv", dv_f0[t - 1])
                    s.wait("scalar", "ac", s.cnt["ac"])
                    s.op("scalar", lambda sc, xt=xt: sc.activation(
                        sqb[:, :], xt[:, :], AF.Identity, accum_out=lnst[:, 0:1]), "ac")
                    s.wait("scalar", "ac", s.cnt["ac"])
                    sq0 = s.op("scalar", lambda sc, xt=xt: sc.activation(
                        sqb[:, :], xt[:, :], AF.Square, accum_out=lnst[:, 1:2]), "ac")
                    s.wait("vector", "ac", sq0)
                    d = s.op("vector", lambda v: v.tensor_scalar_mul(
                        lnst[:, 2:3], lnst[:, 0:1], 1.0 / D), "dv")
                    s.wait("vector", "dv", d)
                    e = s.op("vector", lambda v: v.tensor_mul(
                        lnst[:, 4:5], lnst[:, 2:3], lnst[:, 2:3]), "dv")
                    s.wait("vector", "dv", e)
                    f = s.op("vector", lambda v: v.tensor_scalar(
                        lnst[:, 5:6], lnst[:, 1:2], 1.0 / D,
                        lnst[:, 4:5], ALU.mult, ALU.subtract), "dv")
                    dv_f0[t] = f
                    s.wait("scalar", "dv", f)
                    g_ = s.op("scalar", lambda sc: sc.activation(
                        lnst[:, 6:7], lnst[:, 5:6], AF.Sqrt, bias=np.float32(EPS)), "ac")
                    s.wait("vector", "ac", g_)
                    h_ = s.op("vector", lambda v: v.reciprocal(lnst[:, 7:8], lnst[:, 6:7]), "dv")
                    s.wait("vector", "dv", h_)
                    dv_ap1[t] = s.op("vector", lambda v, xt=xt, t=t: v.tensor_scalar(
                        xlt[t % 2][:, :], xt[:, :], lnst[:, 2:3], lnst[:, 7:8],
                        ALU.subtract, ALU.mult), "dv")
                    if t + 2 < NT:
                        s.wait("sync", "dv", dv_ap1[t])
                        s.dmaw("sync", "xa" if t % 2 == 0 else "xb")
                        sd_x[t + 2] = s.op("sync", lambda ss, t=t: ss.dma_start(
                            out=xld[t % 2][:, :],
                            in_=x_in[(t + 2) * 128:(t + 3) * 128, :]),
                            "xa" if t % 2 == 0 else "xb", 16)

                    if t == 0:
                        s.wait("tensor", "gd", gd_consts)
                    s.wait("tensor", "dv", dv_ap1[t])
                    for dc in range(NDC):
                        j = t * NDC + dc
                        if j >= 2:
                            s.wait("tensor", cp1[j - 2][0], cp1[j - 2][1])
                        pe_tr1[t][dc] = s.op("tensor", lambda tn, t=t, dc=dc, j=j: tn.transpose(
                            ptA[j % 2][:, :], xlt[t % 2][:, dc * 128:(dc + 1) * 128],
                            ident[:, :]), "pe")
                        dsl = slice(dc * TC + t * 128, dc * TC + (t + 1) * 128)
                        if dc % 2 == 0:
                            s.wait("vector", "pe", pe_tr1[t][dc])
                            val = s.op("vector", lambda v, j=j, dsl=dsl: v.tensor_copy(
                                xln1[:, dsl], ptA[j % 2][:, :]), "dv2")
                            cp1[j] = ("dv2", val)
                        else:
                            s.wait("scalar", "pe", pe_tr1[t][dc])
                            val = s.op("scalar", lambda sc, j=j, dsl=dsl: sc.copy(
                                xln1[:, dsl], ptA[j % 2][:, :]), "ac2")
                            cp1[j] = ("ac2", val)

                def hln_stats(src, sq_ready_dv):
                    s.wait("scalar", "dv", sq_ready_dv)
                    s.wait("scalar", "ac", s.cnt["ac"])
                    sq_ = s.op("scalar", lambda sc, src=src: sc.activation(
                        sqA[:, :], src[:, :], AF.Square), "ac")
                    a = s.op("vector", lambda v, src=src: v.reduce_sum(
                        hstat[:, 0:16], src[:, :].rearrange("p (h d) -> p h d", d=DH),
                        axis=AX.X), "dv")
                    s.wait("vector", "ac", sq_)
                    c = s.op("vector", lambda v: v.reduce_sum(
                        hstat[:, 48:64], sqA[:, :].rearrange("p (h d) -> p h d", d=DH),
                        axis=AX.X), "dv")
                    s.wait("vector", "dv", a)
                    d = s.op("vector", lambda v: v.tensor_scalar_mul(
                        hstat[:, 32:48], hstat[:, 0:16], 1.0 / DH), "dv")
                    s.wait("vector", "dv", d)
                    e = s.op("vector", lambda v: v.tensor_mul(
                        hstat[:, 0:16], hstat[:, 32:48], hstat[:, 32:48]), "dv")
                    s.wait("vector", "dv", c)
                    f0 = s.op("vector", lambda v: v.tensor_scalar_mul(
                        hstat[:, 48:64], hstat[:, 48:64], 1.0 / DH), "dv")
                    s.wait("vector", "dv", max(f0, e))
                    f = s.op("vector", lambda v: v.tensor_sub(
                        hstat[:, 16:32], hstat[:, 48:64], hstat[:, 0:16]), "dv")
                    s.wait("scalar", "dv", f)
                    g_ = s.op("scalar", lambda sc: sc.activation(
                        hstat[:, 48:64], hstat[:, 16:32], AF.Sqrt, bias=np.float32(EPS)), "ac")
                    s.wait("vector", "ac", g_)
                    h_ = s.op("vector", lambda v: v.reciprocal(
                        hstat[:, 16:32], hstat[:, 48:64]), "dv")
                    s.wait("vector", "dv", h_)

                def hln_apply(src, dstb, dsto, gam, bet):
                    def v3(ap):
                        return ap.rearrange("p (h d) -> p h d", d=DH)

                    def stat_bc(col0):
                        return hstat[:, col0:col0 + NH].rearrange(
                            "p (h o) -> p h o", o=1).broadcast_to([128, NH, DH])

                    def row_bc(t_):
                        return t_[:, :].rearrange(
                            "p (o d) -> p o d", o=1).broadcast_to([128, NH, DH])

                    a = s.op("vector", lambda v, src=src: v.tensor_tensor(
                        v3(fsc[:, :]), v3(src[:, :]), stat_bc(32), ALU.subtract), "dv")
                    s.wait("vector", "dv", a)
                    b = s.op("vector", lambda v, src=src: v.tensor_tensor(
                        v3(src[:, :]), v3(fsc[:, :]), stat_bc(16), ALU.mult), "dv")
                    s.wait("vector", "dv", b)
                    cc_ = s.op("vector", lambda v, src=src, gam=gam: v.tensor_tensor(
                        v3(fsc[:, :]), v3(src[:, :]), row_bc(gam), ALU.mult), "dv")
                    s.wait("vector", "dv", cc_)
                    last = s.op("vector", lambda v, dstb=dstb, dsto=dsto, bet=bet: v.tensor_tensor(
                        v3(dstb[:, dsto:dsto + D]), v3(fsc[:, :]), row_bc(bet), ALU.add), "dv")
                    return last

                def proj_mms(t, wbuf):
                    last = None
                    for dc in range(NDC):
                        lsl = slice(dc * TC + t * 128, dc * TC + (t + 1) * 128)
                        for half in range(2):
                            last = s.op("tensor", lambda tn, dc=dc, half=half, lsl=lsl, w=wbuf:
                                        tn.matmul(ppA[half][:, :], xln1[:, lsl],
                                                  w[:, dc * D + half * 512: dc * D + (half + 1) * 512],
                                                  start=(dc == 0), stop=(dc == NDC - 1)), "pe")
                    return last

                pe_k = [None] * NT
                pe_v = [None] * NT
                ac_k = [None] * NT
                ac_v = [None] * NT
                dv_k = [None] * NT
                dv_b = [None] * NT
                dv_sq = [None] * NT
                dv_b2 = [None] * NT
                dv_sq2 = [None] * NT
                pe_st = [None] * NT

                def emit_b1_ktile(t):
                    if t == 0:
                        s.wait("tensor", "gd", gd_wk)
                    s.wait("tensor", cp1[t * NDC + NDC - 2][0], cp1[t * NDC + NDC - 2][1])
                    s.wait("tensor", cp1[t * NDC + NDC - 1][0], cp1[t * NDC + NDC - 1][1])
                    if t >= 1:
                        s.wait("tensor", "ac", ac_k[t - 1])
                    pe_k[t] = proj_mms(t, wk_s)
                    if t == 0:
                        s.wait("tensor", "gd", gd_wv)
                    if t >= 1:
                        s.wait("tensor", "ac", ac_v[t - 1])
                    last = None
                    for dc in range(NDC):
                        lsl = slice(dc * TC + t * 128, dc * TC + (t + 1) * 128)
                        for half in range(2):
                            last = s.op("tensor", lambda tn, dc=dc, half=half, lsl=lsl:
                                        tn.matmul(ppA[2 + half][:, :], xln1[:, lsl],
                                                  wv_s[:, dc * D + half * 512: dc * D + (half + 1) * 512],
                                                  start=(dc == 0), stop=(dc == NDC - 1)), "pe")
                    pe_v[t] = last

                    s.wait("scalar", "pe", pe_k[t])
                    if t >= 2:
                        s.wait("scalar", "dv", dv_k[t - 2])
                    s.op("scalar", lambda sc, t=t: sc.copy(pf32[t % 2][:, 0:512], ppA[0][:, :]), "ac")
                    ac_k[t] = s.op("scalar", lambda sc, t=t: sc.copy(
                        pf32[t % 2][:, 512:1024], ppA[1][:, :]), "ac")
                    s.wait("scalar", "pe", pe_v[t])
                    s.op("scalar", lambda sc, t=t: sc.copy(
                        vtm[:, t * D: t * D + 512], ppA[2][:, :]), "ac")
                    ac_v[t] = s.op("scalar", lambda sc, t=t: sc.copy(
                        vtm[:, t * D + 512:(t + 1) * D], ppA[3][:, :]), "ac")

                    s.wait("vector", "ac", ac_k[t])
                    if t >= 1:
                        s.wait("vector", "dv", dv_k[t - 1])
                    hln_stats(pf32[t % 2], dv_sq[t - 1] if t >= 1 else 0)
                    dv_sq[t] = s.cnt["dv"]
                    dv_k[t] = hln_apply(pf32[t % 2], ktm, t * D, gamk, betk)
                    dv_b[t] = s.cnt["dv"]

                    s.wait("tensor", "dv", dv_k[t])
                    s.wait("tensor", "ac", ac_v[t])
                    lastst = None
                    for h in range(NH):
                        lastst = s.op("tensor", lambda tn, t=t, h=h: tn.matmul(
                            psS[h // 8][:, (h % 8) * DH:(h % 8 + 1) * DH],
                            ktm[:, t * D + h * DH: t * D + (h + 1) * DH],
                            vtm[:, t * D + h * DH: t * D + (h + 1) * DH],
                            start=(t == 0 and h % 8 == 0),
                            stop=(t == NT - 1 and h % 8 == 7)), "pe")
                    pe_st[t] = lastst

                for t in range(NT):
                    emit_b0_tile(t)
                    if t >= 1:
                        emit_b1_ktile(t - 1)
                emit_b1_ktile(NT - 1)

                s.wait("vector", "pe", pe_st[NT - 1])
                s.op("vector", lambda v: v.tensor_copy(sowns[:, 0:512], psS[0][:, :]), "dv")
                dv_sown = s.op("vector", lambda v: v.tensor_copy(sowns[:, 512:1024], psS[1][:, :]), "dv")
                s.wait("sync", "dv", dv_sown)
                s.dmaw("sync", "sd")
                sd_sown = s.op("sync", lambda ss: ss.dma_start(out=sownd.ap(), in_=sowns[:, :]), "sd", 16)
                s.wait("gpsimd", "sd", sd_sown)
                s.op("gpsimd", lambda g: g.collective_compute(
                    "AllGather", ALU.bypass, replica_groups=[list(range(NCORES))],
                    ins=[sownd.ap().opt()], outs=[sgath.ap().opt()]), "cc", 1)

                s.wait("gpsimd", "pe", pe_k[NT - 1])
                gd_wq = None
                for dc in range(NDC):
                    gd_wq = s.op("gpsimd", lambda g, dc=dc: g.dma_start(
                        out=wk_s[:, dc * D:(dc + 1) * D],
                        in_=wq_in[dc * 128:(dc + 1) * 128, :]), "gd", 16)
                s.wait("gpsimd", "gd", gd_wq)

                pe_q = [None] * NT
                ac_q = [None] * NT
                dv_q = [None] * NT
                for t in range(NT):
                    if t == 0:
                        s.wait("tensor", "gd", gd_wq)
                        s.wait("tensor", "ac", ac_k[NT - 1])
                    if t >= 1:
                        s.wait("tensor", "ac", ac_q[t - 1])
                    pe_q[t] = proj_mms(t, wk_s)
                    s.wait("scalar", "pe", pe_q[t])
                    if t >= 2:
                        s.wait("scalar", "dv", dv_q[t - 2])
                    else:
                        s.wait("scalar", "dv", dv_k[NT - 1])
                    s.op("scalar", lambda sc, t=t: sc.copy(pf32[t % 2][:, 0:512], ppA[0][:, :]), "ac")
                    ac_q[t] = s.op("scalar", lambda sc, t=t: sc.copy(
                        pf32[t % 2][:, 512:1024], ppA[1][:, :]), "ac")
                    s.wait("vector", "ac", ac_q[t])
                    s.wait("vector", "dv", dv_q[t - 1] if t >= 1 else dv_k[NT - 1])
                    hln_stats(pf32[t % 2], dv_sq2[t - 1] if t >= 1 else dv_sq[NT - 1])
                    dv_sq2[t] = s.cnt["dv"]
                    dv_q[t] = hln_apply(pf32[t % 2], qtm, t * D, gamq, betq)
                    dv_b2[t] = s.cnt["dv"]

                s.emit(nc)
                if debug:
                    sdd = s.op("gpsimd", lambda g: g.dma_start(out=dbg_xln1, in_=xln1[:, :]), "gd", 16)
                    sdd = s.op("gpsimd", lambda g: g.dma_start(out=dbg_qtm, in_=qtm[:, :]), "gd", 16)
                    sdd = s.op("gpsimd", lambda g: g.dma_start(out=dbg_ktm, in_=ktm[:, :]), "gd", 16)
                    sdd = s.op("gpsimd", lambda g: g.dma_start(out=dbg_vtm, in_=vtm[:, :]), "gd", 16)
                    s.wait("gpsimd", "gd", sdd)
                    s.emit(nc)

            with ExitStack() as eB:
                wo_s = eB.enter_context(nc.sbuf_tensor("wo_s", [128, NDC * D], BF16))
                qfm = eB.enter_context(nc.sbuf_tensor("qfm", [64, NH * SBT], BF16))
                kfm = eB.enter_context(nc.sbuf_tensor("kfm", [64, NH * SBT], BF16))
                ofm = eB.enter_context(nc.sbuf_tensor("ofm", [128, NDC * SBT], BF16))
                stb = [eB.enter_context(nc.sbuf_tensor(f"stb{i}", [128, SBT], BF16)) for i in range(4)]
                sgt = eB.enter_context(nc.sbuf_tensor("sgt", [64, NH * DH], F32))
                tmpS = eB.enter_context(nc.sbuf_tensor("tmpS", [64, NH * DH], F32))
                xrl = [eB.enter_context(nc.sbuf_tensor(f"xrl{i}", [128, D], F32)) for i in range(2)]
                x2st = [eB.enter_context(nc.sbuf_tensor(f"x2st{i}", [128, D], F32)) for i in range(2)]

                ptB = [eB.enter_context(nc.psum_tensor(f"ptB{i}", [128, 128], BF16)) for i in range(2)]
                psc = [eB.enter_context(nc.psum_tensor(f"psc{i}", [128, 512], F32)) for i in range(2)]
                po = [eB.enter_context(nc.psum_tensor(f"po{i}", [64, 512], F32)) for i in range(2)]
                psS2 = [eB.enter_context(nc.psum_tensor(f"psS2_{i}", [64, 512], F32)) for i in range(2)]

                gd_wo = None
                for dc in range(NDC):
                    gd_wo = s.op("gpsimd", lambda g, dc=dc: g.dma_start(
                        out=wo_s[:, dc * D:(dc + 1) * D],
                        in_=wo_in[dc * 128:(dc + 1) * 128, :]), "gd", 16)
                s.wait("gpsimd", "gd", gd_wo)
                s.wait("gpsimd", "cc", 1)
                dv_comb = [None] * NCORES
                gd_sg = [None] * NCORES
                gd_sg[0] = s.op("gpsimd", lambda g: g.dma_start(
                    out=sgt[:, :], in_=sgath.ap()[0:64, :]), "gd", 16)
                for r in range(NCORES):
                    s.wait("vector", "gd", gd_sg[r])
                    if r == 0:
                        dv_comb[0] = s.op("vector", lambda v: v.tensor_scalar_mul(
                            s_run[:, :], sgt[:, :], selbc[0:64, 0:1]), "dv")
                    else:
                        s.wait("vector", "dv", dv_comb[r - 1])
                        a = s.op("vector", lambda v, r=r: v.tensor_scalar_mul(
                            tmpS[:, :], sgt[:, :], selbc[0:64, r:r + 1]), "dv")
                        s.wait("vector", "dv", a)
                        dv_comb[r] = s.op("vector", lambda v: v.tensor_add(
                            s_run[:, :], s_run[:, :], tmpS[:, :]), "dv")
                    if r + 1 < NCORES:
                        s.wait("gpsimd", "dv", dv_comb[r] if r == 0 else a)
                        s.dmaw("gpsimd", "gd")
                        gd_sg[r + 1] = s.op("gpsimd", lambda g, r=r: g.dma_start(
                            out=sgt[:, :], in_=sgath.ap()[(r + 1) * 64:(r + 2) * 64, :]), "gd", 16)
                s.wait("scalar", "dv", dv_comb[NCORES - 1])
                sbf_ready = s.op("scalar", lambda sc: sc.copy(s_bf[:, :], s_run[:, :]), "ac")

                def emit_transposes(sb):
                    cpd = {}
                    for qi, (src, dstb) in enumerate(((qtm, qfm), (ktm, kfm))):
                        for tt in range(NSB):
                            t = sb * NSB + tt
                            for dc in range(NDC):
                                j = qi * NSB * NDC + tt * NDC + dc
                                if j >= 2:
                                    s.wait("tensor", "dv2", cpd[j - 2][0])
                                    s.wait("tensor", "ac2", cpd[j - 2][1])
                                tv = s.op("tensor", lambda tn, t=t, dc=dc, j=j, src=src:
                                          tn.transpose(ptB[j % 2][:, :],
                                                       src[:, t * D + dc * 128: t * D + (dc + 1) * 128],
                                                       ident[:, :]), "pe")
                                hlo, hhi = 2 * dc, 2 * dc + 1
                                s.wait("vector", "pe", tv)
                                v1 = s.op("vector", lambda v, j=j, b=dstb, hlo=hlo, tt=tt:
                                          v.tensor_copy(
                                              b[:, hlo * SBT + tt * 128: hlo * SBT + (tt + 1) * 128],
                                              ptB[j % 2][0:64, :]), "dv2")
                                s.wait("scalar", "pe", tv)
                                v2 = s.op("scalar", lambda sc, j=j, b=dstb, hhi=hhi, tt=tt:
                                          sc.copy(
                                              b[:, hhi * SBT + tt * 128: hhi * SBT + (tt + 1) * 128],
                                              ptB[j % 2][64:128, :]), "ac2")
                                cpd[j] = (v1, v2)
                    return {"dv2": s.cnt["dv2"], "ac2": s.cnt["ac2"]}

                tr_done = emit_transposes(0)
                s.emit(nc)

                dv_x2 = {}
                sd_xr = {}
                sd_x2w = {}
                for sb in range(NSB):
                    s.wait("tensor", "dv2", tr_done["dv2"])
                    s.wait("tensor", "ac2", tr_done["ac2"])
                    s.wait("tensor", "ac", sbf_ready)
                    if (sb * NSB - 1, 1) in dv_x2:
                        s.wait("tensor", "dv", dv_x2[(sb * NSB - 1, 1)])

                    me = {}
                    oev = {}
                    av_done = [None] * NH
                    sinc_last = [None] * NH
                    sc_done = {}

                    def emit_qs(h):
                        if h >= 2:
                            s.wait("tensor", oev[h - 2][0], oev[h - 2][1])
                        s.op("tensor", lambda tn, h=h: tn.matmul(
                            po[h % 2][:, :], s_bf[:, h * DH:(h + 1) * DH],
                            qfm[:, h * SBT:(h + 1) * SBT],
                            start=True, stop=False), "pe")

                    def emit_st(h, kc):
                        m = h * NSB + kc
                        n = SBT - kc * 128
                        if m >= 2:
                            s.wait("tensor", "dv", me[m - 2])
                        sc_done[m] = s.op("tensor", lambda tn, h=h, kc=kc, m=m, n=n:
                                          tn.matmul(psc[m % 2][:, 0:n],
                                                    kfm[:, h * SBT + kc * 128: h * SBT + (kc + 1) * 128],
                                                    qfm[:, h * SBT + kc * 128:(h + 1) * SBT],
                                                    start=True, stop=True), "pe")
                        s.wait("vector", "pe", sc_done[m])
                        me[m] = s.op("vector", lambda v, m=m, n=n: v.tensor_mul(
                            stb[m % 4][:, 0:n], psc[m % 2][:, 0:n], emask[:, 0:n]), "dv")

                    def emit_av(h, kc):
                        m = h * NSB + kc
                        n = SBT - kc * 128
                        s.wait("tensor", "dv", me[m])
                        av_done[h] = s.op("tensor", lambda tn, h=h, kc=kc, m=m, sb=sb, n=n:
                                          tn.matmul(po[h % 2][:, kc * 128:SBT],
                                                    vtm[:, (sb * NSB + kc) * D + h * DH:
                                                        (sb * NSB + kc) * D + (h + 1) * DH],
                                                    stb[m % 4][:, 0:n],
                                                    start=False, stop=(kc == NSB - 1)), "pe")

                    def emit_sinc(h):
                        for kc in range(NSB):
                            t = sb * NSB + kc
                            sinc_last[h] = s.op("tensor", lambda tn, t=t, h=h, kc=kc: tn.matmul(
                                psS2[h // 8][:, (h % 8) * DH:(h % 8 + 1) * DH],
                                ktm[:, t * D + h * DH: t * D + (h + 1) * DH],
                                vtm[:, t * D + h * DH: t * D + (h + 1) * DH],
                                start=(kc == 0 and h % 8 == 0),
                                stop=(kc == NSB - 1 and h % 8 == 7)), "pe")

                    def emit_oev(h):
                        c = h // 2
                        rlo = (h % 2) * 64
                        if h % 2 == 0:
                            s.wait("vector", "pe", av_done[h])
                            val = s.op("vector", lambda v, h=h, c=c, rlo=rlo: v.tensor_copy(
                                ofm[rlo:rlo + 64, c * SBT:(c + 1) * SBT], po[h % 2][:, :]), "dv2")
                            oev[h] = ("dv2", val)
                        else:
                            s.wait("scalar", "pe", av_done[h])
                            val = s.op("scalar", lambda sc, h=h, c=c, rlo=rlo: sc.copy(
                                ofm[rlo:rlo + 64, c * SBT:(c + 1) * SBT], po[h % 2][:, :]), "ac2")
                            oev[h] = ("ac2", val)

                    for h in range(NH):
                        emit_qs(h)
                        emit_st(h, 0)
                        emit_st(h, 1)
                        if h >= 1:
                            emit_av(h - 1, 2)
                            emit_av(h - 1, 3)
                            emit_sinc(h - 1)
                            emit_oev(h - 1)
                        emit_st(h, 2)
                        emit_st(h, 3)
                        emit_av(h, 0)
                        emit_av(h, 1)
                    emit_av(NH - 1, 2)
                    emit_av(NH - 1, 3)
                    emit_sinc(NH - 1)
                    emit_oev(NH - 1)
                    last_sinc = sinc_last[NH - 1]

                    s.wait("vector", "pe", last_sinc)
                    a = s.op("vector", lambda v: v.tensor_add(
                        s_run[:, 0:512], s_run[:, 0:512], psS2[0][:, :]), "dv")
                    s.wait("vector", "dv", a)
                    dv_supd = s.op("vector", lambda v: v.tensor_add(
                        s_run[:, 512:1024], s_run[:, 512:1024], psS2[1][:, :]), "dv")
                    if sb < NSB - 1:
                        s.wait("scalar", "dv", dv_supd)
                        sbf_ready = s.op("scalar", lambda sc: sc.copy(s_bf[:, :], s_run[:, :]), "ac")
                        tr_done = emit_transposes(sb + 1)

                    oev_dv2 = s.cnt["dv2"]
                    oev_ac2 = s.cnt["ac2"]
                    s.wait("tensor", "dv2", oev_dv2)
                    s.wait("tensor", "ac2", oev_ac2)
                    if sb == 0:
                        s.wait("tensor", "gd", gd_wo)
                    for tt in range(NSB):
                        tglob = sb * NSB + tt
                        if (tglob - 2, 1) in dv_x2:
                            s.wait("sync", "dv", dv_x2[(tglob - 2, 1)])
                        s.dmaw("sync", "ra" if tglob % 2 == 0 else "rb")
                        sd_xr[tglob] = s.op("sync", lambda ss, tglob=tglob: ss.dma_start(
                            out=xrl[tglob % 2][:, :], in_=x_in[tglob * 128:(tglob + 1) * 128, :]),
                            "ra" if tglob % 2 == 0 else "rb", 16)
                        for half in range(2):
                            if tt == 0:
                                s.wait("tensor", "dv", me[NH * NSB - 1])
                            else:
                                s.wait("tensor", "dv", dv_x2[(tglob - 1, half)])
                            last = None
                            for c in range(NDC):
                                last = s.op("tensor", lambda tn, tt=tt, half=half, c=c:
                                            tn.matmul(psc[half][:, :],
                                                      ofm[:, c * SBT + tt * 128: c * SBT + (tt + 1) * 128],
                                                      wo_s[:, c * D + half * 512: c * D + (half + 1) * 512],
                                                      start=(c == 0), stop=(c == NDC - 1)), "pe")
                            s.wait("vector", "pe", last)
                            s.wait("vector", "ra" if tglob % 2 == 0 else "rb", sd_xr[tglob])
                            if tglob >= 2:
                                s.wait("vector", "sta" if tglob % 2 == 0 else "stb",
                                       sd_x2w[tglob - 2])
                            dv_x2[(tglob, half)] = s.op("vector", lambda v, tglob=tglob, half=half:
                                                        v.tensor_add(
                                                            x2st[tglob % 2][:, half * 512:(half + 1) * 512],
                                                            xrl[tglob % 2][:, half * 512:(half + 1) * 512],
                                                            psc[half][:, :]), "dv")
                        s.wait("sync", "dv", dv_x2[(tglob, 0)])
                        s.wait("sync", "dv", dv_x2[(tglob, 1)])
                        s.dmaw("sync", "sta" if tglob % 2 == 0 else "stb")
                        sd_x2w[tglob] = s.op("sync", lambda ss, tglob=tglob: ss.dma_start(
                            out=x2d[tglob * 128:(tglob + 1) * 128, :],
                            in_=x2st[tglob % 2][:, :]),
                            "sta" if tglob % 2 == 0 else "stb", 16)
                    s.emit(nc)
                    if debug and sb == 0:
                        sdd = s.op("gpsimd", lambda g: g.dma_start(out=dbg_ofm, in_=ofm[:, :]), "gd", 16)
                        sdd = s.op("gpsimd", lambda g: g.dma_start(out=dbg_srun, in_=s_run[:, :]), "gd", 16)
                        s.wait("gpsimd", "gd", sdd)
                        s.emit(nc)

        with ExitStack() as eC:
            xhfm = eC.enter_context(nc.sbuf_tensor("xhfm", [128, NDC * TC], BF16))
            yac = eC.enter_context(nc.sbuf_tensor("yac", [128, NDC * TC], BF16))
            G_all = eC.enter_context(nc.sbuf_tensor("G_all", [128, 128], BF16))
            GTf = eC.enter_context(nc.sbuf_tensor("GTf", [128, 128], F32))

            with ExitStack() as eD:
                x2l = [eD.enter_context(nc.sbuf_tensor(f"x2l{i}", [128, D], F32)) for i in range(3)]
                xl2tm = [eD.enter_context(nc.sbuf_tensor(f"xl2tm{i}", [128, D], F32)) for i in range(2)]
                xh_tm = [eD.enter_context(nc.sbuf_tensor(f"xh_tm{i}", [128, D], BF16)) for i in range(2)]
                xl_tm = [eD.enter_context(nc.sbuf_tensor(f"xl_tm{i}", [128, D], BF16)) for i in range(2)]
                xlfm = [eD.enter_context(nc.sbuf_tensor(f"xlfm{i}", [128, NDC * 128], BF16)) for i in range(2)]
                sqb3 = eD.enter_context(nc.sbuf_tensor("sqb3", [128, D], F32))
                lnst3 = eD.enter_context(nc.sbuf_tensor("lnst3", [128, 8], F32))
                Lt4 = eD.enter_context(nc.sbuf_tensor("Lt4", [128, 32], F32))
                tA = eD.enter_context(nc.sbuf_tensor("tA", [128, 32], F32))
                tB = eD.enter_context(nc.sbuf_tensor("tB", [128, 32], F32))
                q4 = [eD.enter_context(nc.sbuf_tensor(f"q4_{i}", [128, 4], F32)) for i in range(8)]
                ptC = [eD.enter_context(nc.psum_tensor(f"ptC{i}", [128, 128], BF16)) for i in range(2)]
                pgl = [eD.enter_context(nc.psum_tensor(f"pgl{i}", [128, 2 * NE], F32)) for i in range(2)]

                dv_g = [None] * NT
                dv_tile = [None] * NT
                dv_f3 = [None] * NT
                dv_ap3 = [None] * NT
                lt4_done = [None] * (NT // 4)
                sd_x2r = [None] * NT
                pe_gate = [None] * NT
                ac_hi = [None] * NT
                cpC = {}
                trj = 0

                s.op("vector", lambda v: v.memset(G_all[:, 0:16], 0.0), "dv")

                s.wait("sync", "sta", s.cnt["sta"])
                s.wait("sync", "stb", s.cnt["stb"])
                for t in range(3):
                    s.dmaw("sync", ["ja", "jb", "jc"][t % 3])
                    sd_x2r[t] = s.op("sync", lambda ss, t=t: ss.dma_start(
                        out=x2l[t % 3][:, :], in_=x2d[t * 128:(t + 1) * 128, :]),
                        ["ja", "jb", "jc"][t % 3], 16)

                for t in range(NT):
                    xt = x2l[t % 3]
                    s.wait("vector", ["ja", "jb", "jc"][t % 3], sd_x2r[t])
                    if t >= 1:
                        s.wait("vector", "dv", dv_tile[t - 1])
                    if t >= 2:
                        s.wait("vector", "ac", ac_hi[t - 2])
                    s.wait("scalar", ["ja", "jb", "jc"][t % 3], sd_x2r[t])
                    if t >= 1:
                        s.wait("scalar", "dv", dv_f3[t - 1])
                    s.wait("scalar", "ac", s.cnt["ac"])
                    s.op("scalar", lambda sc, xt=xt: sc.activation(
                        sqb3[:, :], xt[:, :], AF.Identity, accum_out=lnst3[:, 0:1]), "ac")
                    s.wait("scalar", "ac", s.cnt["ac"])
                    sq3 = s.op("scalar", lambda sc, xt=xt: sc.activation(
                        sqb3[:, :], xt[:, :], AF.Square, accum_out=lnst3[:, 1:2]), "ac")
                    s.wait("vector", "ac", sq3)
                    d = s.op("vector", lambda v: v.tensor_scalar_mul(lnst3[:, 2:3], lnst3[:, 0:1], 1.0 / D), "dv")
                    s.wait("vector", "dv", d)
                    e = s.op("vector", lambda v: v.tensor_mul(lnst3[:, 4:5], lnst3[:, 2:3], lnst3[:, 2:3]), "dv")
                    s.wait("vector", "dv", e)
                    f = s.op("vector", lambda v: v.tensor_scalar(
                        lnst3[:, 5:6], lnst3[:, 1:2], 1.0 / D, lnst3[:, 4:5],
                        ALU.mult, ALU.subtract), "dv")
                    dv_f3[t] = f
                    s.wait("scalar", "dv", f)
                    g_ = s.op("scalar", lambda sc: sc.activation(
                        lnst3[:, 6:7], lnst3[:, 5:6], AF.Sqrt, bias=np.float32(EPS)), "ac")
                    s.wait("vector", "ac", g_)
                    h_ = s.op("vector", lambda v: v.reciprocal(lnst3[:, 7:8], lnst3[:, 6:7]), "dv")
                    s.wait("vector", "dv", h_)
                    ap_ = s.op("vector", lambda v, xt=xt, t=t: v.tensor_scalar(
                        xl2tm[t % 2][:, :], xt[:, :], lnst3[:, 2:3], lnst3[:, 7:8],
                        ALU.subtract, ALU.mult), "dv")
                    dv_ap3[t] = ap_
                    s.wait("scalar", "dv", ap_)
                    if t >= 2:
                        s.wait("scalar", "pe", pe_gate[t - 2])
                    ac_hi[t] = s.op("scalar", lambda sc, t=t: sc.copy(
                        xh_tm[t % 2][:, :], xl2tm[t % 2][:, :]), "ac")
                    s.wait("vector", "ac", ac_hi[t])
                    lo_ = s.op("vector", lambda v, t=t: v.tensor_sub(
                        xl_tm[t % 2][:, :], xl2tm[t % 2][:, :], xh_tm[t % 2][:, :]), "dv")

                    s.wait("tensor", "ac", ac_hi[t])
                    s.wait("tensor", "dv", lo_)
                    for srcb, kind in ((xh_tm, "h"), (xl_tm, "l")):
                        for dc in range(NDC):
                            j = trj
                            trj += 1
                            if j >= 2:
                                s.wait("tensor", cpC[j - 2][0], cpC[j - 2][1])
                            tv = s.op("tensor", lambda tn, t=t, dc=dc, j=j, srcb=srcb: tn.transpose(
                                ptC[j % 2][:, :], srcb[t % 2][:, dc * 128:(dc + 1) * 128],
                                ident[:, :]), "pe")
                            if kind == "h":
                                dsl = slice(dc * TC + t * 128, dc * TC + (t + 1) * 128)
                                s.wait("scalar", "pe", tv)
                                val = s.op("scalar", lambda sc, j=j, dsl=dsl: sc.copy(
                                    xhfm[:, dsl], ptC[j % 2][:, :]), "ac2")
                                cpC[j] = ("ac2", val)
                            else:
                                dsl = slice(dc * 128, (dc + 1) * 128)
                                s.wait("vector", "pe", tv)
                                val = s.op("vector", lambda v, j=j, dsl=dsl, t=t: v.tensor_copy(
                                    xlfm[t % 2][:, dsl], ptC[j % 2][:, :]), "dv2")
                                cpC[j] = ("dv2", val)

                    s.wait("tensor", "ac2", cpC[t * 16 + 7][1])
                    s.wait("tensor", "dv2", cpC[t * 16 + 15][1])
                    g = t // 4
                    bank = (t % 4) // 2
                    col0 = (t % 2) * NE
                    if g >= 1 and t % 4 == 0:
                        s.wait("tensor", "dv", lt4_done[g - 1])
                    nmm = 0
                    for term in range(3):
                        rbuf = gwh if term != 1 else gwl
                        uselo = (term == 2)
                        for dc in range(NDC):
                            st_ = (nmm == 0 and t % 2 == 0)
                            sp_ = (nmm == 23 and t % 2 == 1)
                            if uselo:
                                pe_gate[t] = s.op("tensor", lambda tn, dc=dc, t=t, rbuf=rbuf,
                                                  st_=st_, sp_=sp_, bank=bank, col0=col0: tn.matmul(
                                    pgl[bank][:, col0:col0 + NE],
                                    xlfm[t % 2][:, dc * 128:(dc + 1) * 128],
                                    rbuf[:, dc * NE:(dc + 1) * NE], start=st_, stop=sp_), "pe")
                            else:
                                pe_gate[t] = s.op("tensor", lambda tn, dc=dc, t=t, rbuf=rbuf,
                                                  st_=st_, sp_=sp_, bank=bank, col0=col0: tn.matmul(
                                    pgl[bank][:, col0:col0 + NE],
                                    xhfm[:, dc * TC + t * 128: dc * TC + (t + 1) * 128],
                                    rbuf[:, dc * NE:(dc + 1) * NE], start=st_, stop=sp_), "pe")
                            nmm += 1
                    dv_tile[t] = s.cnt["dv"]

                    if t % 4 == 3:
                        s.wait("vector", "pe", pe_gate[t])

                        def dvc(fn):
                            val = s.op("vector", fn, "dv")
                            s.wait("vector", "dv", val)
                            return val

                        def v48(ap):
                            return ap.rearrange("p (g e) -> p g e", e=NE)

                        def bc4(t4_):
                            return t4_[:, :].rearrange(
                                "p (g o) -> p g o", o=1).broadcast_to([128, 4, NE])

                        iota_bc = iota8[:, :].rearrange(
                            "p (o e) -> p o e", o=1).broadcast_to([128, 4, NE])
                        gb2 = gb_bc[:, :].rearrange(
                            "p (o e) -> p o e", o=1).broadcast_to([128, 2, NE])
                        dvc(lambda v: v.tensor_tensor(
                            Lt4[:, 0:16].rearrange("p (g e) -> p g e", e=NE),
                            pgl[0][:, :].rearrange("p (g e) -> p g e", e=NE),
                            gb2, ALU.add))
                        lt4_done[g] = dvc(lambda v: v.tensor_tensor(
                            Lt4[:, 16:32].rearrange("p (g e) -> p g e", e=NE),
                            pgl[1][:, :].rearrange("p (g e) -> p g e", e=NE),
                            gb2, ALU.add))
                        dvc(lambda v: v.tensor_reduce(q4[0][:, :], v48(Lt4[:, :]), AX.X, ALU.max))
                        dvc(lambda v: v.tensor_tensor(v48(tA[:, :]), v48(Lt4[:, :]),
                                                      bc4(q4[0]), ALU.subtract))
                        s.wait("scalar", "dv", s.cnt["dv"])
                        ex_ = s.op("scalar", lambda sc: sc.activation(tB[:, :], tA[:, :], AF.Exp), "ac")
                        s.wait("vector", "ac", ex_)
                        dvc(lambda v: v.tensor_reduce(q4[1][:, :], v48(tB[:, :]), AX.X, ALU.add))
                        dvc(lambda v: v.reciprocal(q4[2][:, :], q4[1][:, :]))
                        dvc(lambda v: v.tensor_tensor(v48(tA[:, :]), v48(Lt4[:, :]),
                                                      bc4(q4[0]), ALU.is_equal))
                        dvc(lambda v: v.tensor_scalar(tA[:, :], tA[:, :], -64.0, 64.0,
                                                      ALU.mult, ALU.add))
                        dvc(lambda v: v.tensor_tensor(v48(tA[:, :]), v48(tA[:, :]),
                                                      iota_bc, ALU.add))
                        dvc(lambda v: v.tensor_reduce(q4[3][:, :], v48(tA[:, :]), AX.X, ALU.min))
                        dvc(lambda v: v.tensor_tensor(v48(tA[:, :]), iota_bc,
                                                      bc4(q4[3]), ALU.is_equal))
                        dvc(lambda v: v.scalar_tensor_tensor(tA[:, :], tA[:, :], -1e30,
                                                             Lt4[:, :], ALU.mult, ALU.add))
                        dvc(lambda v: v.tensor_reduce(q4[4][:, :], v48(tA[:, :]), AX.X, ALU.max))
                        dvc(lambda v: v.tensor_tensor(v48(tA[:, :]), v48(tA[:, :]),
                                                      bc4(q4[4]), ALU.is_equal))
                        dvc(lambda v: v.tensor_scalar(tA[:, :], tA[:, :], -64.0, 64.0,
                                                      ALU.mult, ALU.add))
                        dvc(lambda v: v.tensor_tensor(v48(tA[:, :]), v48(tA[:, :]),
                                                      iota_bc, ALU.add))
                        dvc(lambda v: v.tensor_reduce(q4[5][:, :], v48(tA[:, :]), AX.X, ALU.min))
                        dvc(lambda v: v.tensor_max(q4[6][:, :], q4[3][:, :], q4[5][:, :]))
                        dvc(lambda v: v.tensor_tensor(q4[7][:, :], q4[3][:, :], q4[5][:, :], ALU.is_gt))
                        dvc(lambda v: v.tensor_sub(q4[0][:, :], q4[0][:, :], q4[4][:, :]))
                        dvc(lambda v: v.tensor_mul(q4[0][:, :], q4[0][:, :], q4[7][:, :]))
                        dvc(lambda v: v.tensor_add(q4[0][:, :], q4[0][:, :], q4[4][:, :]))
                        dvc(lambda v: v.tensor_reduce(q4[1][:, :], v48(Lt4[:, :]), AX.X, ALU.max))
                        dvc(lambda v: v.tensor_sub(q4[0][:, :], q4[0][:, :], q4[1][:, :]))
                        s.wait("scalar", "dv", s.cnt["dv"])
                        ex2 = s.op("scalar", lambda sc: sc.activation(
                            q4[1][:, :], q4[0][:, :], AF.Exp), "ac")
                        s.wait("vector", "ac", ex2)
                        dvc(lambda v: v.tensor_mul(q4[1][:, :], q4[1][:, :], q4[2][:, :]))
                        lastg = None
                        for e in range(1, NE):
                            if lastg is not None:
                                s.wait("vector", "dv", lastg)
                            aa = s.op("vector", lambda v, e=e: v.tensor_scalar(
                                tA[:, 0:4], q4[6][:, :], float(e), None, ALU.is_equal), "dv")
                            s.wait("vector", "dv", aa)
                            lastg = s.op("vector", lambda v, e=e, g=g: v.tensor_mul(
                                G_all[:, e * 16 + 4 * g: e * 16 + 4 * g + 4],
                                tA[:, 0:4], q4[1][:, :]), "dv")
                        for tt_ in range(4 * g, 4 * g + 4):
                            dv_g[tt_] = lastg
                        dv_tile[t] = lastg
                    if t + 3 < NT:
                        s.wait("sync", "dv", dv_ap3[t])
                        s.dmaw("sync", ["ja", "jb", "jc"][t % 3])
                        sd_x2r[t + 3] = s.op("sync", lambda ss, t=t: ss.dma_start(
                            out=x2l[t % 3][:, :],
                            in_=x2d[(t + 3) * 128:(t + 4) * 128, :]),
                            ["ja", "jb", "jc"][t % 3], 16)

                s.wait("tensor", "dv", dv_g[NT - 1])
                s.wait("tensor", cpC[trj - 2][0], cpC[trj - 2][1])
                s.wait("tensor", cpC[trj - 1][0], cpC[trj - 1][1])
                gtr = s.op("tensor", lambda tn: tn.transpose(ptC[0][:, :], G_all[:, :], ident[:, :]), "pe")
                s.wait("vector", "pe", gtr)
                gcp = s.op("vector", lambda v: v.tensor_copy(GTf[:, :], ptC[0][:, :]), "dv")
                s.wait("sync", "dv", gcp)
                sd_grow = None
                for e in range(1, NE):
                    sd_grow = s.op("sync", lambda ss, e=e: ss.dma_start(
                        out=growd[e, :].rearrange("(a t) -> a t", t=128),
                        in_=GTf[e * 16:(e + 1) * 16, :]), "sd", 16)
                s.wait("sync", "sd", sd_grow)
                s.emit(nc)

            with ExitStack() as eE:
                hbuf = eE.enter_context(nc.sbuf_tensor("hbuf", [128, NHC * 1024], BF16))
                xe = [eE.enter_context(nc.sbuf_tensor(f"xe{i}", [128, NDC * 1024], BF16)) for i in range(2)]
                w1b = [eE.enter_context(nc.sbuf_tensor(f"w1b{i}", [128, 2048], BF16)) for i in range(4)]
                w2b = [eE.enter_context(nc.sbuf_tensor(f"w2b{i}", [128, 1024], BF16)) for i in range(2)]
                gbc = eE.enter_context(nc.sbuf_tensor("gbc", [128, TC], F32))
                m01 = [eE.enter_context(nc.sbuf_tensor(f"m01_{i}", [128, TC], BF16)) for i in range(2)]
                ytmp = [eE.enter_context(nc.sbuf_tensor(f"ytmp{i}", [128, 512], BF16)) for i in range(2)]
                ps = [eE.enter_context(nc.psum_tensor(f"ps{i}", [128, 512], F32)) for i in range(8)]

                base = {k: s.cnt[k] for k in ("pe", "dv", "ac", "gd", "sd")}
                NW1 = 16
                NW2 = 32

                sd_gbc = {}
                for ei in range(7):
                    sd_gbc[ei] = s.cnt["gba"] + 16 * (ei + 1)
                gd_w1, gd_w2 = {}, {}
                w1sem, w2sem = {}, {}
                W1S = ["wa", "wb", "we", "wf"]
                W2S = ["wc", "wd"]
                w1c = {i: s.cnt[W1S[i]] for i in range(4)}
                w2c = {i: s.cnt[W2S[i]] for i in range(2)}
                g1i = 0
                g2i = 0
                for eh in range(NEH):
                    for fi in range(NW1):
                        p = g1i % 4
                        g1i += 1
                        w1c[p] += 16
                        gd_w1[(eh, fi)] = w1c[p]
                        w1sem[(eh, fi)] = W1S[p]
                    for dg in range(4):
                        for fi in range(8):
                            p = g2i % 2
                            g2i += 1
                            w2c[p] += 16
                            gd_w2[(eh, dg, fi)] = w2c[p]
                            w2sem[(eh, dg, fi)] = W2S[p]
                pe_h, pe_y, pe_hlast = {}, {}, {}
                pec = base["pe"]
                for eh in range(NEH):
                    for hc in range(NHC):
                        for tb in range(2):
                            pec += 1
                            pe_h[(eh, hc, tb)] = pec
                    pe_hlast[eh] = pec
                    for dg in range(4):
                        for fi in range(8):
                            pec += 1
                            pe_y[(eh, dg, fi)] = pec
                ac_h = {}
                acc = base["ac"]
                for eh in range(NEH):
                    for hc in range(NHC):
                        for tb in range(2):
                            acc += 1
                            ac_h[(eh, hc, tb)] = acc
                dv_prep, dv_yev = {}, {}
                dvc_ = base["dv"] + 1

                def count_prep(eh):
                    nonlocal dvc_
                    ei, hp = eh // 2, eh % 2
                    if hp == 0:
                        dvc_ += 1
                    dvc_ += NDC
                    dv_prep[eh] = dvc_

                def count_yev(eh):
                    nonlocal dvc_
                    for dg in range(4):
                        dvc_ += 8
                        dv_yev[(eh, dg)] = dvc_

                count_prep(0)
                count_prep(1)
                for eh in range(NEH):
                    if eh >= 1:
                        count_yev(eh - 1)
                    if eh + 2 < NEH:
                        count_prep(eh + 2)
                count_yev(NEH - 1)
                dv_final = dvc_

                for ei in range(7):
                    e = ei + 1
                    if ei == 0:
                        s.wait("sync", "sd", sd_grow)
                    if ei >= 1:
                        s.wait("sync", "dv", dv_prep[2 * ei - 1])
                    s.dmaw("sync", "gba")
                    got = s.op("sync", lambda ss, e=e: ss.dma_start(
                        out=gbc[:, :], in_=growd[e:e + 1, :].broadcast_to([128, TC])), "gba", 16)
                    assert got == sd_gbc[ei]

                for eh in range(NEH):
                    ei, hp = eh // 2, eh % 2
                    e = ei + 1
                    for fi in range(NW1):
                        g1 = eh * NW1 + fi
                        if g1 >= 4:
                            peh, pfi = (g1 - 4) // NW1, (g1 - 4) % NW1
                            s.wait("gpsimd", "pe", pe_h[(peh, pfi * 2 + 1, 1)])
                        s.dmaw("gpsimd", w1sem[(eh, fi)])
                        got = s.op("gpsimd", lambda g, ei=ei, fi=fi, g1=g1: g.dma_start(
                            out=w1b[g1 % 4][:, :],
                            in_=w1_in[ei, fi * 2:(fi + 1) * 2, :, :, :].rearrange("a p c h -> p a c h")),
                            w1sem[(eh, fi)], 16)
                        assert got == gd_w1[(eh, fi)]
                    for dg in range(4):
                        for fi in range(8):
                            g2 = (eh * 4 + dg) * 8 + fi
                            if g2 >= 2:
                                pg = g2 - 2
                                peh, prem = pg // 32, pg % 32
                                pdg, pfi = prem // 8, prem % 8
                                s.wait("gpsimd", "pe", pe_y[(peh, pdg, pfi)])
                            s.dmaw("gpsimd", w2sem[(eh, dg, fi)])
                            got = s.op("gpsimd", lambda g, ei=ei, dg=dg, fi=fi, g2=g2: g.dma_start(
                                out=w2b[g2 % 2][:, :],
                                in_=w2_in[ei, fi * 4:(fi + 1) * 4, :, dg * 2:(dg + 1) * 2, :].rearrange(
                                    "a p c h -> p a c h")), w2sem[(eh, dg, fi)], 16)
                            assert got == gd_w2[(eh, dg, fi)]

                for eh in range(NEH):
                    ei, hp = eh // 2, eh % 2
                    s.wait("tensor", "dv", dv_prep[eh])
                    if eh >= 1:
                        s.wait("tensor", "dv", dv_yev[(eh - 1, 2)])
                    for hc in range(NHC):
                        fi = hc // 2
                        g1 = eh * NW1 + fi
                        s.wait("tensor", w1sem[(eh, fi)], gd_w1[(eh, fi)])
                        if hc >= 2:
                            s.wait("tensor", "ac", ac_h[(eh, hc - 2, 1)])
                        for tb in range(2):
                            got = None
                            for dc in range(NDC):
                                inc = "pe" if dc == NDC - 1 else None
                                got = s.op("tensor", lambda tn, hc=hc, tb=tb, dc=dc, g1=g1, eh=eh:
                                           tn.matmul(ps[2 * (hc % 2) + tb][:, :],
                                                     w1b[g1 % 4][:, ((hc % 2) * NDC + dc) * 128:
                                                                 ((hc % 2) * NDC + dc + 1) * 128],
                                                     xe[eh % 2][:, dc * 1024 + tb * 512:
                                                                dc * 1024 + (tb + 1) * 512],
                                                     start=(dc == 0), stop=(dc == NDC - 1)), inc)
                            assert got == pe_h[(eh, hc, tb)]
                    s.wait("tensor", "ac", ac_h[(eh, NHC - 1, 1)])
                    for dg in range(4):
                        if dg >= 2:
                            s.wait("tensor", "dv", dv_yev[(eh, dg - 2)])
                        elif dg == 1 and eh >= 1:
                            s.wait("tensor", "dv", dv_yev[(eh - 1, 3)])
                        for fi in range(8):
                            g2 = (eh * 4 + dg) * 8 + fi
                            s.wait("tensor", w2sem[(eh, dg, fi)], gd_w2[(eh, dg, fi)])
                            got = None
                            for hcl in range(4):
                                hc = fi * 4 + hcl
                                for dcl in range(2):
                                    for tb in range(2):
                                        pidx = 4 * (dg % 2) + dcl * 2 + tb
                                        inc = "pe" if (hcl == 3 and dcl == 1 and tb == 1) else None
                                        got = s.op("tensor", lambda tn, hc=hc, hcl=hcl, dcl=dcl,
                                                   tb=tb, g2=g2, pidx=pidx: tn.matmul(
                                            ps[pidx][:, :],
                                            w2b[g2 % 2][:, (hcl * 2 + dcl) * 128:
                                                        (hcl * 2 + dcl + 1) * 128],
                                            hbuf[:, hc * 1024 + tb * 512: hc * 1024 + (tb + 1) * 512],
                                            start=(hc == 0), stop=(hc == NHC - 1)), inc)
                            assert got == pe_y[(eh, dg, fi)]
                assert s.cnt["pe"] == pec

                for eh in range(NEH):
                    ei = eh // 2
                    for hc in range(NHC):
                        for tb in range(2):
                            s.wait("scalar", "pe", pe_h[(eh, hc, tb)])
                            got = s.op("scalar", lambda sc, hc=hc, tb=tb, ei=ei: sc.activation(
                                hbuf[:, hc * 1024 + tb * 512: hc * 1024 + (tb + 1) * 512],
                                ps[2 * (hc % 2) + tb][:, :], AF.Relu,
                                bias=b1s[:, ei * NHC + hc: ei * NHC + hc + 1]), "ac")
                            assert got == ac_h[(eh, hc, tb)]

                got = s.op("vector", lambda v: v.memset(yac[:, :], 0.0), "dv")
                assert got == base["dv"] + 1

                def rec_prep(eh):
                    ei, hp = eh // 2, eh % 2
                    if eh >= 2:
                        s.wait("vector", "pe", pe_hlast[eh - 2])
                    s.wait("vector", "gba", sd_gbc[ei])
                    last = None
                    if hp == 0:
                        last = s.op("vector", lambda v, ei=ei: v.tensor_single_scalar(
                            m01[ei % 2][:, :], gbc[:, :], 0.0, ALU.is_gt), "dv")
                    for dc in range(NDC):
                        last = s.op("vector", lambda v, dc=dc, hp=hp, eh=eh: v.tensor_mul(
                            xe[eh % 2][:, dc * 1024:(dc + 1) * 1024],
                            xhfm[:, dc * TC + hp * 1024: dc * TC + (hp + 1) * 1024],
                            gbc[:, hp * 1024:(hp + 1) * 1024]), "dv")
                    assert last == dv_prep[eh]

                def rec_yev(eh):
                    ei, hp = eh // 2, eh % 2
                    for dg in range(4):
                        s.wait("vector", "pe", pe_y[(eh, dg, 7)])
                        last = None
                        for dcl in range(2):
                            dc = dg * 2 + dcl
                            for tb in range(2):
                                pidx = 4 * (dg % 2) + dcl * 2 + tb
                                ysl = slice(dc * TC + hp * 1024 + tb * 512,
                                            dc * TC + hp * 1024 + (tb + 1) * 512)
                                msl = slice(hp * 1024 + tb * 512, hp * 1024 + (tb + 1) * 512)
                                aa = s.op("vector", lambda v, pidx=pidx, ei=ei, dc=dc, msl=msl, tb=tb:
                                          v.scalar_tensor_tensor(
                                              ytmp[tb][:, :], ps[pidx][:, :],
                                              b2s[:, ei * NDC + dc: ei * NDC + dc + 1],
                                              m01[ei % 2][:, msl], ALU.add, ALU.mult), "dv")
                                s.wait("vector", "dv", aa)
                                last = s.op("vector", lambda v, ysl=ysl, tb=tb: v.tensor_add(
                                    yac[:, ysl], yac[:, ysl], ytmp[tb][:, :]), "dv")
                        assert last == dv_yev[(eh, dg)]

                rec_prep(0)
                rec_prep(1)
                for eh in range(NEH):
                    if eh >= 1:
                        rec_yev(eh - 1)
                    if eh + 2 < NEH:
                        rec_prep(eh + 2)
                rec_yev(NEH - 1)
                assert s.cnt["dv"] == dv_final

                s.emit(nc)

            with ExitStack() as eF:
                x2l5 = [eF.enter_context(nc.sbuf_tensor(f"x2l5_{i}", [128, D], F32)) for i in range(2)]
                ostg = [eF.enter_context(nc.sbuf_tensor(f"ostg{i}", [128, D], F32)) for i in range(2)]
                ptD = [eF.enter_context(nc.psum_tensor(f"ptD{i}", [128, 128], BF16)) for i in range(2)]

                dv_o5 = {}
                sd_5 = [None] * NT
                st_5 = [None] * NT
                tr5 = {}
                for t in range(2):
                    s.dmaw("sync", "ja" if t % 2 == 0 else "jb")
                    sd_5[t] = s.op("sync", lambda ss, t=t: ss.dma_start(
                        out=x2l5[t % 2][:, :], in_=x2d[t * 128:(t + 1) * 128, :]),
                        "ja" if t % 2 == 0 else "jb", 16)
                for t in range(NT):
                    for dc in range(NDC):
                        j = t * NDC + dc
                        if j >= 2:
                            s.wait("tensor", "dv", dv_o5[((j - 2) // NDC, (j - 2) % NDC)])
                        tr5[j] = s.op("tensor", lambda tn, t=t, dc=dc, j=j: tn.transpose(
                            ptD[j % 2][:, :], yac[:, dc * TC + t * 128: dc * TC + (t + 1) * 128],
                            ident[:, :]), "pe")
                        s.wait("vector", "pe", tr5[j])
                        s.wait("vector", "ja" if t % 2 == 0 else "jb", sd_5[t])
                        if t >= 2 and dc == 0:
                            s.wait("vector", "sta" if t % 2 == 0 else "stb", st_5[t - 2])
                        dv_o5[(t, dc)] = s.op("vector", lambda v, t=t, dc=dc, j=j: v.tensor_add(
                            ostg[t % 2][:, dc * 128:(dc + 1) * 128],
                            x2l5[t % 2][:, dc * 128:(dc + 1) * 128],
                            ptD[j % 2][:, :]), "dv")
                    s.wait("sync", "dv", dv_o5[(t, NDC - 1)])
                    s.dmaw("sync", "sta" if t % 2 == 0 else "stb")
                    st_5[t] = s.op("sync", lambda ss, t=t: ss.dma_start(
                        out=y_out[t * 128:(t + 1) * 128, :], in_=ostg[t % 2][:, :]),
                        "sta" if t % 2 == 0 else "stb", 16)
                    if t + 2 < NT:
                        s.dmaw("sync", "ja" if t % 2 == 0 else "jb")
                        sd_5[t + 2] = s.op("sync", lambda ss, t=t: ss.dma_start(
                            out=x2l5[t % 2][:, :], in_=x2d[(t + 2) * 128:(t + 3) * 128, :]),
                            "ja" if t % 2 == 0 else "jb", 16)
                s.wait("sync", "sta", s.cnt["sta"])
                s.wait("sync", "stb", s.cnt["stb"])
                s.emit(nc)

    return nc


_PROG_CACHE = {}


def _get_program():
    if "nc" not in _PROG_CACHE:
        _PROG_CACHE["nc"] = build_program()
    return _PROG_CACHE["nc"]


def kernel(L0_attn_x, L0_attn_Wq, L0_attn_Wk, L0_attn_Wv, L0_attn_Wo,
           L0_attn_q_gamma, L0_attn_q_beta, L0_attn_k_gamma, L0_attn_k_beta,
           L0_ffn_gate_w, L0_ffn_gate_b, L0_ffn_w1, L0_ffn_b1, L0_ffn_w2,
           L0_ffn_b2):
    f = np.float32
    bf = ml_dtypes.bfloat16
    x = np.ascontiguousarray(np.asarray(L0_attn_x, dtype=f).reshape(T, D))

    wqT = np.ascontiguousarray(np.asarray(L0_attn_Wq, dtype=f).T)
    wkT = np.ascontiguousarray(np.asarray(L0_attn_Wk, dtype=f).T)
    wvT = np.ascontiguousarray(np.asarray(L0_attn_Wv, dtype=f).T)
    woT = np.ascontiguousarray(np.asarray(L0_attn_Wo, dtype=f).T)

    qg_t = np.ascontiguousarray(np.broadcast_to(np.asarray(L0_attn_q_gamma, f), (128, DH)))
    qb_t = np.ascontiguousarray(np.broadcast_to(np.asarray(L0_attn_q_beta, f), (128, DH)))
    kg_t = np.ascontiguousarray(np.broadcast_to(np.asarray(L0_attn_k_gamma, f), (128, DH)))
    kb_t = np.ascontiguousarray(np.broadcast_to(np.asarray(L0_attn_k_beta, f), (128, DH)))

    gwt = np.ascontiguousarray(np.asarray(L0_ffn_gate_w, f).reshape(NDC, 128, NE))
    gbb = np.ascontiguousarray(np.broadcast_to(np.asarray(L0_ffn_gate_b, f), (128, NE)))

    w1 = np.asarray(L0_ffn_w1, f)
    w2 = np.asarray(L0_ffn_w2, f)
    b1 = np.asarray(L0_ffn_b1, f)
    b2 = np.asarray(L0_ffn_b2, f)
    w1t = np.ascontiguousarray(
        w1[1:].reshape(7, NDC, 128, NHC, 128).transpose(0, 3, 2, 1, 4))
    w2t = np.ascontiguousarray(
        w2[1:].reshape(7, NHC, 128, NDC, 128))
    b1t = np.ascontiguousarray(b1[1:].reshape(7, NHC, 128).transpose(0, 2, 1))
    b2t = np.ascontiguousarray(b2[1:].reshape(7, NDC, 128).transpose(0, 2, 1))

    identity = np.eye(128, dtype=bf)
    em = np.ones((128, 512), dtype=bf)
    tri = (np.arange(128)[:, None] <= np.arange(128)[None, :]).astype(bf)
    em[:, 0:128] = tri
    iota8 = np.ascontiguousarray(np.broadcast_to(np.arange(NE, dtype=f), (128, NE)))

    common = {
        "wqT": wqT, "wkT": wkT, "wvT": wvT, "woT": woT,
        "qg_t": qg_t, "qb_t": qb_t, "kg_t": kg_t, "kb_t": kb_t,
        "gwt": gwt, "gbb": gbb, "w1t": w1t, "w2t": w2t, "b1t": b1t, "b2t": b2t,
        "ident": identity, "emask": em, "iota8": iota8,
    }
    in_maps = []
    for c in range(NCORES):
        sel = np.zeros((128, NE), dtype=f)
        sel[:, :c] = 1.0
        m = dict(common)
        m["x"] = np.ascontiguousarray(x[c * TC:(c + 1) * TC, :])
        m["selbc"] = sel
        in_maps.append(m)

    nc = _get_program()
    res = run_bass_kernel_spmd(nc, in_maps, core_ids=list(range(NCORES)))
    out = np.concatenate([res.results[c]["y"] for c in range(NCORES)], axis=0)
    return out.reshape(1, T, D).astype(np.float32)
